# version 13
# baseline (speedup 1.0000x reference)
import contextlib
import os

import numpy as np

import concourse.bass as bass
import concourse.mybir as mybir
from concourse.bass_utils import run_bass_kernel_spmd

F32 = mybir.dt.float32
BF16 = mybir.dt.bfloat16
AF = mybir.ActivationFunctionType
ALU = mybir.AluOpType
AX = mybir.AxisListType

D = 1024
DI = 2048
DH = 1024
DS = 16
DTR = 64
DFF = 4096
L = 8192
TOK = 4096
T = 512
NTL = L // T
NTO = TOK // T
ND = D // 128
NCH = DH // 128
EPS = 1e-5

AW_COLS = 32800
AF_COLS = 21510
AT_COLS = 12290

STAGE = 5
DEBUG = False
SIM_SAFE = os.environ.get("BASS_SIM_FP64") == "1"
P5_LVL = 8
NTL_RUN = 16

ENGS = ("sync", "gpsimd", "scalar", "vector", "tensor")
GROUPS = [[0, 1], [2, 3], [4, 5], [6, 7]]


class KB:
    def __init__(self):
        self.nc = bass.Bass("TRN2", target_bir_lowering=False, debug=False)
        self.es = contextlib.ExitStack()
        self.q = {e: [] for e in ENGS}
        self.cnt = {}

    def sem(self, name):
        h = self.es.enter_context(self.nc.semaphore(name))
        self.cnt[h] = 0
        return h

    def val(self, s):
        return self.cnt[s]

    def sb(self, name, shape, dtype):
        return self.es.enter_context(self.nc.sbuf_tensor(name, shape, dtype))

    def psum(self, name, shape):
        return self.es.enter_context(self.nc.psum_tensor(name, shape, F32))

    def op(self, eng, fn, waits=(), incs=(), amt=1):
        if SIM_SAFE and amt == 1 and eng != "sync":
            lp = self.last_pub.get(eng)
            if lp is not None:
                waits = tuple(waits) + (lp,)
            if not incs:
                incs = (self.selfsem[eng],)
            for s in incs:
                self.cnt[s] += amt
            self.last_pub[eng] = (incs[-1], self.cnt[incs[-1]])
        else:
            for s in incs:
                self.cnt[s] += amt
        self.q[eng].append((fn, tuple(waits), tuple(incs), amt))

    def dma(self, eng, out, in_, waits=(), incs=(), slow=False):
        self.op(eng, lambda e, o=out, i=in_, s=slow: e.dma_start(
            out=o, in_=i, allow_slow_non_contiguous=s),
            waits, incs, amt=16)

    def init_selfsems(self):
        self.selfsem = {e: self.sem(f"zz_self_{e}") for e in ENGS}
        self.last_pub = {}

    def finalize(self):
        block = self.es.enter_context(self.nc.Block())
        for name in ENGS:
            items = self.q[name]

            def make_emit(items=items):
                def emit(engine):
                    for fn, waits, incs, amt in items:
                        for (s, v) in waits:
                            engine.wait_ge(s, v)
                        instr = fn(engine)
                        if incs:
                            if instr is None:
                                instr = engine.nop()
                            for s in incs:
                                instr.then_inc(s, amt)
                return emit

            getattr(block, name)(make_emit())


def view3(tensor, off, c, t, p=128):
    return tensor[0:p, off:off + c * t].rearrange("p (c t) -> p c t", c=c)


_NC_CACHE = {}


def build_program():
    if (STAGE, DEBUG) in _NC_CACHE:
        return _NC_CACHE[(STAGE, DEBUG)]
    _ = P5_LVL
    k = KB()
    nc = k.nc

    def din(name, shape, dtype=F32):
        return nc.dram_tensor(name, shape, dtype, kind="ExternalInput").ap()

    def dout(name, shape, dtype=F32):
        return nc.dram_tensor(name, shape, dtype, kind="ExternalOutput").ap()

    def dint(name, shape, dtype=F32):
        return nc.dram_tensor(name, shape, dtype).ap()

    xT = din("xT", [NTL, 128, ND * T])
    xTh = din("xTh", [NTO, 128, ND * T])
    w_inj = din("w_inj", [128, ND * 2 * DH])
    conv_w = din("conv_w", [DH, 4])
    conv_b = din("conv_b", [DH, 1])
    w_xp = din("w_xp", [128, NCH * 96])
    w_dtp = din("w_dtp", [DTR, DH])
    b_dtp = din("b_dtp", [DH, 1])
    A_log = din("A_log", [DH, DS])
    D_in = din("D_h", [DH, 1])
    w_out = din("w_out", [128, NCH * D])
    ident_in = din("p3_ident", [128, 128])
    g1r_in = din("ln1_g", [1, D])
    b1r_in = din("ln1_b", [1, D])
    b1c_in = din("ln1_b_col", [D, 1])
    g1c_in = din("ln1_g_col", [D, 1])
    g2r_in = din("ln2_g", [1, D])
    b2c_in = din("ln2_b_col", [D, 1])
    g2c_in = din("ln2_g_col", [D, 1])
    w_f1 = din("w_f1", [128, ND * DFF])
    b_f1 = din("b_f1", [DFF, 1])
    w_f2 = din("w_f2", [128, (DFF // 128) * D])
    b_f2 = din("b_f2", [D, 1])
    cn_w1 = din("cn_w1", [128, ND * D])
    cn_b1 = din("cn_b1", [D, 1])
    cn_w2 = din("cn_w2", [128, ND * 512])
    cn_b2 = din("cn_b2", [512, 1])
    cn_w3 = din("cn_w3", [128, 4 * 256])
    cn_b3 = din("cn_b3", [256, 1])
    cn_w4 = din("cn_w4", [128, 2 * 1])
    cn_b4 = din("cn_b4", [1, 1])
    gc_w1 = din("gc_w1", [128, ND * 512])
    gc_b1 = din("gc_b1", [512, 1])
    gc_w2 = din("gc_w2", [128, 4 * 256])
    gc_b2 = din("gc_b2", [256, 1])
    gc_w3 = din("gc_w3", [128, 2 * 1])
    gc_b3 = din("gc_b3", [1, 1])
    ar_in = din("ar", [1, 1])

    out_d = dout("out", [NTO, 128, ND * T])

    cc_xdbl_in = dint("cc_xdbl_in", [2, 96, L // 2])
    cc_xdbl = dint("cc_xdbl", [2, 96, L // 2])
    cc_o1_in = dint("cc_o1_in", [2, NTO, 128, ND * T])
    cc_o1 = dint("cc_o1", [NTO, 128, ND * T])
    cc_mean_in = dint("cc_mean_in", [128, ND])
    cc_mean = dint("cc_mean", [128, ND])
    sp_xc = dint("sp_xc", [NTL, 128, NCH * T])
    sp_sz = dint("sp_sz", [NTL, 128, NCH * T], BF16)
    sp_g = dint("sp_g", [NTO, 128, 32 * T], BF16)
    sp_v = dint("sp_v", [NTO, 128, ND * T])
    sp_x1 = dint("sp_x1", [NTO, 128, ND * T])
    sp_w = dint("sp_w", [NTO, 128, ND * T])

    if DEBUG:
        dbg_xdbl = dout("dbg_xdbl", [2, 96, L // 2])
        dbg_xc = dout("dbg_xc", [NTL, 128, NCH * T])
        dbg_sz = dout("dbg_sz", [NTL, 128, NCH * T], BF16)
        if STAGE >= 3:
            dbg_o1 = dout("dbg_o1", [NTO, 128, ND * T])
            dbg_dt = dout("dbg_dt", [NTL, 128, NCH * T])
            dbg_y = dout("dbg_y", [NTL, 128, NCH * T], BF16)
        if STAGE >= 4:
            dbg_v = dout("dbg_v", [NTO, 128, ND * T])
            dbg_x1 = dout("dbg_x1", [NTO, 128, ND * T])
            dbg_w = dout("dbg_w", [NTO, 128, ND * T])
            dbg_mean = dout("dbg_mean", [128, ND])

    AW = k.sb("AW", [128, AW_COLS], BF16)
    AFa = k.sb("AFa", [128, AF_COLS], F32)
    AT = k.sb("AT", [128, AT_COLS], BF16)

    ones_c16 = k.sb("ones_c16", [128, 1], BF16)
    ones_r = k.sb("ones_r", [1, 128], F32)
    ones_r16 = k.sb("ones_r16", [1, 128], BF16)
    g1row = k.sb("g1row", [1, D], F32)
    g2row = g1row
    g1col16 = k.sb("g1col16", [128, ND], BF16)
    b1col16 = k.sb("b1col16", [128, ND], BF16)
    g2col16 = k.sb("g2col16", [128, ND], BF16)
    b2col16 = k.sb("b2col16", [128, ND], BF16)
    nwg2 = k.sb("nwg2", [1, DFF], BF16)
    nwg1 = nwg2[0:1, 0:2 * DH]
    rMR16 = k.sb("rMR16", [1, T], BF16)
    wb1c = k.sb("wb1c", [128, 16], F32)
    wb2c = k.sb("wb2c", [128, 32], F32)
    bdtc = k.sb("bdtc", [128, NCH], F32)
    cbc = k.sb("cbc", [128, NCH], F32)
    cwc = k.sb("cwc", [128, NCH, 4], F32)
    Acol = k.sb("Acol", [128, NCH, DS], F32)
    Dcol = k.sb("Dcol", [128, NCH], F32)
    b2c = k.sb("b2c", [128, ND], F32)
    cb1c = k.sb("cb1c", [128, ND], F32)
    bf1c = k.sb("bf1c", [128, 32], F32)
    mred = k.sb("mred", [128, ND], F32)
    cb2c = k.sb("cb2c", [128, 4], F32)
    cb3c = k.sb("cb3c", [128, 2], F32)
    gb1c = k.sb("gb1c", [128, 4], F32)
    gb2c = k.sb("gb2c", [128, 2], F32)
    cst = k.sb("cst", [1, 16], F32)
    mean_acc = k.sb("mean_acc", [128, ND], F32)
    ctx16 = k.sb("ctx16", [128, ND], BF16)
    gc_h1 = k.sb("gc_h1", [128, 4], BF16)
    gc_h2 = k.sb("gc_h2", [128, 2], BF16)
    state = k.sb("state", [128, NCH, DS], F32)

    i32 = k.sb("i32", [128, 128], F32)
    i16 = k.sb("i16", [128, 128], BF16)
    r0 = k.sb("r0", [1, T], F32)
    r1 = k.sb("r1", [1, T], F32)
    r2 = k.sb("r2", [1, T], F32)
    r3 = k.sb("r3", [1, T], F32)
    rR = k.sb("rR", [1, T], F32)
    rMR = k.sb("rMR", [1, T], F32)
    brow = [r0, r1]
    crow = [r2, r3]
    st16 = [k.sb(f"st16_{i}", [128, T], BF16) for i in range(2)]
    sqt16 = [k.sb(f"sqt16_{i}", [128, T], BF16) for i in range(2)]

    mmP = [k.psum(f"mmP{i}", [128, T]) for i in range(2)]
    repP = [k.psum(f"repP{i}", [128, T]) for i in range(4)]
    rowP = [k.psum(f"rowP{i}", [128, T]) for i in range(2)]

    SEMS = ("ldw", "prm", "misc", "lx0", "lx1", "sstat", "tstat", "vrow",
            "srow", "trep", "vrep", "tmm", "sevac", "vconv", "vtail", "sxc",
            "dspx", "dspz", "sxd", "dxdbl", "cc1",
            "ldt0", "ldt1", "lxc0", "lxc1", "lsz0", "lsz1",
            "lbr", "lcr", "tdtp", "sdt", "vdtx", "sda", "vscan", "vsdone",
            "sbr", "tbr", "tcr", "sbe", "vhc", "ghc", "scr", "gacc", "tacc",
            "ldwa", "ldwb", "ldwc", "ldwd", "do1b", "cc1b", "prmg",
            "lbr1", "lcr1",
            "vy3", "top", "so1", "do1", "cc2",
            "lo0", "lo1", "lh0", "lh1", "vv", "svm", "tvm", "vvm", "sg",
            "dspg", "dspv", "lg0", "lg1", "lv0", "lv1", "vu2", "vx1",
            "dspx1", "vw", "dspw", "vmean", "dmean", "cc3",
            "lw0", "lw1", "l10", "l11", "sgf", "vcst", "vcoup", "vtw",
            "vfin", "dout", "ddbg")
    S = {n: k.sem(n) for n in SEMS}
    k.init_selfsems()

    def W(*pairs):
        return tuple(pairs)

    def cur(name):
        return (S[name], k.val(S[name]))

    WJ_O, WXP_O = 0, 16384
    WJ = view3(AW, WJ_O, ND, 2 * DH)
    WXP = view3(AW, WXP_O, NCH, 96)
    k.dma("gpsimd", WJ, w_inj.rearrange("p (c m) -> p c m", c=ND), incs=[S["ldw"]])
    k.dma("gpsimd", WXP, w_xp.rearrange("p (c m) -> p c m", c=NCH), incs=[S["ldw"]])
    LDW_P1 = k.val(S["ldw"])
    WDTP_O, WOUT_O = 17152, 18176
    WDTP = view3(AW, WDTP_O, NCH, 128, p=64)
    WOUT = view3(AW, WOUT_O, NCH, D)
    k.dma("gpsimd", WDTP, w_dtp.rearrange("k (c m) -> k c m", c=NCH),
          incs=[S["ldwa"]])
    k.dma("gpsimd", WOUT, w_out.rearrange("p (c m) -> p c m", c=NCH),
          incs=[S["ldwa"]])
    LDW_P3 = ("ldwa", k.val(S["ldwa"]))

    for dst, src in (
        (g1row[:, :], g1r_in), (i32[:, :], ident_in),
        (bdtc[:, :], b_dtp.rearrange("(c p) o -> p (c o)", p=128)),
        (cbc[:, :], conv_b.rearrange("(c p) o -> p (c o)", p=128)),
        (cwc[:, :, :], conv_w.rearrange("(c p) m -> p c m", p=128)),
        (Acol[:, :, :], A_log.rearrange("(c p) m -> p c m", p=128)),
        (Dcol[:, :], D_in.rearrange("(c p) o -> p (c o)", p=128)),
        (b2c[:, :], b_f2.rearrange("(c p) o -> p (c o)", p=128)),
        (cb1c[:, :], cn_b1.rearrange("(c p) o -> p (c o)", p=128)),
        (bf1c[:, :], b_f1.rearrange("(c p) o -> p (c o)", p=128)),
        (cb2c[:, 0:4], cn_b2.rearrange("(c p) o -> p (c o)", p=128)),
        (cb3c[:, 0:2], cn_b3.rearrange("(c p) o -> p (c o)", p=128)),
        (gb1c[:, 0:4], gc_b1.rearrange("(c p) o -> p (c o)", p=128)),
        (gb2c[:, 0:2], gc_b2.rearrange("(c p) o -> p (c o)", p=128)),
        (cst[:, 0:1], ar_in), (cst[:, 1:2], gc_b3), (cst[:, 2:3], cn_b4),
    ):
        k.dma("sync", dst, src, incs=[S["prm"]], slow=True)
    for dst, src in (
        (g1col16[:, :], g1c_in.rearrange("(c p) o -> p (c o)", p=128)),
        (b1col16[:, :], b1c_in.rearrange("(c p) o -> p (c o)", p=128)),
        (g2col16[:, :], g2c_in.rearrange("(c p) o -> p (c o)", p=128)),
        (b2col16[:, :], b2c_in.rearrange("(c p) o -> p (c o)", p=128)),
    ):
        k.dma("gpsimd", dst, src, incs=[S["prmg"]], slow=True)
    PRM = cur("prm")
    PRMG = cur("prmg")

    def early_stop():
        k.op("gpsimd", lambda e: None, waits=[PRM, (S["ldw"], LDW_P1)])
        k.finalize()
        _NC_CACHE[(STAGE, DEBUG)] = (k.nc, None)
        return _NC_CACHE[(STAGE, DEBUG)]

    if NTL_RUN == -1:
        return early_stop()

    k.op("scalar", lambda e: e.activation(Acol[:, :, :], Acol[:, :, :], AF.Exp),
         waits=[PRM], incs=[S["misc"]])
    k.op("vector", lambda e: e.tensor_scalar_mul(Acol[:, :, :], Acol[:, :, :], -1.0),
         waits=[cur("misc")], incs=[S["misc"]])
    k.op("vector", lambda e: e.memset(ones_c16[:, :], 1.0), incs=[S["misc"]])
    k.op("vector", lambda e: e.memset(ones_r[:, :], 1.0), incs=[S["misc"]])
    k.op("vector", lambda e: e.memset(ones_r16[:, :], 1.0), incs=[S["misc"]])
    k.op("scalar", lambda e: e.activation(i16[:, :], i32[:, :], AF.Copy),
         waits=[PRM], incs=[S["misc"]])

    if NTL_RUN == -2:
        return early_stop()

    WJLD = (S["ldw"], LDW_P1)
    for mt in range((2 * DH) // T):
        for kc in range(ND):
            w = []
            if kc == 0:
                w = [WJLD, PRM, PRMG, (S["vrow"], k.val(S["vrow"]))]
            k.op("tensor", lambda e, mt=mt, kc=kc: e.matmul(
                rowP[0][0:1, :], g1col16[:, kc:kc + 1], WJ[:, kc, mt * T:(mt + 1) * T],
                start=(kc == 0), stop=(kc == ND - 1)),
                waits=w,
                incs=[S["tstat"]] if kc == ND - 1 else ())
        k.op("vector", lambda e, mt=mt: e.tensor_scalar_mul(
            nwg1[0:1, mt * T:(mt + 1) * T], rowP[0][0:1, :], -1.0),
            waits=[cur("tstat")], incs=[S["vrow"]])
    for m in range(16):
        for kc in range(ND):
            k.op("tensor", lambda e, m=m, kc=kc: e.matmul(
                repP[0][:, m:m + 1], WJ[:, kc, m * 128:(m + 1) * 128],
                b1col16[:, kc:kc + 1],
                start=(kc == 0), stop=(kc == ND - 1)),
                incs=[S["tstat"]] if (kc == ND - 1 and m == 15) else ())
    k.op("scalar", lambda e: e.activation(wb1c[:, :], repP[0][:, 0:16], AF.Copy),
         waits=[cur("tstat")], incs=[S["sevac"]])
    P0_EVAC = cur("sevac")
    P0_VROW = cur("vrow")

    if NTL_RUN == -3:
        return early_stop()

    XB_O, XI_O, XCP_O, XD_O = 0, 8192, 12312, 16408
    XC_O = XCP_O
    xbuf = [view3(AFa, XB_O + i * 4096, ND, T) for i in range(2)]
    xibuf = view3(AFa, XI_O, NCH, T + 3)
    xcpre = view3(AFa, XCP_O, NCH, T)
    xct = view3(AFa, XC_O, NCH, T)
    xdblt = AFa[0:96, XD_O:XD_O + T]
    XM_O, SZ_O, XC16_O = 0, 4096, 8192
    xmul16 = view3(AT, XM_O, ND, T)
    szt16 = view3(AT, SZ_O, NCH, T)
    xc16 = view3(AT, XC16_O, NCH, T)

    def p1_load(j):
        p = j % 2
        w = [cur("vrep")] if j >= 2 else ()
        k.dma("sync", xbuf[p][:, :, :],
              xT[j].rearrange("p (c t) -> p c t", c=ND),
              waits=w, incs=[S["lx0" if p == 0 else "lx1"]])

    k.op("vector", lambda e: e.memset(xibuf[:, :, 0:3], 0.0), incs=[S["vtail"]])
    if NTL_RUN > 0:
        p1_load(0)
    if NTL_RUN > 1:
        p1_load(1)

    for j in range(NTL_RUN):
        p = j % 2
        lxv = (S["lx0" if p == 0 else "lx1"], 16 * (j // 2 + 1))
        for c in range(ND):
            cp = c % 2
            w = [lxv, PRM, (S["tstat"], max(0, k.val(S["tstat"]) - 2))]
            k.op("scalar", lambda e, c=c, cp=cp, p=p: e.activation(
                st16[cp][:, :], xbuf[p][:, c, :], AF.Copy),
                waits=w, incs=[S["sstat"]])
            k.op("scalar", lambda e, c=c, cp=cp, p=p: e.activation(
                sqt16[cp][:, :], xbuf[p][:, c, :], AF.Square),
                incs=[S["sstat"]])
            w2 = [cur("sstat")]
            if c == 0:
                w2.append(cur("vrow"))
            k.op("tensor", lambda e, cp=cp, c=c: e.matmul(
                rowP[0][0:1, :], ones_c16[:, :], st16[cp][:, :],
                start=(c == 0), stop=(c == ND - 1)),
                waits=w2, incs=[S["tstat"]])
            k.op("tensor", lambda e, cp=cp, c=c: e.matmul(
                rowP[1][0:1, :], ones_c16[:, :], sqt16[cp][:, :],
                start=(c == 0), stop=(c == ND - 1)),
                incs=[S["tstat"]])
        tsv = cur("tstat")

        def vop(fn, waits=()):
            k.op("vector", fn, waits=waits, incs=[S["vrow"]])

        vop(lambda e: e.tensor_scalar_mul(r0[:, :], rowP[0][0:1, :], 1.0 / D),
            waits=[tsv])
        vop(lambda e: e.tensor_scalar_mul(r1[:, :], rowP[1][0:1, :], 1.0 / D))
        vop(lambda e: e.tensor_tensor(r2[:, :], r0[:, :], r0[:, :], ALU.mult))
        vop(lambda e: e.tensor_tensor(r1[:, :], r1[:, :], r2[:, :], ALU.subtract))
        vop(lambda e: e.tensor_scalar_add(r2[:, :], r1[:, :], EPS))
        k.op("scalar", lambda e: e.activation(r3[:, :], r2[:, :], AF.Ln),
             waits=[cur("vrow")], incs=[S["srow"]])
        k.op("scalar", lambda e: e.activation(r3[:, :], r3[:, :], AF.Exp,
                                              scale=-0.5), incs=[S["srow"]])
        vop(lambda e: e.reciprocal(r2[:, :], r2[:, :]), waits=[cur("srow")])
        vop(lambda e: e.tensor_tensor(r1[:, :], r1[:, :], r2[:, :], ALU.mult))
        vop(lambda e: e.tensor_scalar_add(r1[:, :], r1[:, :], EPS))
        k.op("scalar", lambda e: e.activation(r1[:, :], r1[:, :], AF.Ln),
             waits=[cur("vrow")], incs=[S["srow"]])
        k.op("scalar", lambda e: e.activation(r1[:, :], r1[:, :], AF.Exp,
                                              scale=-0.5), incs=[S["srow"]])
        vop(lambda e: e.tensor_tensor(rR[:, :], r3[:, :], r1[:, :], ALU.mult),
            waits=[cur("srow")])
        vop(lambda e: e.tensor_tensor(rMR[:, :], r0[:, :], rR[:, :], ALU.mult))
        vop(lambda e: e.tensor_copy(rMR16[:, :], rMR[:, :]))
        vrows = cur("vrow")

        for c in range(ND):
            bank = repP[c % 2]
            w = [vrows, (S["vrep"], max(0, k.val(S["vrep"]) - 1))]
            k.op("tensor", lambda e, c=c, bank=bank: e.matmul(
                bank[:, :], g1row[0:1, c * 128:(c + 1) * 128], rR[:, :],
                start=True, stop=True),
                waits=w, incs=[S["trep"]])
            k.op("vector", lambda e, c=c, bank=bank, p=p: e.tensor_tensor(
                xmul16[:, c, :], xbuf[p][:, c, :], bank[:, :], ALU.mult),
                waits=[cur("trep")], incs=[S["vrep"]])
        vxm = cur("vrep")

        for m in range(16):
            bank = mmP[m % 2]
            for kc in range(ND):
                w = []
                if kc == 0:
                    w = [vxm, (S["sevac"], max(0, k.val(S["sevac"]) - 1))]
                k.op("tensor", lambda e, m=m, kc=kc, bank=bank: e.matmul(
                    bank[:, :], WJ[:, kc, m * 128:(m + 1) * 128],
                    xmul16[:, kc, :], start=(kc == 0), stop=False),
                    waits=w)
            k.op("tensor", lambda e, m=m, bank=bank: e.matmul(
                bank[:, :], nwg1[0:1, m * 128:(m + 1) * 128], rMR16[:, :],
                start=False, stop=True), incs=[S["tmm"]])
            tev = cur("tmm")
            if m < 8:
                k.op("scalar", lambda e, m=m, bank=bank: e.add(
                    xibuf[:, m, 3:3 + T], bank[:, :], wb1c[:, m:m + 1]),
                    waits=[tev, cur("vtail"), P0_EVAC], incs=[S["sevac"]])
            else:
                k.op("scalar", lambda e, m=m, bank=bank: e.activation(
                    szt16[:, m - 8, :], bank[:, :], AF.Silu,
                    bias=wb1c[:, m:m + 1]),
                    waits=[tev] + ([cur("dspz")] if m == 8 else []),
                    incs=[S["sevac"]])
        sev = k.val(S["sevac"])

        for c in range(NCH):
            k.op("vector", lambda e, c=c: e.tensor_scalar(
                xcpre[:, c, :], xibuf[:, c, 0:T], cwc[:, c, 0:1], None, ALU.mult),
                waits=[(S["sevac"], sev - 16 + c + 1), cur("sxc"), cur("dspx")],
                incs=[S["vconv"]])
            for jj in range(1, 4):
                k.op("vector", lambda e, c=c, jj=jj: e.scalar_tensor_tensor(
                    xcpre[:, c, :], xibuf[:, c, jj:jj + T], cwc[:, c, jj:jj + 1],
                    xcpre[:, c, :], ALU.mult, ALU.add),
                    incs=[S["vconv"]])
        k.op("vector", lambda e: e.tensor_copy(xibuf[:, :, 0:3],
                                               xibuf[:, :, T:T + 3]),
             waits=[cur("vconv")], incs=[S["vtail"]])
        vcv = cur("vconv")

        for c in range(NCH):
            w = [vcv] if c == 0 else []
            k.op("scalar", lambda e, c=c: e.activation(
                xc16[:, c, :], xcpre[:, c, :], AF.Silu, bias=cbc[:, c:c + 1]),
                waits=w, incs=[S["sxc"]])
            k.op("scalar", lambda e, c=c: e.activation(
                xct[:, c, :], xcpre[:, c, :], AF.Silu, bias=cbc[:, c:c + 1]),
                incs=[S["sxc"]])
        sxcv = cur("sxc")

        k.dma("sync", sp_xc[j].rearrange("p (c t) -> p c t", c=NCH),
              xct[:, :, :], waits=[sxcv], incs=[S["dspx"]])
        k.dma("sync", sp_sz[j].rearrange("p (c t) -> p c t", c=NCH),
              szt16[:, :, :], waits=[(S["sevac"], sev)], incs=[S["dspz"]])

        for kc in range(NCH):
            w = []
            if kc == 0:
                w = [sxcv, cur("sxd"),
                     (S["sevac"], max(0, k.val(S["sevac"]) - 1))]
            k.op("tensor", lambda e, kc=kc: e.matmul(
                mmP[0][0:96, :], WXP[:, kc, :], xc16[:, kc, :],
                start=(kc == 0), stop=(kc == NCH - 1)),
                waits=w, incs=[S["tmm"]] if kc == NCH - 1 else ())
        k.op("scalar", lambda e: e.activation(xdblt[:, :], mmP[0][0:96, :],
                                              AF.Copy),
             waits=[cur("tmm"), cur("dxdbl")], incs=[S["sxd"]])
        k.dma("sync", cc_xdbl_in[j // 8][:, (j % 8) * T:(j % 8 + 1) * T],
              xdblt[:, :], waits=[cur("sxd")], incs=[S["dxdbl"]])
        if j == 7:
            k.op("gpsimd", lambda e: e.collective_compute(
                "AllReduce", ALU.add, replica_groups=GROUPS,
                ins=[cc_xdbl_in[0]], outs=[cc_xdbl[0]]),
                waits=[(S["dxdbl"], 16 * 8)], incs=[S["cc1"]])

        if j + 2 < NTL_RUN:
            p1_load(j + 2)

    k.op("gpsimd", lambda e: e.collective_compute(
        "AllReduce", ALU.add, replica_groups=GROUPS,
        ins=[cc_xdbl_in[1]], outs=[cc_xdbl[1]]),
        waits=[cur("dxdbl")], incs=[S["cc1b"]])

    if DEBUG:
        k.dma("gpsimd", dbg_xdbl, cc_xdbl, waits=[cur("cc1")], incs=[S["ddbg"]])
        k.dma("gpsimd", dbg_xc, sp_xc, waits=[cur("dspx")], incs=[S["ddbg"]])
        k.dma("gpsimd", dbg_sz, sp_sz, waits=[cur("dspz")], incs=[S["ddbg"]])

    if STAGE == 1:
        k.op("gpsimd", lambda e: None,
             waits=[cur("cc1"), cur("cc1b")] + ([cur("ddbg")] if DEBUG else []))
        k.finalize()
        _NC_CACHE[(STAGE, DEBUG)] = (k.nc, None)
        return _NC_CACHE[(STAGE, DEBUG)]

    xct3 = view3(AFa, 0, NCH, T)
    dtt = view3(AFa, 4096, NCH, T)
    yacc = view3(AFa, 8192, NCH, T)
    dtxt = view3(AFa, 12288, NCH, T)
    h_r = [AFa[0:128, 16384 + i * 512:16384 + (i + 1) * 512] for i in range(2)]
    u_t = AFa[0:128, 17408:17920]
    dtex = AFa[0:128, 17920:18432]
    hc16 = [AT[0:128, 9216 + i * T:9216 + (i + 1) * T] for i in range(4)]
    brow16 = [AFa[0:1, 18432 + i * 256:18432 + (i + 1) * 256].bitcast(BF16)
              for i in range(2)]
    crow16 = [AFa[0:1, 18944 + i * 256:18944 + (i + 1) * 256].bitcast(BF16)
              for i in range(2)]
    dAr = [AFa[0:128, 19456 + i * 512:19456 + (i + 1) * 512] for i in range(2)]
    yscr = AFa[0:128, 8192:8704]
    crep_sb = [AFa[0:128, 20480 + i * 512:20480 + (i + 1) * 512]
               for i in range(2)]
    o1ev = [AW[0:128, 26368 + i * 1024:26368 + (i + 1) * 1024].bitcast(F32)
            for i in range(2)]
    y3_16 = view3(AT, 0, NCH, T)
    szt3 = view3(AT, 4096, NCH, T)
    dtraw16 = [AT[0:64, 8192 + i * T:8192 + (i + 1) * T] for i in range(2)]
    brP = [repP[0], repP[1]]
    crP = [repP[2], repP[3]]

    k.op("vector", lambda e: e.memset(state[:, :, :], 0.0), incs=[S["misc"]])
    CC1A = (S["cc1"], 1)
    CC1B = (S["cc1b"], 1)
    P1DONE = [cur("vrep"), cur("tmm"), cur("sxc"), cur("vrow"), cur("srow"),
              cur("sevac"), cur("dspx"), cur("dspz"), cur("dxdbl")]

    def p3_load(j):
        p = j % 2
        cc = CC1A if j < 8 else CC1B
        dw = [cc, (S["tdtp"], max(0, k.val(S["tdtp"]) - 8))]
        if j == 0:
            dw += P1DONE
        k.dma("gpsimd", dtraw16[p],
              cc_xdbl[j // 8][0:64, (j % 8) * T:(j % 8 + 1) * T],
              waits=dw,
              incs=[S["ldt0" if p == 0 else "ldt1"]])
        xw = [(S["dspx"], 16 * (j + 1)), (S["vy3"], k.val(S["vy3"]))]
        zw = [(S["dspz"], 16 * (j + 1)), (S["vy3"], k.val(S["vy3"]))]
        if j == 0:
            xw += P1DONE
            zw += P1DONE
        k.dma("sync", xct3[:, :, :],
              sp_xc[j].rearrange("p (c t) -> p c t", c=NCH),
              waits=xw, incs=[S["lxc0" if p == 0 else "lxc1"]])
        k.dma("sync", szt3[:, :, :],
              sp_sz[j].rearrange("p (c t) -> p c t", c=NCH),
              waits=zw, incs=[S["lsz0" if p == 0 else "lsz1"]])

    def p3_row_load(j, s, sp):
        cc = CC1A if j < 8 else CC1B
        extra = P1DONE if j == 0 else []
        k.dma("gpsimd", brow16[sp][:, :],
              cc_xdbl[j // 8][64 + s:65 + s, (j % 8) * T:(j % 8 + 1) * T],
              waits=[cc, (S["tbr"], k.val(S["tbr"]))] + extra,
              incs=[S["gacc" if sp == 0 else "ghc"]])
        k.dma("gpsimd", crow16[sp][:, :],
              cc_xdbl[j // 8][80 + s:81 + s, (j % 8) * T:(j % 8 + 1) * T],
              waits=[cc, (S["tcr"], k.val(S["tcr"]))] + extra,
              incs=[S["scr" if sp == 0 else "sbe"]])

    p3_load(0)
    for s in range(2):
        p3_row_load(0, s, s % 2)
    nrow_loaded = [2]
    it3 = [0]

    for j in range(NTL):
        p = j % 2
        ldt = (S["ldt0" if p == 0 else "ldt1"], 16 * (j // 2 + 1))
        lxc = (S["lxc0" if p == 0 else "lxc1"], 16 * (j // 2 + 1))
        lsz = (S["lsz0" if p == 0 else "lsz1"], 16 * (j // 2 + 1))
        for m in range(NCH):
            bank = mmP[m % 2]
            k.op("tensor", lambda e, m=m, bank=bank, p=p: e.matmul(
                bank[:, :], WDTP[:, m, :], dtraw16[p],
                start=True, stop=True),
                waits=[ldt, (S[LDW_P3[0]], LDW_P3[1]),
                       (S["sdt"], max(0, k.val(S["sdt"]) - 2)),
                       (S["so1"], 8 * j)],
                incs=[S["tdtp"]])
            k.op("scalar", lambda e, m=m, bank=bank: e.activation(
                dtex[:, :], bank[:, :], AF.Exp, bias=bdtc[:, m:m + 1]),
                waits=[cur("tdtp")], incs=[S["sdt"]])
            k.op("scalar", lambda e, m=m: e.activation(
                dtt[:, m, :], dtex[:, :], AF.Ln, bias=1.0),
                incs=[S["sdt"]])
        sdtv = cur("sdt")
        if DEBUG:
            k.dma("gpsimd", dbg_dt[j].rearrange("p (c t) -> p c t", c=NCH),
                  dtt[:, :, :], waits=[sdtv], incs=[S["ddbg"]])
        for m in range(NCH):
            w = []
            if m == 0:
                w = [sdtv, lxc]
            k.op("vector", lambda e, m=m: e.tensor_tensor(
                dtxt[:, m, :], dtt[:, m, :], xct3[:, m, :], ALU.mult),
                waits=w, incs=[S["vdtx"]])
        vdtxv = cur("vdtx")
        yPb = [rowP[0], rowP[1], mmP[0], mmP[1]]
        for H in range(2):
            for s in range(DS):
                gi = 32 * j + 16 * H + s
                sp = gi % 2
                bw = [(S["gacc" if sp == 0 else "ghc"],
                       16 * (gi // 2 + 1))]
                if gi >= 2:
                    bw.append((S["vscan"], 4 * (gi - 1)))
                if gi == 0:
                    bw.append((S["misc"], k.val(S["misc"])))
                k.op("tensor", lambda e, sp=sp: e.matmul(
                    brP[sp][:, :], ones_r16[:, :], brow16[sp][:, :],
                    start=True, stop=True),
                    waits=bw, incs=[S["tbr"]])
                cw = [(S["scr" if sp == 0 else "sbe"],
                       16 * (gi // 2 + 1))]
                if gi >= 2:
                    cw.append((S["vhc"], 4 * (gi - 1)))
                k.op("tensor", lambda e, sp=sp: e.matmul(
                    crP[sp][:, :], ones_r16[:, :], crow16[sp][:, :],
                    start=True, stop=True),
                    waits=cw, incs=[S["tcr"]])
                if nrow_loaded[0] < 32 * NTL:
                    n = nrow_loaded[0]
                    p3_row_load(n // 32, n % 16, n % 2)
                    nrow_loaded[0] += 1
                for kc in range(4 * H, 4 * H + 4):
                    i = it3[0]
                    k.op("scalar", lambda e, kc=kc, s=s, i=i: e.activation(
                        dAr[i % 2][:, :], dtt[:, kc, :], AF.Exp,
                        scale=Acol[:, kc, s:s + 1]),
                        waits=[(S["vscan"], max(0, k.val(S["vscan"]) - 1))],
                        incs=[S["sda"]])
                    k.op("vector", lambda e, kc=kc, sp=sp: e.tensor_tensor(
                        u_t[:, :], dtxt[:, kc, :], brP[sp][:, :], ALU.mult),
                        waits=([vdtxv, (S["tbr"], k.val(S["tbr"]))]
                               if kc == 4 * H else ()))
                    k.op("vector",
                         lambda e, kc=kc, s=s, i=i: e.tensor_tensor_scan(
                             h_r[i % 2][:, :], dAr[i % 2][:, :], u_t[:, :],
                             state[:, kc, s:s + 1], ALU.mult, ALU.add),
                         waits=[(S["sda"], k.val(S["sda"])),
                                (S["lbr"], max(0, k.val(S["lbr"]) - 1))],
                         incs=[S["vscan"]])
                    hw = [(S["tacc"], max(0, k.val(S["tacc"]) - 3))]
                    if kc == 4 * H:
                        hw.append((S["tcr"], k.val(S["tcr"])))
                    k.op("vector", lambda e, i=i, sp=sp: e.tensor_tensor(
                        hc16[i % 4][:, :], h_r[i % 2][:, :], crP[sp][:, :],
                        ALU.mult),
                        waits=hw, incs=[S["vhc"]])
                    k.op("scalar",
                         lambda e, kc=kc, s=s, i=i: e.activation(
                             state[:, kc, s:s + 1], h_r[i % 2][:, T - 1:T],
                             AF.Copy),
                         waits=[(S["vscan"], k.val(S["vscan"]))],
                         incs=[S["lbr"]])
                    tw = [(S["vhc"], k.val(S["vhc"]))]
                    if s == 0 and kc == 4 * H:
                        if H == 0:
                            tw.append((S["vy3"], k.val(S["vy3"])))
                            if j == 0:
                                tw += P1DONE
                        else:
                            tw.append((S["sdt"], k.val(S["sdt"])))
                    k.op("tensor", lambda e, kc=kc, s=s, i=i: e.matmul(
                        yPb[kc % 4][:, :], i16[:, :], hc16[i % 4][:, :],
                        start=(s == 0), stop=(s == DS - 1)),
                        waits=tw, incs=[S["tacc"]])
                    it3[0] += 1
            for kc in range(4 * H, 4 * H + 4):
                sw = []
                if kc == 4 * H:
                    sw = [(S["tacc"], k.val(S["tacc"]))]
                    if H == 0:
                        sw += [lsz, (S["top"], k.val(S["top"]))]
                k.op("vector", lambda e, kc=kc: e.scalar_tensor_tensor(
                    yscr[:, :], xct3[:, kc, :], Dcol[:, kc:kc + 1],
                    yPb[kc % 4][:, :], ALU.mult, ALU.add),
                    waits=sw, incs=[S["vy3"]])
                k.op("vector", lambda e, kc=kc: e.tensor_tensor(
                    y3_16[:, kc, :], yscr[:, :], szt3[:, kc, :], ALU.mult),
                    incs=[S["vy3"]])
        vy3v = cur("vy3")
        if DEBUG:
            k.dma("gpsimd", dbg_y[j].rearrange("p (c t) -> p c t", c=NCH),
                  y3_16[:, :, :], waits=[vy3v], incs=[S["ddbg"]])
        if j + 1 < NTL:
            p3_load(j + 1)
        for m in range(ND):
            bank = mmP[m % 2]
            for kc in range(NCH):
                w = []
                if kc == 0:
                    w = [vy3v, (S["so1"], max(0, k.val(S["so1"]) - 1))]
                k.op("tensor", lambda e, m=m, kc=kc, bank=bank: e.matmul(
                    bank[:, :], WOUT[:, kc, m * 128:(m + 1) * 128],
                    y3_16[:, kc, :], start=(kc == 0), stop=(kc == NCH - 1)),
                    waits=w, incs=[S["top"]] if kc == NCH - 1 else ())
            dsem = S["do1" if m % 2 == 0 else "do1b"]
            k.op("scalar", lambda e, m=m, bank=bank: e.activation(
                o1ev[m % 2][:, :], bank[:, :], AF.Copy),
                waits=[cur("top"), (dsem, k.val(dsem))],
                incs=[S["so1"]])
            k.dma("sync", cc_o1_in[j // 8, j % 8][:, m * T:(m + 1) * T],
                  o1ev[m % 2][:, :],
                  waits=[cur("so1")], incs=[dsem])

    k.op("gpsimd", lambda e: e.collective_compute(
        "ReduceScatter", ALU.add, replica_groups=GROUPS,
        ins=[cc_o1_in], outs=[cc_o1]),
        waits=[(S["do1"], k.val(S["do1"])), (S["do1b"], k.val(S["do1b"]))],
        incs=[S["cc2"]])

    if DEBUG:
        k.dma("gpsimd", dbg_o1, cc_o1, waits=[cur("cc2")], incs=[S["ddbg"]])

    if STAGE == 3:
        k.op("gpsimd", lambda e: None,
             waits=[cur("cc2")] + ([cur("ddbg")] if DEBUG else []))
        k.finalize()
        _NC_CACHE[(STAGE, DEBUG)] = (k.nc, None)
        return _NC_CACHE[(STAGE, DEBUG)]

    CC2 = cur("cc2")

    def ln_chain(stats_src, nchunks, pre_waits, want_mr16=False):
        for c in range(nchunks):
            cp = c % 2
            w = list(pre_waits) if c == 0 else []
            w.append((S["tstat"], max(0, k.val(S["tstat"]) - 2)))
            k.op("scalar", lambda e, c=c, cp=cp: e.activation(
                st16[cp][:, :], stats_src[:, c, :], AF.Copy),
                waits=w, incs=[S["sstat"]])
            k.op("scalar", lambda e, c=c, cp=cp: e.activation(
                sqt16[cp][:, :], stats_src[:, c, :], AF.Square),
                incs=[S["sstat"]])
            w2 = [cur("sstat")]
            if c == 0:
                w2.append(cur("vrow"))
            k.op("tensor", lambda e, cp=cp, c=c, n=nchunks: e.matmul(
                rowP[0][0:1, :], ones_c16[:, :], st16[cp][:, :],
                start=(c == 0), stop=(c == n - 1)),
                waits=w2, incs=[S["tstat"]])
            k.op("tensor", lambda e, cp=cp, c=c, n=nchunks: e.matmul(
                rowP[1][0:1, :], ones_c16[:, :], sqt16[cp][:, :],
                start=(c == 0), stop=(c == n - 1)),
                incs=[S["tstat"]])
        tsv = cur("tstat")

        def vop(fn, waits=()):
            k.op("vector", fn, waits=waits, incs=[S["vrow"]])

        vop(lambda e: e.tensor_scalar_mul(r0[:, :], rowP[0][0:1, :], 1.0 / D),
            waits=[tsv])
        vop(lambda e: e.tensor_scalar_mul(r1[:, :], rowP[1][0:1, :], 1.0 / D))
        vop(lambda e: e.tensor_tensor(r2[:, :], r0[:, :], r0[:, :], ALU.mult))
        vop(lambda e: e.tensor_tensor(r1[:, :], r1[:, :], r2[:, :], ALU.subtract))
        vop(lambda e: e.tensor_scalar_add(r1[:, :], r1[:, :], EPS))
        k.op("scalar", lambda e: e.activation(r1[:, :], r1[:, :], AF.Ln),
             waits=[cur("vrow")], incs=[S["srow"]])
        k.op("scalar", lambda e: e.activation(rR[:, :], r1[:, :], AF.Exp,
                                              scale=-0.5), incs=[S["srow"]])
        vop(lambda e: e.tensor_tensor(rMR[:, :], r0[:, :], rR[:, :], ALU.mult),
            waits=[cur("srow")])
        if want_mr16:
            vop(lambda e: e.tensor_copy(rMR16[:, :], rMR[:, :]))
        return cur("vrow")

    o1t = view3(AFa, 0, ND, T)
    xht = view3(AFa, 4096, ND, T)
    vt = view3(AFa, 8192, ND, T)
    u2t = view3(AFa, 12288, ND, T)
    x1t = view3(AFa, 16384 - 4096 + 4096, ND, T) if False else None
    x1t = o1t
    vm16 = view3(AT, 0, ND, T)
    g16 = view3(AT, 4096, ND, T)
    g16h = view3(AT, 0, 16, T)

    WF1V = view3(AW, 0, ND, DFF)
    k.dma("gpsimd", WF1V, w_f1.rearrange("p (c m) -> p c m", c=ND),
          waits=[CC2], incs=[S["ldwb"]])
    LDW_F1 = k.val(S["ldwb"])
    k.dma("sync", g2row[:, :], g2r_in, waits=[CC2], incs=[S["prm"]])
    G2R_LD = (S["prm"], k.val(S["prm"]))
    for mt in range(DFF // T):
        for kc in range(ND):
            w = []
            if kc == 0:
                w = [(S["ldwb"], LDW_F1), PRM, PRMG,
                     (S["vrow"], k.val(S["vrow"]))]
            k.op("tensor", lambda e, mt=mt, kc=kc: e.matmul(
                rowP[0][0:1, :], g2col16[:, kc:kc + 1],
                WF1V[:, kc, mt * T:(mt + 1) * T],
                start=(kc == 0), stop=(kc == ND - 1)),
                waits=w, incs=[S["tstat"]] if kc == ND - 1 else ())
        k.op("vector", lambda e, mt=mt: e.tensor_scalar_mul(
            nwg2[0:1, mt * T:(mt + 1) * T], rowP[0][0:1, :], -1.0),
            waits=[cur("tstat")], incs=[S["vrow"]])
    for m in range(32):
        for kc in range(ND):
            k.op("tensor", lambda e, m=m, kc=kc: e.matmul(
                repP[0][:, m:m + 1], WF1V[:, kc, m * 128:(m + 1) * 128],
                b2col16[:, kc:kc + 1],
                start=(kc == 0), stop=(kc == ND - 1)),
                incs=[S["tstat"]] if (kc == ND - 1 and m == 31) else ())
    k.op("scalar", lambda e: e.activation(wb2c[:, :], repP[0][:, 0:32], AF.Copy),
         waits=[cur("tstat")], incs=[S["sevac"]])
    k.op("vector", lambda e: e.tensor_tensor(wb2c[:, :], wb2c[:, :],
                                             bf1c[:, :], ALU.add),
         waits=[cur("sevac"), PRM], incs=[S["vrow"]])
    P4A_RDY = [(S["vrow"], k.val(S["vrow"])), (S["sevac"], k.val(S["sevac"]))]

    def p4a_load(i):
        p = i % 2
        k.dma("sync", o1t[:, :, :], cc_o1[i].rearrange("p (c t) -> p c t", c=ND),
              waits=[CC2, (S["vv"], k.val(S["vv"]))],
              incs=[S["lo0" if p == 0 else "lo1"]])
        k.dma("sync", xht[:, :, :],
              xTh[i].rearrange("p (c t) -> p c t", c=ND),
              waits=[CC2, (S["vvm"], k.val(S["vvm"]))],
              incs=[S["lh0" if p == 0 else "lh1"]])

    p4a_load(0)
    for i in range(NTO):
        p = i % 2
        lo = (S["lo0" if p == 0 else "lo1"], 16 * (i // 2 + 1))
        lh = (S["lh0" if p == 0 else "lh1"], 16 * (i // 2 + 1))
        vr = ln_chain(xht, ND, [lh] + (P4A_RDY if i == 0 else []))
        k.op("tensor", lambda e: e.matmul(repP[0][:, :], ones_r[:, :],
                                          rR[:, :], start=True, stop=True),
             waits=[vr, (S["vv"], k.val(S["vv"]))], incs=[S["trep"]])
        k.op("tensor", lambda e: e.matmul(repP[1][:, :], ones_r[:, :],
                                          rMR[:, :], start=True, stop=True),
             incs=[S["trep"]])
        trv = cur("trep")
        for c in range(ND):
            k.op("vector", lambda e, c=c: e.tensor_tensor(
                vt[:, c, :], xht[:, c, :], repP[0][:, :], ALU.mult),
                waits=[trv, lo, (S["dspv"], k.val(S["dspv"]))]
                if c == 0 else (), incs=[S["vv"]])
            k.op("vector", lambda e, c=c: e.tensor_tensor(
                vt[:, c, :], vt[:, c, :], repP[1][:, :], ALU.subtract))
            k.op("vector", lambda e, c=c: e.tensor_tensor(
                vt[:, c, :], vt[:, c, :], o1t[:, c, :], ALU.add),
                incs=[S["vv"]])
        vvv = cur("vv")
        k.dma("sync", sp_v[i].rearrange("p (c t) -> p c t", c=ND), vt[:, :, :],
              waits=[vvv], incs=[S["dspv"]])
        vr2 = ln_chain(vt, ND, [vvv], want_mr16=True)
        for c in range(ND):
            bank = repP[2 + c % 2]
            k.op("tensor", lambda e, c=c, bank=bank: e.matmul(
                bank[:, :], g2row[0:1, c * 128:(c + 1) * 128], rR[:, :],
                start=True, stop=True),
                waits=[vr2, (S["vvm"], max(0, k.val(S["vvm"]) - 1))] +
                      ([G2R_LD] if i == 0 and c == 0 else []),
                incs=[S["trep"]])
            k.op("vector", lambda e, c=c, bank=bank: e.tensor_tensor(
                vm16[:, c, :], vt[:, c, :], bank[:, :], ALU.mult),
                waits=[cur("trep")], incs=[S["vvm"]])
        vmv = cur("vvm")
        for m in range(32):
            bank = mmP[m % 2]
            for kc in range(ND):
                w = []
                if kc == 0:
                    w = [vmv, (S["sg"], max(0, k.val(S["sg"]) - 1))]
                k.op("tensor", lambda e, m=m, kc=kc, bank=bank: e.matmul(
                    bank[:, :], WF1V[:, kc, m * 128:(m + 1) * 128],
                    vm16[:, kc, :], start=(kc == 0), stop=False),
                    waits=w)
            k.op("tensor", lambda e, m=m, bank=bank: e.matmul(
                bank[:, :], nwg2[0:1, m * 128:(m + 1) * 128], rMR16[:, :],
                start=False, stop=True), incs=[S["tmm"]])
            k.op("scalar", lambda e, m=m, bank=bank: e.activation(
                g16[:, m % 8, :], bank[:, :], AF.Gelu_apprx_tanh,
                bias=wb2c[:, m:m + 1]),
                waits=[cur("tmm")] +
                      ([(S["dspg"], 16 * (4 * i + m // 8))]
                       if m % 8 == 0 else []),
                incs=[S["sg"]])
            if m % 8 == 7:
                k.dma("sync", sp_g[i][:, (m - 7) * T:(m + 1) * T],
                      g16[:, :, :], waits=[cur("sg")], incs=[S["dspg"]])
        if i + 1 < NTO:
            p4a_load(i + 1)

    P4A_DONE = [(S["tmm"], k.val(S["tmm"])), (S["sg"], k.val(S["sg"])),
                (S["vvm"], k.val(S["vvm"]))]
    WF2V = view3(AW, 0, 32, D)
    k.dma("gpsimd", WF2V, w_f2.rearrange("p (c m) -> p c m", c=32),
          waits=P4A_DONE, incs=[S["ldwc"]])
    LDW_F2 = k.val(S["ldwc"])

    def p4b_load(i, half):
        p = (2 * i + half) % 2
        w = [(S["dspg"], 16 * 4 * (NTO if i == 0 and half == 0 else i + 1)),
             (S["tvm"], k.val(S["tvm"])),
             (S["tmm"], k.val(S["tmm"]))]
        if i == 0 and half == 0:
            w += P4A_DONE + [(S["vv"], k.val(S["vv"]))]
        k.dma("sync", g16h[:, :, :],
              sp_g[i][:, half * 16 * T:(half + 1) * 16 * T],
              waits=w, incs=[S["lg0" if p == 0 else "lg1"]])

    def p4b_load_vx(i):
        w1 = [(S["dspv"], 16 * (NTO if i == 0 else i + 1)),
              (S["vu2"], k.val(S["vu2"]))]
        w2 = [(S["vx1"], k.val(S["vx1"]))]
        if i == 0:
            w1 += P4A_DONE
            w2 += P4A_DONE + [(S["vv"], k.val(S["vv"]))]
        k.dma("sync", vt[:, :, :], sp_v[i].rearrange("p (c t) -> p c t", c=ND),
              waits=w1, incs=[S["lv0" if i % 2 == 0 else "lv1"]])
        k.dma("sync", xht[:, :, :], xTh[i].rearrange("p (c t) -> p c t", c=ND),
              waits=w2, incs=[S["lh0" if i % 2 == 0 else "lh1"]])

    LH_BASE = k.val(S["lh0"]), k.val(S["lh1"])
    p4b_load(0, 0)
    p4b_load_vx(0)
    for i in range(NTO):
        p = i % 2
        lv = (S["lv0" if p == 0 else "lv1"], 16 * (i // 2 + 1))
        lh = (S["lh0" if p == 0 else "lh1"],
              (LH_BASE[p] if p == 0 else LH_BASE[1]) + 16 * (i // 2 + 1))
        for half in range(2):
            lg = (S["lg0" if (2 * i + half) % 2 == 0 else "lg1"],
                  16 * ((2 * i + half) // 2 + 1))
            banks = [mmP[0], mmP[1], repP[0], repP[1], repP[2], repP[3],
                     rowP[0], rowP[1]]
            for m in range(ND):
                for kcl in range(16):
                    kc = half * 16 + kcl
                    w = []
                    if m == 0 and kcl == 0:
                        w = [lg, (S["ldwc"], LDW_F2)]
                        if half == 0:
                            w += [(S["vu2"], k.val(S["vu2"])),
                                  (S["vx1"], k.val(S["vx1"])),
                                  (S["vw"], k.val(S["vw"]))]
                    incs = []
                    if half == 1 and kcl == 15:
                        incs = [S["tmm"]]
                    elif half == 0 and kcl == 15 and m == ND - 1:
                        incs = [S["tvm"]]
                    k.op("tensor", lambda e, m=m, kc=kc, kcl=kcl, half=half,
                         bank=banks[m]: e.matmul(
                        bank[:, :], WF2V[:, kc, m * 128:(m + 1) * 128],
                        g16h[:, kcl, :],
                        start=(half == 0 and kcl == 0),
                        stop=(half == 1 and kcl == 15)),
                        waits=w, incs=incs)
            if half == 0:
                p4b_load(i, 1)
        tff2 = cur("tmm")
        banks = [mmP[0], mmP[1], repP[0], repP[1], repP[2], repP[3],
                 rowP[0], rowP[1]]
        for c in range(ND):
            k.op("vector", lambda e, c=c, bank=banks[c]: e.scalar_tensor_tensor(
                u2t[:, c, :], bank[:, :], b2c[:, c:c + 1], vt[:, c, :],
                ALU.add, ALU.add),
                waits=[tff2, lv, (S["dspw"], k.val(S["dspw"]))]
                if c == 0 else (), incs=[S["vu2"]])
        vu2v = cur("vu2")
        vr3 = ln_chain(u2t, ND, [vu2v])
        k.op("tensor", lambda e: e.matmul(repP[0][:, :], ones_r[:, :],
                                          rR[:, :], start=True, stop=True),
             waits=[vr3, (S["vx1"], k.val(S["vx1"]))], incs=[S["trep"]])
        k.op("tensor", lambda e: e.matmul(repP[1][:, :], ones_r[:, :],
                                          rMR[:, :], start=True, stop=True),
             incs=[S["trep"]])
        trv = cur("trep")
        for c in range(ND):
            k.op("vector", lambda e, c=c: e.tensor_tensor(
                x1t[:, c, :], u2t[:, c, :], repP[0][:, :], ALU.mult),
                waits=[trv, lh, (S["dspx1"], k.val(S["dspx1"]))]
                if c == 0 else (), incs=[S["vx1"]])
            k.op("vector", lambda e, c=c: e.tensor_tensor(
                x1t[:, c, :], x1t[:, c, :], repP[1][:, :], ALU.subtract))
            k.op("vector", lambda e, c=c: e.tensor_tensor(
                x1t[:, c, :], x1t[:, c, :], xht[:, c, :], ALU.add),
                incs=[S["vx1"]])
        vx1v = cur("vx1")
        k.dma("sync", sp_x1[i].rearrange("p (c t) -> p c t", c=ND),
              x1t[:, :, :], waits=[vx1v], incs=[S["dspx1"]])
        vr4 = ln_chain(x1t, ND, [vx1v])
        k.op("tensor", lambda e: e.matmul(repP[2][:, :], ones_r[:, :],
                                          rR[:, :], start=True, stop=True),
             waits=[vr4, (S["vw"], k.val(S["vw"]))], incs=[S["trep"]])
        k.op("tensor", lambda e: e.matmul(repP[3][:, :], ones_r[:, :],
                                          rMR[:, :], start=True, stop=True),
             incs=[S["trep"]])
        trv = cur("trep")
        for c in range(ND):
            k.op("vector", lambda e, c=c: e.tensor_tensor(
                u2t[:, c, :], x1t[:, c, :], repP[2][:, :], ALU.mult),
                waits=[trv] if c == 0 else (), incs=[S["vw"]])
            k.op("vector", lambda e, c=c: e.tensor_tensor(
                u2t[:, c, :], u2t[:, c, :], repP[3][:, :], ALU.subtract),
                incs=[S["vw"]])
        vwv = cur("vw")
        k.dma("sync", sp_w[i].rearrange("p (c t) -> p c t", c=ND),
              u2t[:, :, :], waits=[vwv], incs=[S["dspw"]])
        k.op("vector", lambda e: e.tensor_reduce(
            mred[:, :], u2t[:, :, :], AX.X, ALU.add),
            waits=[vwv], incs=[S["vmean"]])
        k.op("vector", lambda e: e.nop(), waits=[cur("vmean")])
        if i == 0:
            k.op("vector", lambda e: e.tensor_copy(mean_acc[:, :], mred[:, :]),
                 incs=[S["vmean"]])
        else:
            k.op("vector", lambda e: e.tensor_tensor(
                mean_acc[:, :], mean_acc[:, :], mred[:, :], ALU.add),
                incs=[S["vmean"]])
        if i + 1 < NTO:
            p4b_load(i + 1, 0)
            p4b_load_vx(i + 1)

    k.dma("sync", cc_mean_in, mean_acc[:, :], waits=[cur("vmean")],
          incs=[S["dmean"]])
    k.op("gpsimd", lambda e: e.collective_compute(
        "AllReduce", ALU.add, replica_groups=GROUPS,
        ins=[cc_mean_in], outs=[cc_mean]),
        waits=[cur("dmean")], incs=[S["cc3"]])

    if DEBUG:
        k.dma("gpsimd", dbg_v, sp_v, waits=[cur("dspv")], incs=[S["ddbg"]])
        k.dma("gpsimd", dbg_x1, sp_x1, waits=[cur("dspx1")], incs=[S["ddbg"]])
        k.dma("gpsimd", dbg_w, sp_w, waits=[cur("dspw")], incs=[S["ddbg"]])
        k.dma("gpsimd", dbg_mean, cc_mean, waits=[cur("cc3")], incs=[S["ddbg"]])

    if STAGE == 4:
        k.op("gpsimd", lambda e: None,
             waits=[cur("cc3")] + ([cur("ddbg")] if DEBUG else []))
        k.finalize()
        _NC_CACHE[(STAGE, DEBUG)] = (k.nc, None)
        return _NC_CACHE[(STAGE, DEBUG)]

    P4B_TMM = (S["tmm"], k.val(S["tmm"]))
    WCN1 = view3(AW, 0, ND, D)
    WCN2 = view3(AW, 8192, ND, 512)
    WCN3 = view3(AW, 12288, 4, 256)
    WCN4 = view3(AW, 13312, 2, 1)
    WGC1 = view3(AW, 13320, ND, 512)
    WGC2 = view3(AW, 17416, 4, 256)
    WGC3 = view3(AW, 18440, 2, 1)
    for dst, srcw, c in ((WCN1, cn_w1, ND), (WCN2, cn_w2, ND),
                         (WCN3, cn_w3, 4), (WCN4, cn_w4, 2),
                         (WGC1, gc_w1, ND), (WGC2, gc_w2, 4),
                         (WGC3, gc_w3, 2)):
        k.dma("gpsimd", dst, srcw.rearrange("p (c m) -> p c m", c=c),
              waits=[P4B_TMM], incs=[S["ldwd"]])
    LDW_P5 = k.val(S["ldwd"])

    k.dma("sync", mean_acc[:, :], cc_mean, waits=[cur("cc3")], incs=[S["dmean"]])
    k.op("vector", lambda e: e.tensor_copy(ctx16[:, :], mean_acc[:, :]),
         waits=[cur("dmean")], incs=[S["vcst"]])
    k.op("vector", lambda e: e.tensor_scalar_mul(ctx16[:, :], ctx16[:, :], 0.0),
         waits=[(S["vcst"], 0)], incs=[]) if False else None
    k.op("vector", lambda e: e.tensor_scalar_mul(ctx16[:, :], mean_acc[:, :],
                                                 1.0 / L),
         waits=[cur("vcst")], incs=[S["vcst"]])
    CTXV = cur("vcst")
    for m in range(4):
        for kc in range(ND):
            w = []
            if m == 0 and kc == 0:
                w = [CTXV, (S["ldwd"], LDW_P5)]
            k.op("tensor", lambda e, m=m, kc=kc: e.matmul(
                repP[0][:, m:m + 1], WGC1[:, kc, m * 128:(m + 1) * 128],
                ctx16[:, kc:kc + 1], start=(kc == 0), stop=(kc == ND - 1)),
                waits=w, incs=[S["trep"]] if kc == ND - 1 else ())
        k.op("scalar", lambda e, m=m: e.activation(
            gc_h1[:, m:m + 1], repP[0][:, m:m + 1], AF.Gelu,
            bias=gb1c[:, m:m + 1]),
            waits=[cur("trep")], incs=[S["sgf"]])
    for m in range(2):
        for kc in range(4):
            k.op("tensor", lambda e, m=m, kc=kc: e.matmul(
                repP[1][:, m:m + 1], WGC2[:, kc, m * 128:(m + 1) * 128],
                gc_h1[:, kc:kc + 1], start=(kc == 0), stop=(kc == 3)),
                waits=[cur("sgf")] if kc == 0 else (),
                incs=[S["trep"]] if kc == 3 else ())
        k.op("scalar", lambda e, m=m: e.activation(
            gc_h2[:, m:m + 1], repP[1][:, m:m + 1], AF.Gelu,
            bias=gb2c[:, m:m + 1]),
            waits=[cur("trep")], incs=[S["sgf"]])
    for kc in range(2):
        k.op("tensor", lambda e, kc=kc: e.matmul(
            rowP[0][0:1, 0:1], WGC3[:, kc, 0:1], gc_h2[:, kc:kc + 1],
            start=(kc == 0), stop=(kc == 1)),
            waits=[cur("sgf")] if kc == 0 else (),
            incs=[S["trep"]] if kc == 1 else ())
    def cst_op(fn, extra=()):
        k.op("vector", fn, waits=[cur("vcst")] + list(extra), incs=[S["vcst"]])

    cst_op(lambda e: e.tensor_scalar(cst[:, 3:4], cst[:, 1:2], 0.5, None,
                                     ALU.mult), extra=[PRM])
    k.op("scalar", lambda e: e.activation(cst[:, 5:6], rowP[0][0:1, 0:1],
                                          AF.Tanh, scale=0.5,
                                          bias=cst[:, 3:4]),
         waits=[cur("trep"), cur("vcst")], incs=[S["sgf"]])
    cst_op(lambda e: e.tensor_scalar(cst[:, 6:7], cst[:, 5:6], 0.5, 0.5,
                                     ALU.mult, ALU.add), extra=[cur("sgf")])
    cst_op(lambda e: e.tensor_scalar(cst[:, 7:8], cst[:, 6:7], cst[:, 0:1],
                                     None, ALU.mult))
    cst_op(lambda e: e.tensor_scalar(cst[:, 8:9], cst[:, 7:8], 1.4, 0.1,
                                     ALU.mult, ALU.add))
    cst_op(lambda e: e.tensor_scalar(cst[:, 9:10], cst[:, 8:9], cst[:, 0:1],
                                     None, ALU.subtract))
    cst_op(lambda e: e.tensor_scalar(cst[:, 11:12], cst[:, 0:1], 0.6, None,
                                     ALU.mult))
    cst_op(lambda e: e.tensor_scalar(cst[:, 10:11], cst[:, 11:12], 0.5,
                                     cst[:, 9:10], ALU.mult, ALU.add))
    cst_op(lambda e: e.tensor_scalar(cst[:, 12:13], cst[:, 11:12], 0.5, None,
                                     ALU.mult))
    cst_op(lambda e: e.tensor_scalar(cst[:, 4:5], cst[:, 2:3], 0.5, None,
                                     ALU.mult))
    GFV = cur("vcst")

    outs5 = [view3(AFa, 0, ND, T), view3(AFa, 4096, ND, T)]
    x1t5 = view3(AFa, 8192, ND, T)
    twsc = AFa[0:128, 12288:12800]
    o16 = view3(AT, 0, ND, T)
    c1_16 = view3(AT, 4096, ND, T)
    c2_16 = view3(AT, 8192, 4, T)
    c3_16 = view3(AT, 10240, 2, T)
    SIGN = [ALU.add, ALU.add, ALU.subtract, ALU.subtract,
            ALU.subtract, ALU.subtract, ALU.add, ALU.add]
    P4B_ALLSP = [(S["dspw"], k.val(S["dspw"])), (S["dspx1"], k.val(S["dspx1"]))]

    def p5_load(i):
        w1 = [(S["dspw"], 16 * (i + 1))]
        w2 = [(S["dspx1"], 16 * (i + 1)), (S["vfin"], k.val(S["vfin"]))]
        if i == 0:
            w1 += P4B_ALLSP + [(S["vmean"], k.val(S["vmean"]))]
            w2 += P4B_ALLSP
        else:
            w1.append((S["dout"], 16 * i))
        k.dma("sync", outs5[0][:, :, :],
              sp_w[i].rearrange("p (c t) -> p c t", c=ND),
              waits=w1, incs=[S["lw0" if i % 2 == 0 else "lw1"]])
        k.dma("sync", x1t5[:, :, :],
              sp_x1[i].rearrange("p (c t) -> p c t", c=ND),
              waits=w2, incs=[S["l10" if i % 2 == 0 else "l11"]])

    if P5_LVL == 0:
        k.op("sync", lambda e: None, waits=[GFV] +
             ([cur("ddbg")] if DEBUG else []))
        k.finalize()
        _NC_CACHE[(STAGE, DEBUG)] = (k.nc, None)
        return _NC_CACHE[(STAGE, DEBUG)]

    p5_load(0)
    for i in range(min(NTO, P5_LVL)):
        p = i % 2
        lw = (S["lw0" if p == 0 else "lw1"], 16 * (i // 2 + 1))
        l1 = (S["l10" if p == 0 else "l11"], 16 * (i // 2 + 1))
        for r in range(3):
            src_, dst = outs5[r % 2], outs5[1 - r % 2]
            w = [lw] if r == 0 else [(S["vtw"], k.val(S["vtw"]))]
            w.append((S["tmm"], k.val(S["tmm"])))
            k.op("scalar", lambda e, src_=src_: e.activation(
                o16[:, :, :], src_[:, :, :], AF.Copy),
                waits=w, incs=[S["svm"]])
            sv = cur("svm")
            for m in range(ND):
                bank = mmP[m % 2]
                for kc in range(ND):
                    w = []
                    if kc == 0:
                        w = [sv, (S["sg"], max(0, k.val(S["sg"]) - 1))]
                    k.op("tensor", lambda e, m=m, kc=kc, bank=bank: e.matmul(
                        bank[:, :], WCN1[:, kc, m * 128:(m + 1) * 128],
                        o16[:, kc, :], start=(kc == 0), stop=(kc == ND - 1)),
                        waits=w, incs=[S["tmm"]] if kc == ND - 1 else ())
                k.op("scalar", lambda e, m=m, bank=bank: e.activation(
                    c1_16[:, m, :], bank[:, :], AF.Gelu, bias=cb1c[:, m:m + 1]),
                    waits=[cur("tmm")], incs=[S["sg"]])
            for m in range(4):
                bank = mmP[m % 2]
                for kc in range(ND):
                    w = []
                    if kc == 0:
                        w = [cur("sg") if m == 0 else (S["sg"], k.val(S["sg"])),
                             (S["sg"], max(0, k.val(S["sg"]) - 1))]
                    k.op("tensor", lambda e, m=m, kc=kc, bank=bank: e.matmul(
                        bank[:, :], WCN2[:, kc, m * 128:(m + 1) * 128],
                        c1_16[:, kc, :], start=(kc == 0), stop=(kc == ND - 1)),
                        waits=w, incs=[S["tmm"]] if kc == ND - 1 else ())
                k.op("scalar", lambda e, m=m, bank=bank: e.activation(
                    c2_16[:, m, :], bank[:, :], AF.Gelu, bias=cb2c[:, m:m + 1]),
                    waits=[cur("tmm")], incs=[S["sg"]])
            for m in range(2):
                bank = mmP[m % 2]
                for kc in range(4):
                    w = []
                    if kc == 0:
                        w = [(S["sg"], k.val(S["sg"])),
                             (S["sg"], max(0, k.val(S["sg"]) - 1))]
                    k.op("tensor", lambda e, m=m, kc=kc, bank=bank: e.matmul(
                        bank[:, :], WCN3[:, kc, m * 128:(m + 1) * 128],
                        c2_16[:, kc, :], start=(kc == 0), stop=(kc == 3)),
                        waits=w, incs=[S["tmm"]] if kc == 3 else ())
                k.op("scalar", lambda e, m=m, bank=bank: e.activation(
                    c3_16[:, m, :], bank[:, :], AF.Gelu, bias=cb3c[:, m:m + 1]),
                    waits=[cur("tmm")], incs=[S["sg"]])
            for kc in range(2):
                w = []
                if kc == 0:
                    w = [(S["sg"], k.val(S["sg"])),
                         (S["vrow"], k.val(S["vrow"]))]
                k.op("tensor", lambda e, kc=kc: e.matmul(
                    rowP[0][0:1, :], WCN4[:, kc, 0:1], c3_16[:, kc, :],
                    start=(kc == 0), stop=(kc == 1)),
                    waits=w, incs=[S["tmm"]] if kc == 1 else ())
            k.op("scalar", lambda e: e.activation(
                r1[:, :], rowP[0][0:1, :], AF.Tanh, scale=0.5,
                bias=cst[:, 4:5]),
                waits=[cur("tmm"), GFV], incs=[S["sgf"]])
            k.op("vector", lambda e: e.tensor_scalar(
                r2[:, :], r1[:, :], cst[:, 12:13], cst[:, 10:11],
                ALU.mult, ALU.add),
                waits=[cur("sgf")], incs=[S["vcoup"]])
            k.op("tensor", lambda e: e.matmul(
                repP[2][:, :], ones_r[:, :], r2[:, :], start=True, stop=True),
                waits=[cur("vcoup"), (S["vtw"], k.val(S["vtw"]))],
                incs=[S["trep"]])
            trv = cur("trep")
            for c in range(ND):
                k.op("vector", lambda e, c=c, src_=src_: e.tensor_tensor(
                    twsc[:, :], src_[:, (c + 4) % ND, :], repP[2][:, :],
                    ALU.mult),
                    waits=[trv] if c == 0 else (), incs=[S["vtw"]])
                k.op("vector", lambda e, c=c, src_=src_, dst=dst, op=SIGN[c]:
                     e.tensor_tensor(dst[:, c, :], src_[:, c, :], twsc[:, :],
                                     op),
                     incs=[S["vtw"]])
        res5 = outs5[1]
        vtwf = (S["vtw"], k.val(S["vtw"]))
        vr5 = ln_chain(res5, ND, [vtwf])
        k.op("tensor", lambda e: e.matmul(repP[0][:, :], ones_r[:, :],
                                          rR[:, :], start=True, stop=True),
             waits=[vr5, (S["vfin"], k.val(S["vfin"]))], incs=[S["trep"]])
        k.op("tensor", lambda e: e.matmul(repP[1][:, :], ones_r[:, :],
                                          rMR[:, :], start=True, stop=True),
             incs=[S["trep"]])
        trv = cur("trep")
        for c in range(ND):
            k.op("vector", lambda e, c=c: e.tensor_tensor(
                outs5[0][:, c, :], res5[:, c, :], repP[0][:, :], ALU.mult),
                waits=[trv, l1, (S["dout"], k.val(S["dout"]))]
                if c == 0 else (), incs=[S["vfin"]])
            k.op("vector", lambda e, c=c: e.tensor_tensor(
                outs5[0][:, c, :], outs5[0][:, c, :], repP[1][:, :],
                ALU.subtract))
            k.op("vector", lambda e, c=c: e.tensor_tensor(
                outs5[0][:, c, :], outs5[0][:, c, :], x1t5[:, c, :], ALU.add),
                incs=[S["vfin"]])
        k.dma("sync", out_d[i], outs5[0][:, :, :].rearrange("p c t -> p (c t)"),
              waits=[cur("vfin")], incs=[S["dout"]])
        if i + 1 < min(NTO, P5_LVL):
            p5_load(i + 1)

    k.op("sync", lambda e: None,
         waits=[(S["dout"], 16 * min(NTO, P5_LVL))] +
               ([cur("ddbg")] if DEBUG else []))
    k.finalize()
    _NC_CACHE[(STAGE, DEBUG)] = (k.nc, None)
    return _NC_CACHE[(STAGE, DEBUG)]


def pm(w):
    K, M = w.shape
    nk = K // 128
    return np.ascontiguousarray(
        w.reshape(nk, 128, M).transpose(1, 0, 2).reshape(128, nk * M))


def tok_tiles(a_dt, nt):
    Dm, Lt = a_dt.shape
    t = Lt // nt
    r = a_dt.reshape(ND, 128, nt, t).transpose(2, 1, 0, 3)
    return np.ascontiguousarray(r.reshape(nt, 128, ND * t))


def make_inmaps(inputs):
    x = np.asarray(inputs["L0_attn_x"])
    xt = np.ascontiguousarray(np.transpose(x, (0, 2, 1)))
    ipw = np.asarray(inputs["L0_attn_in_proj_w"])
    maps = []
    for c in range(8):
        b, h = c // 2, c % 2
        ch0 = h * DH
        m = {
            "xT": tok_tiles(xt[b], NTL),
            "xTh": tok_tiles(xt[b][:, h * TOK:(h + 1) * TOK], NTO),
            "w_inj": pm(np.concatenate(
                [ipw[:, ch0:ch0 + DH], ipw[:, DI + ch0:DI + ch0 + DH]], axis=1)),
            "conv_w": np.ascontiguousarray(
                np.asarray(inputs["L0_attn_conv_w"])[ch0:ch0 + DH, 0, :]),
            "conv_b": np.ascontiguousarray(
                np.asarray(inputs["L0_attn_conv_b"])[ch0:ch0 + DH].reshape(DH, 1)),
            "w_xp": pm(np.asarray(inputs["L0_attn_x_proj_w"])[ch0:ch0 + DH, :]),
            "w_dtp": np.ascontiguousarray(
                np.asarray(inputs["L0_attn_dt_proj_w"])[:, ch0:ch0 + DH]),
            "b_dtp": np.ascontiguousarray(
                np.asarray(inputs["L0_attn_dt_proj_b"])[ch0:ch0 + DH].reshape(DH, 1)),
            "A_log": np.ascontiguousarray(
                np.asarray(inputs["L0_attn_A_log"])[ch0:ch0 + DH, :]),
            "D_h": np.ascontiguousarray(
                np.asarray(inputs["L0_attn_D"])[ch0:ch0 + DH].reshape(DH, 1)),
            "w_out": pm(np.asarray(inputs["L0_attn_out_proj_w"])[ch0:ch0 + DH, :]),
            "p3_ident": np.eye(128, dtype=np.float32),
            "ln1_g": np.asarray(inputs["L0_attn_ln1_g"]).reshape(1, D),
            "ln1_b": np.asarray(inputs["L0_attn_ln1_b"]).reshape(1, D),
            "ln1_g_col": np.asarray(inputs["L0_attn_ln1_g"]).reshape(D, 1),
            "ln1_b_col": np.asarray(inputs["L0_attn_ln1_b"]).reshape(D, 1),
            "ln2_g": np.asarray(inputs["L0_attn_ln2_g"]).reshape(1, D),
            "ln2_g_col": np.asarray(inputs["L0_attn_ln2_g"]).reshape(D, 1),
            "ln2_b_col": np.asarray(inputs["L0_attn_ln2_b"]).reshape(D, 1),
            "w_f1": pm(np.asarray(inputs["L0_attn_ffn_w1"])),
            "b_f1": np.asarray(inputs["L0_attn_ffn_b1"]).reshape(DFF, 1),
            "w_f2": pm(np.asarray(inputs["L0_attn_ffn_w2"])),
            "b_f2": np.asarray(inputs["L0_attn_ffn_b2"]).reshape(D, 1),
            "cn_w1": pm(np.asarray(inputs["L0_ffn_cn_w1"])),
            "cn_b1": np.asarray(inputs["L0_ffn_cn_b1"]).reshape(D, 1),
            "cn_w2": pm(np.asarray(inputs["L0_ffn_cn_w2"])),
            "cn_b2": np.asarray(inputs["L0_ffn_cn_b2"]).reshape(512, 1),
            "cn_w3": pm(np.asarray(inputs["L0_ffn_cn_w3"])),
            "cn_b3": np.asarray(inputs["L0_ffn_cn_b3"]).reshape(256, 1),
            "cn_w4": pm(np.asarray(inputs["L0_ffn_cn_w4"])),
            "cn_b4": np.asarray(inputs["L0_ffn_cn_b4"]).reshape(1, 1),
            "gc_w1": pm(np.asarray(inputs["L0_ffn_gc_w1"])),
            "gc_b1": np.asarray(inputs["L0_ffn_gc_b1"]).reshape(512, 1),
            "gc_w2": pm(np.asarray(inputs["L0_ffn_gc_w2"])),
            "gc_b2": np.asarray(inputs["L0_ffn_gc_b2"]).reshape(256, 1),
            "gc_w3": pm(np.asarray(inputs["L0_ffn_gc_w3"])),
            "gc_b3": np.asarray(inputs["L0_ffn_gc_b3"]).reshape(1, 1),
            "ar": np.asarray(inputs["L0_ffn_adaptive_range"]).reshape(1, 1),
        }
        maps.append(m)
    return maps


def kernel(**inputs):
    nc, _ = build_program()
    maps = make_inmaps(inputs)
    res = run_bass_kernel_spmd(nc, maps, core_ids=list(range(8)))
    out = np.empty((4, L, D), np.float32)
    for c in range(8):
        b, h = c // 2, c % 2
        o = np.asarray(res.results[c]["out"])
        o = o.reshape(NTO, 128, ND, T).transpose(0, 3, 2, 1)
        out[b, h * TOK:(h + 1) * TOK] = o.reshape(TOK, D)
    return np.ascontiguousarray(out)


def run_debug(inputs):
    nc, _ = build_program()
    maps = make_inmaps(inputs)
    res = run_bass_kernel_spmd(nc, maps, core_ids=list(range(8)))
    return res


# version 16
# speedup vs baseline: 1.0819x; 1.0819x over previous
import contextlib
import os

import numpy as np

import concourse.bass as bass
import concourse.mybir as mybir
from concourse.bass_utils import run_bass_kernel_spmd

F32 = mybir.dt.float32
BF16 = mybir.dt.bfloat16
AF = mybir.ActivationFunctionType
ALU = mybir.AluOpType
AX = mybir.AxisListType

D = 1024
DI = 2048
DH = 1024
DS = 16
DTR = 64
DFF = 4096
L = 8192
TOK = 4096
T = 512
NTL = L // T
NTO = TOK // T
ND = D // 128
NCH = DH // 128
EPS = 1e-5

AW_COLS = 32800
AF_COLS = 21510
AT_COLS = 12290

STAGE = 5
DEBUG = False
SIM_SAFE = os.environ.get("BASS_SIM_FP64") == "1"
P5_LVL = 8
NTL_RUN = 16

ENGS = ("sync", "gpsimd", "scalar", "vector", "tensor")
GROUPS = [[0, 1], [2, 3], [4, 5], [6, 7]]


class KB:
    def __init__(self):
        self.nc = bass.Bass("TRN2", target_bir_lowering=False, debug=False)
        self.es = contextlib.ExitStack()
        self.q = {e: [] for e in ENGS}
        self.cnt = {}

    def sem(self, name):
        h = self.es.enter_context(self.nc.semaphore(name))
        self.cnt[h] = 0
        return h

    def val(self, s):
        return self.cnt[s]

    def sb(self, name, shape, dtype):
        return self.es.enter_context(self.nc.sbuf_tensor(name, shape, dtype))

    def psum(self, name, shape):
        return self.es.enter_context(self.nc.psum_tensor(name, shape, F32))

    def op(self, eng, fn, waits=(), incs=(), amt=1):
        if SIM_SAFE and amt == 1 and eng != "sync":
            lp = self.last_pub.get(eng)
            if lp is not None:
                waits = tuple(waits) + (lp,)
            if not incs:
                incs = (self.selfsem[eng],)
            for s in incs:
                self.cnt[s] += amt
            self.last_pub[eng] = (incs[-1], self.cnt[incs[-1]])
        else:
            for s in incs:
                self.cnt[s] += amt
        self.q[eng].append((fn, tuple(waits), tuple(incs), amt))

    def dma(self, eng, out, in_, waits=(), incs=(), slow=False):
        self.op(eng, lambda e, o=out, i=in_, s=slow: e.dma_start(
            out=o, in_=i, allow_slow_non_contiguous=s),
            waits, incs, amt=16)

    def init_selfsems(self):
        self.selfsem = {e: self.sem(f"zz_self_{e}") for e in ENGS}
        self.last_pub = {}

    def finalize(self):
        block = self.es.enter_context(self.nc.Block())
        for name in ENGS:
            items = self.q[name]

            def make_emit(items=items):
                def emit(engine):
                    for fn, waits, incs, amt in items:
                        for (s, v) in waits:
                            engine.wait_ge(s, v)
                        instr = fn(engine)
                        if incs:
                            if instr is None:
                                instr = engine.nop()
                            for s in incs:
                                instr.then_inc(s, amt)
                return emit

            getattr(block, name)(make_emit())


def view3(tensor, off, c, t, p=128):
    return tensor[0:p, off:off + c * t].rearrange("p (c t) -> p c t", c=c)


_NC_CACHE = {}


def build_program():
    if (STAGE, DEBUG) in _NC_CACHE:
        return _NC_CACHE[(STAGE, DEBUG)]
    _ = P5_LVL
    k = KB()
    nc = k.nc

    def din(name, shape, dtype=F32):
        return nc.dram_tensor(name, shape, dtype, kind="ExternalInput").ap()

    def dout(name, shape, dtype=F32):
        return nc.dram_tensor(name, shape, dtype, kind="ExternalOutput").ap()

    def dint(name, shape, dtype=F32):
        return nc.dram_tensor(name, shape, dtype).ap()

    xT = din("xT", [NTL, 128, ND * T])
    xTh = din("xTh", [NTO, 128, ND * T])
    w_inj = din("w_inj", [128, ND * 2 * DH])
    conv_w = din("conv_w", [DH, 4])
    conv_b = din("conv_b", [DH, 1])
    w_xp = din("w_xp", [128, NCH * 96])
    w_dtp = din("w_dtp", [DTR, DH])
    b_dtp = din("b_dtp", [DH, 1])
    A_log = din("A_log", [DH, DS])
    D_in = din("D_h", [DH, 1])
    w_out = din("w_out", [128, NCH * D])
    ident_in = din("p3_ident", [128, 128])
    g1r_in = din("ln1_g", [1, D])
    b1r_in = din("ln1_b", [1, D])
    b1c_in = din("ln1_b_col", [D, 1])
    g1c_in = din("ln1_g_col", [D, 1])
    g2r_in = din("ln2_g", [1, D])
    b2c_in = din("ln2_b_col", [D, 1])
    g2c_in = din("ln2_g_col", [D, 1])
    w_f1 = din("w_f1", [128, ND * DFF])
    b_f1 = din("b_f1", [DFF, 1])
    w_f2 = din("w_f2", [128, (DFF // 128) * D])
    b_f2 = din("b_f2", [D, 1])
    cn_w1 = din("cn_w1", [128, ND * D])
    cn_b1 = din("cn_b1", [D, 1])
    cn_w2 = din("cn_w2", [128, ND * 512])
    cn_b2 = din("cn_b2", [512, 1])
    cn_w3 = din("cn_w3", [128, 4 * 256])
    cn_b3 = din("cn_b3", [256, 1])
    cn_w4 = din("cn_w4", [128, 2 * 1])
    cn_b4 = din("cn_b4", [1, 1])
    gc_w1 = din("gc_w1", [128, ND * 512])
    gc_b1 = din("gc_b1", [512, 1])
    gc_w2 = din("gc_w2", [128, 4 * 256])
    gc_b2 = din("gc_b2", [256, 1])
    gc_w3 = din("gc_w3", [128, 2 * 1])
    gc_b3 = din("gc_b3", [1, 1])
    ar_in = din("ar", [1, 1])

    out_d = dout("out", [NTO, 128, ND * T])

    cc_xdbl_in = dint("cc_xdbl_in", [2, 96, L // 2])
    cc_xdbl = dint("cc_xdbl", [2, 96, L // 2])
    cc_o1_in = dint("cc_o1_in", [2, NTO, 128, ND * T])
    cc_o1 = dint("cc_o1", [NTO, 128, ND * T])
    cc_mean_in = dint("cc_mean_in", [128, ND])
    cc_mean = dint("cc_mean", [128, ND])
    sp_xc = dint("sp_xc", [NTL, 128, NCH * T])
    sp_sz = dint("sp_sz", [NTL, 128, NCH * T], BF16)
    sp_g = dint("sp_g", [NTO, 128, 32 * T], BF16)
    sp_v = dint("sp_v", [NTO, 128, ND * T])
    sp_x1 = dint("sp_x1", [NTO, 128, ND * T])
    sp_w = dint("sp_w", [NTO, 128, ND * T])

    if DEBUG:
        dbg_xdbl = dout("dbg_xdbl", [2, 96, L // 2])
        dbg_xc = dout("dbg_xc", [NTL, 128, NCH * T])
        dbg_sz = dout("dbg_sz", [NTL, 128, NCH * T], BF16)
        if STAGE >= 3:
            dbg_o1 = dout("dbg_o1", [NTO, 128, ND * T])
            dbg_dt = dout("dbg_dt", [NTL, 128, NCH * T])
            dbg_y = dout("dbg_y", [NTL, 128, NCH * T], BF16)
        if STAGE >= 4:
            dbg_v = dout("dbg_v", [NTO, 128, ND * T])
            dbg_x1 = dout("dbg_x1", [NTO, 128, ND * T])
            dbg_w = dout("dbg_w", [NTO, 128, ND * T])
            dbg_mean = dout("dbg_mean", [128, ND])

    AW = k.sb("AW", [128, AW_COLS], BF16)
    AFa = k.sb("AFa", [128, AF_COLS], F32)
    AT = k.sb("AT", [128, AT_COLS], BF16)

    ones_c16 = k.sb("ones_c16", [128, 1], BF16)
    ones_r = k.sb("ones_r", [1, 128], F32)
    ones_r16 = k.sb("ones_r16", [1, 128], BF16)
    g1row = k.sb("g1row", [1, D], F32)
    g2row = g1row
    g1col16 = k.sb("g1col16", [128, ND], BF16)
    b1col16 = k.sb("b1col16", [128, ND], BF16)
    g2col16 = k.sb("g2col16", [128, ND], BF16)
    b2col16 = k.sb("b2col16", [128, ND], BF16)
    nwg2 = k.sb("nwg2", [1, DFF], BF16)
    nwg1 = nwg2[0:1, 0:2 * DH]
    rMR16 = k.sb("rMR16", [1, T], BF16)
    wb1c = k.sb("wb1c", [128, 16], F32)
    wb2c = k.sb("wb2c", [128, 32], F32)
    bdtc = k.sb("bdtc", [128, NCH], F32)
    cbc = k.sb("cbc", [128, NCH], F32)
    cwc = k.sb("cwc", [128, NCH, 4], F32)
    Acol = k.sb("Acol", [128, NCH, DS], F32)
    Dcol = k.sb("Dcol", [128, NCH], F32)
    b2c = k.sb("b2c", [128, ND], F32)
    cb1c = k.sb("cb1c", [128, ND], F32)
    bf1c = k.sb("bf1c", [128, 32], F32)
    mred = k.sb("mred", [128, ND], F32)
    cb2c = k.sb("cb2c", [128, 4], F32)
    cb3c = k.sb("cb3c", [128, 2], F32)
    gb1c = k.sb("gb1c", [128, 4], F32)
    gb2c = k.sb("gb2c", [128, 2], F32)
    cst = k.sb("cst", [1, 16], F32)
    mean_acc = k.sb("mean_acc", [128, ND], F32)
    ctx16 = k.sb("ctx16", [128, ND], BF16)
    gc_h1 = k.sb("gc_h1", [128, 4], BF16)
    gc_h2 = k.sb("gc_h2", [128, 2], BF16)
    state = k.sb("state", [128, NCH, DS], F32)

    i32 = k.sb("i32", [128, 128], F32)
    i16 = k.sb("i16", [128, 128], BF16)
    r0 = k.sb("r0", [1, T], F32)
    r1 = k.sb("r1", [1, T], F32)
    r2 = k.sb("r2", [1, T], F32)
    r3 = k.sb("r3", [1, T], F32)
    rR = k.sb("rR", [1, T], F32)
    rMR = k.sb("rMR", [1, T], F32)
    brow = [r0, r1]
    crow = [r2, r3]
    st16 = [k.sb(f"st16_{i}", [128, T], BF16) for i in range(2)]
    sqt16 = [k.sb(f"sqt16_{i}", [128, T], BF16) for i in range(2)]

    mmP = [k.psum(f"mmP{i}", [128, T]) for i in range(2)]
    repP = [k.psum(f"repP{i}", [128, T]) for i in range(4)]
    rowP = [k.psum(f"rowP{i}", [128, T]) for i in range(2)]

    SEMS = ("ldw", "prm", "misc", "lx0", "lx1", "sstat", "tstat", "vrow",
            "srow", "trep", "vrep", "tmm", "sevac", "vconv", "vtail", "sxc",
            "dspx", "dspz", "sxd", "dxdbl", "cc1",
            "ldt0", "ldt1", "lxc0", "lxc1", "lsz0", "lsz1",
            "lbr", "lcr", "tdtp", "sdt", "vdtx", "sda", "vscan", "vsdone",
            "sbr", "tbr", "tcr", "sbe", "vhc", "ghc", "scr", "gacc", "tacc",
            "ldwa", "ldwb", "ldwc", "ldwd", "do1b", "cc1b", "prmg",
            "lbr1", "lcr1",
            "vy3", "top", "so1", "do1", "cc2",
            "lo0", "lo1", "lh0", "lh1", "vv", "svm", "tvm", "vvm", "sg",
            "dspg", "dspv", "lg0", "lg1", "lv0", "lv1", "vu2", "vx1",
            "dspx1", "vw", "dspw", "vmean", "dmean", "cc3",
            "lw0", "lw1", "l10", "l11", "sgf", "vcst", "vcoup", "vtw",
            "vfin", "dout", "ddbg")
    S = {n: k.sem(n) for n in SEMS}
    k.init_selfsems()

    def W(*pairs):
        return tuple(pairs)

    def cur(name):
        return (S[name], k.val(S[name]))

    WJ_O, WXP_O = 0, 16384
    WJ = view3(AW, WJ_O, ND, 2 * DH)
    WXP = view3(AW, WXP_O, NCH, 96)
    k.dma("gpsimd", WJ, w_inj.rearrange("p (c m) -> p c m", c=ND), incs=[S["ldw"]])
    k.dma("gpsimd", WXP, w_xp.rearrange("p (c m) -> p c m", c=NCH), incs=[S["ldw"]])
    LDW_P1 = k.val(S["ldw"])
    WDTP_O, WOUT_O = 17152, 18176
    WDTP = view3(AW, WDTP_O, NCH, 128, p=64)
    WOUT = view3(AW, WOUT_O, NCH, D)
    k.dma("gpsimd", WDTP, w_dtp.rearrange("k (c m) -> k c m", c=NCH),
          incs=[S["ldwa"]])
    k.dma("gpsimd", WOUT, w_out.rearrange("p (c m) -> p c m", c=NCH),
          incs=[S["ldwa"]])
    LDW_P3 = ("ldwa", k.val(S["ldwa"]))

    for dst, src in (
        (g1row[:, :], g1r_in), (i32[:, :], ident_in),
        (bdtc[:, :], b_dtp.rearrange("(c p) o -> p (c o)", p=128)),
        (cbc[:, :], conv_b.rearrange("(c p) o -> p (c o)", p=128)),
        (cwc[:, :, :], conv_w.rearrange("(c p) m -> p c m", p=128)),
        (Acol[:, :, :], A_log.rearrange("(c p) m -> p c m", p=128)),
        (Dcol[:, :], D_in.rearrange("(c p) o -> p (c o)", p=128)),
        (b2c[:, :], b_f2.rearrange("(c p) o -> p (c o)", p=128)),
        (cb1c[:, :], cn_b1.rearrange("(c p) o -> p (c o)", p=128)),
        (bf1c[:, :], b_f1.rearrange("(c p) o -> p (c o)", p=128)),
        (cb2c[:, 0:4], cn_b2.rearrange("(c p) o -> p (c o)", p=128)),
        (cb3c[:, 0:2], cn_b3.rearrange("(c p) o -> p (c o)", p=128)),
        (gb1c[:, 0:4], gc_b1.rearrange("(c p) o -> p (c o)", p=128)),
        (gb2c[:, 0:2], gc_b2.rearrange("(c p) o -> p (c o)", p=128)),
        (cst[:, 0:1], ar_in), (cst[:, 1:2], gc_b3), (cst[:, 2:3], cn_b4),
    ):
        k.dma("sync", dst, src, incs=[S["prm"]], slow=True)
    for dst, src in (
        (g1col16[:, :], g1c_in.rearrange("(c p) o -> p (c o)", p=128)),
        (b1col16[:, :], b1c_in.rearrange("(c p) o -> p (c o)", p=128)),
        (g2col16[:, :], g2c_in.rearrange("(c p) o -> p (c o)", p=128)),
        (b2col16[:, :], b2c_in.rearrange("(c p) o -> p (c o)", p=128)),
    ):
        k.dma("gpsimd", dst, src, incs=[S["prmg"]], slow=True)
    PRM = cur("prm")
    PRMG = cur("prmg")

    def early_stop():
        k.op("gpsimd", lambda e: None, waits=[PRM, (S["ldw"], LDW_P1)])
        k.finalize()
        _NC_CACHE[(STAGE, DEBUG)] = (k.nc, None)
        return _NC_CACHE[(STAGE, DEBUG)]

    if NTL_RUN == -1:
        return early_stop()

    k.op("scalar", lambda e: e.activation(Acol[:, :, :], Acol[:, :, :], AF.Exp),
         waits=[PRM], incs=[S["misc"]])
    k.op("vector", lambda e: e.tensor_scalar_mul(Acol[:, :, :], Acol[:, :, :], -1.0),
         waits=[cur("misc")], incs=[S["misc"]])
    k.op("vector", lambda e: e.memset(ones_c16[:, :], 1.0), incs=[S["misc"]])
    k.op("vector", lambda e: e.memset(ones_r[:, :], 1.0), incs=[S["misc"]])
    k.op("vector", lambda e: e.memset(ones_r16[:, :], 1.0), incs=[S["misc"]])
    k.op("scalar", lambda e: e.activation(i16[:, :], i32[:, :], AF.Copy),
         waits=[PRM], incs=[S["misc"]])

    if NTL_RUN == -2:
        return early_stop()

    WJLD = (S["ldw"], LDW_P1)
    for mt in range((2 * DH) // T):
        for kc in range(ND):
            w = []
            if kc == 0:
                w = [WJLD, PRM, PRMG, (S["vrow"], k.val(S["vrow"]))]
            k.op("tensor", lambda e, mt=mt, kc=kc: e.matmul(
                rowP[0][0:1, :], g1col16[:, kc:kc + 1], WJ[:, kc, mt * T:(mt + 1) * T],
                start=(kc == 0), stop=(kc == ND - 1)),
                waits=w,
                incs=[S["tstat"]] if kc == ND - 1 else ())
        k.op("vector", lambda e, mt=mt: e.tensor_scalar_mul(
            nwg1[0:1, mt * T:(mt + 1) * T], rowP[0][0:1, :], -1.0),
            waits=[cur("tstat")], incs=[S["vrow"]])
    for m in range(16):
        for kc in range(ND):
            k.op("tensor", lambda e, m=m, kc=kc: e.matmul(
                repP[0][:, m:m + 1], WJ[:, kc, m * 128:(m + 1) * 128],
                b1col16[:, kc:kc + 1],
                start=(kc == 0), stop=(kc == ND - 1)),
                incs=[S["tstat"]] if (kc == ND - 1 and m == 15) else ())
    k.op("scalar", lambda e: e.activation(wb1c[:, :], repP[0][:, 0:16], AF.Copy),
         waits=[cur("tstat")], incs=[S["sevac"]])
    P0_EVAC = cur("sevac")
    P0_VROW = cur("vrow")

    if NTL_RUN == -3:
        return early_stop()

    XB_O, XI_O, XCP_O, XD_O = 0, 8192, 12312, 16408
    XC_O = XCP_O
    xbuf = [view3(AFa, XB_O + i * 4096, ND, T) for i in range(2)]
    xibuf = view3(AFa, XI_O, NCH, T + 3)
    xcpre = view3(AFa, XCP_O, NCH, T)
    xct = view3(AFa, XC_O, NCH, T)
    xdblt = AFa[0:96, XD_O:XD_O + T]
    XM_O, SZ_O, XC16_O = 0, 4096, 8192
    xmul16 = view3(AT, XM_O, ND, T)
    szt16 = view3(AT, SZ_O, NCH, T)
    xc16 = view3(AT, XC16_O, NCH, T)

    def p1_load(j):
        p = j % 2
        w = [cur("vrep")] if j >= 2 else ()
        k.dma("sync", xbuf[p][:, :, :],
              xT[j].rearrange("p (c t) -> p c t", c=ND),
              waits=w, incs=[S["lx0" if p == 0 else "lx1"]])

    k.op("vector", lambda e: e.memset(xibuf[:, :, 0:3], 0.0), incs=[S["vtail"]])
    if NTL_RUN > 0:
        p1_load(0)
    if NTL_RUN > 1:
        p1_load(1)

    for j in range(NTL_RUN):
        p = j % 2
        lxv = (S["lx0" if p == 0 else "lx1"], 16 * (j // 2 + 1))
        for c in range(ND):
            cp = c % 2
            w = [lxv, PRM, (S["tstat"], max(0, k.val(S["tstat"]) - 2))]
            k.op("scalar", lambda e, c=c, cp=cp, p=p: e.activation(
                st16[cp][:, :], xbuf[p][:, c, :], AF.Copy),
                waits=w, incs=[S["sstat"]])
            k.op("scalar", lambda e, c=c, cp=cp, p=p: e.activation(
                sqt16[cp][:, :], xbuf[p][:, c, :], AF.Square),
                incs=[S["sstat"]])
            w2 = [cur("sstat")]
            if c == 0:
                w2.append(cur("vrow"))
            k.op("tensor", lambda e, cp=cp, c=c: e.matmul(
                rowP[0][0:1, :], ones_c16[:, :], st16[cp][:, :],
                start=(c == 0), stop=(c == ND - 1)),
                waits=w2, incs=[S["tstat"]])
            k.op("tensor", lambda e, cp=cp, c=c: e.matmul(
                rowP[1][0:1, :], ones_c16[:, :], sqt16[cp][:, :],
                start=(c == 0), stop=(c == ND - 1)),
                incs=[S["tstat"]])
        tsv = cur("tstat")

        def vop(fn, waits=()):
            k.op("vector", fn, waits=waits, incs=[S["vrow"]])

        vop(lambda e: e.tensor_scalar_mul(r0[:, :], rowP[0][0:1, :], 1.0 / D),
            waits=[tsv])
        vop(lambda e: e.tensor_scalar_mul(r1[:, :], rowP[1][0:1, :], 1.0 / D))
        vop(lambda e: e.tensor_tensor(r2[:, :], r0[:, :], r0[:, :], ALU.mult))
        vop(lambda e: e.tensor_tensor(r1[:, :], r1[:, :], r2[:, :], ALU.subtract))
        vop(lambda e: e.tensor_scalar_add(r2[:, :], r1[:, :], EPS))
        k.op("scalar", lambda e: e.activation(r3[:, :], r2[:, :], AF.Ln),
             waits=[cur("vrow")], incs=[S["srow"]])
        k.op("scalar", lambda e: e.activation(r3[:, :], r3[:, :], AF.Exp,
                                              scale=-0.5), incs=[S["srow"]])
        vop(lambda e: e.reciprocal(r2[:, :], r2[:, :]), waits=[cur("srow")])
        vop(lambda e: e.tensor_tensor(r1[:, :], r1[:, :], r2[:, :], ALU.mult))
        vop(lambda e: e.tensor_scalar_add(r1[:, :], r1[:, :], EPS))
        k.op("scalar", lambda e: e.activation(r1[:, :], r1[:, :], AF.Ln),
             waits=[cur("vrow")], incs=[S["srow"]])
        k.op("scalar", lambda e: e.activation(r1[:, :], r1[:, :], AF.Exp,
                                              scale=-0.5), incs=[S["srow"]])
        vop(lambda e: e.tensor_tensor(rR[:, :], r3[:, :], r1[:, :], ALU.mult),
            waits=[cur("srow")])
        vop(lambda e: e.tensor_tensor(rMR[:, :], r0[:, :], rR[:, :], ALU.mult))
        vop(lambda e: e.tensor_copy(rMR16[:, :], rMR[:, :]))
        vrows = cur("vrow")

        for c in range(ND):
            bank = repP[c % 2]
            w = [vrows, (S["vrep"], max(0, k.val(S["vrep"]) - 1))]
            k.op("tensor", lambda e, c=c, bank=bank: e.matmul(
                bank[:, :], g1row[0:1, c * 128:(c + 1) * 128], rR[:, :],
                start=True, stop=True),
                waits=w, incs=[S["trep"]])
            k.op("vector", lambda e, c=c, bank=bank, p=p: e.tensor_tensor(
                xmul16[:, c, :], xbuf[p][:, c, :], bank[:, :], ALU.mult),
                waits=[cur("trep")], incs=[S["vrep"]])
        vxm = cur("vrep")

        for m in range(16):
            bank = mmP[m % 2]
            for kc in range(ND):
                w = []
                if kc == 0:
                    w = [vxm, (S["sevac"], max(0, k.val(S["sevac"]) - 1))]
                k.op("tensor", lambda e, m=m, kc=kc, bank=bank: e.matmul(
                    bank[:, :], WJ[:, kc, m * 128:(m + 1) * 128],
                    xmul16[:, kc, :], start=(kc == 0), stop=False),
                    waits=w)
            k.op("tensor", lambda e, m=m, bank=bank: e.matmul(
                bank[:, :], nwg1[0:1, m * 128:(m + 1) * 128], rMR16[:, :],
                start=False, stop=True), incs=[S["tmm"]])
            tev = cur("tmm")
            if m < 8:
                k.op("scalar", lambda e, m=m, bank=bank: e.add(
                    xibuf[:, m, 3:3 + T], bank[:, :], wb1c[:, m:m + 1]),
                    waits=[tev, cur("vtail"), P0_EVAC], incs=[S["sevac"]])
            else:
                k.op("scalar", lambda e, m=m, bank=bank: e.activation(
                    szt16[:, m - 8, :], bank[:, :], AF.Silu,
                    bias=wb1c[:, m:m + 1]),
                    waits=[tev] + ([cur("dspz")] if m == 8 else []),
                    incs=[S["sevac"]])
        sev = k.val(S["sevac"])

        for c in range(NCH):
            k.op("vector", lambda e, c=c: e.tensor_scalar(
                xcpre[:, c, :], xibuf[:, c, 0:T], cwc[:, c, 0:1], None, ALU.mult),
                waits=[(S["sevac"], sev - 16 + c + 1), cur("sxc"), cur("dspx")],
                incs=[S["vconv"]])
            for jj in range(1, 4):
                k.op("vector", lambda e, c=c, jj=jj: e.scalar_tensor_tensor(
                    xcpre[:, c, :], xibuf[:, c, jj:jj + T], cwc[:, c, jj:jj + 1],
                    xcpre[:, c, :], ALU.mult, ALU.add),
                    incs=[S["vconv"]])
        k.op("vector", lambda e: e.tensor_copy(xibuf[:, :, 0:3],
                                               xibuf[:, :, T:T + 3]),
             waits=[cur("vconv")], incs=[S["vtail"]])
        vcv = cur("vconv")

        for c in range(NCH):
            w = [vcv] if c == 0 else []
            k.op("scalar", lambda e, c=c: e.activation(
                xc16[:, c, :], xcpre[:, c, :], AF.Silu, bias=cbc[:, c:c + 1]),
                waits=w, incs=[S["sxc"]])
            k.op("scalar", lambda e, c=c: e.activation(
                xct[:, c, :], xcpre[:, c, :], AF.Silu, bias=cbc[:, c:c + 1]),
                incs=[S["sxc"]])
        sxcv = cur("sxc")

        k.dma("sync", sp_xc[j].rearrange("p (c t) -> p c t", c=NCH),
              xct[:, :, :], waits=[sxcv], incs=[S["dspx"]])
        k.dma("sync", sp_sz[j].rearrange("p (c t) -> p c t", c=NCH),
              szt16[:, :, :], waits=[(S["sevac"], sev)], incs=[S["dspz"]])

        for kc in range(NCH):
            w = []
            if kc == 0:
                w = [sxcv, cur("sxd"),
                     (S["sevac"], max(0, k.val(S["sevac"]) - 1))]
            k.op("tensor", lambda e, kc=kc: e.matmul(
                mmP[0][0:96, :], WXP[:, kc, :], xc16[:, kc, :],
                start=(kc == 0), stop=(kc == NCH - 1)),
                waits=w, incs=[S["tmm"]] if kc == NCH - 1 else ())
        k.op("scalar", lambda e: e.activation(xdblt[:, :], mmP[0][0:96, :],
                                              AF.Copy),
             waits=[cur("tmm"), cur("dxdbl")], incs=[S["sxd"]])
        k.dma("sync", cc_xdbl_in[j // 8][:, (j % 8) * T:(j % 8 + 1) * T],
              xdblt[:, :], waits=[cur("sxd")], incs=[S["dxdbl"]])
        if j == 7:
            k.op("gpsimd", lambda e: e.collective_compute(
                "AllReduce", ALU.add, replica_groups=GROUPS,
                ins=[cc_xdbl_in[0]], outs=[cc_xdbl[0]]),
                waits=[(S["dxdbl"], 16 * 8)], incs=[S["cc1"]])

        if j + 2 < NTL_RUN:
            p1_load(j + 2)

    k.op("gpsimd", lambda e: e.collective_compute(
        "AllReduce", ALU.add, replica_groups=GROUPS,
        ins=[cc_xdbl_in[1]], outs=[cc_xdbl[1]]),
        waits=[cur("dxdbl")], incs=[S["cc1b"]])

    if DEBUG:
        k.dma("gpsimd", dbg_xdbl, cc_xdbl, waits=[cur("cc1")], incs=[S["ddbg"]])
        k.dma("gpsimd", dbg_xc, sp_xc, waits=[cur("dspx")], incs=[S["ddbg"]])
        k.dma("gpsimd", dbg_sz, sp_sz, waits=[cur("dspz")], incs=[S["ddbg"]])

    if STAGE == 1:
        k.op("gpsimd", lambda e: None,
             waits=[cur("cc1"), cur("cc1b")] + ([cur("ddbg")] if DEBUG else []))
        k.finalize()
        _NC_CACHE[(STAGE, DEBUG)] = (k.nc, None)
        return _NC_CACHE[(STAGE, DEBUG)]

    xct3 = view3(AFa, 0, NCH, T)
    dtt = view3(AFa, 4096, NCH, T)
    yacc = view3(AFa, 8192, NCH, T)
    dtxt = view3(AFa, 12288, NCH, T)
    h_r = [AFa[0:128, 16384 + i * 512:16384 + (i + 1) * 512] for i in range(2)]
    u_t = AFa[0:128, 17408:17920]
    dtex = AFa[0:128, 17920:18432]
    hc16 = [AT[0:128, 9216 + i * T:9216 + (i + 1) * T] for i in range(4)]
    brow16 = [AFa[0:1, 18432 + i * 256:18432 + (i + 1) * 256].bitcast(BF16)
              for i in range(2)]
    crow16 = [AFa[0:1, 18944 + i * 256:18944 + (i + 1) * 256].bitcast(BF16)
              for i in range(2)]
    dAr = [AFa[0:128, 19456 + i * 512:19456 + (i + 1) * 512] for i in range(4)]
    yscr = AFa[0:128, 8192:8704]
    crep_sb = [AFa[0:128, 20480 + i * 512:20480 + (i + 1) * 512]
               for i in range(2)]
    o1ev = [AW[0:128, 26368 + i * 1024:26368 + (i + 1) * 1024].bitcast(F32)
            for i in range(2)]
    y3_16 = view3(AT, 0, NCH, T)
    szt3 = view3(AT, 4096, NCH, T)
    dtraw16 = [AT[0:64, 8192 + i * T:8192 + (i + 1) * T] for i in range(2)]
    brP = [repP[0], repP[1]]
    crP = [repP[2], repP[3]]

    k.op("vector", lambda e: e.memset(state[:, :, :], 0.0), incs=[S["misc"]])
    CC1A = (S["cc1"], 1)
    CC1B = (S["cc1b"], 1)
    P1DONE = [cur("vrep"), cur("tmm"), cur("sxc"), cur("vrow"), cur("srow"),
              cur("sevac"), cur("dspx"), cur("dspz"), cur("dxdbl")]

    def p3_load(j):
        p = j % 2
        cc = CC1A if j < 8 else CC1B
        dw = [cc, (S["tdtp"], max(0, k.val(S["tdtp"]) - 8))]
        if j == 0:
            dw += P1DONE
        k.dma("gpsimd", dtraw16[p],
              cc_xdbl[j // 8][0:64, (j % 8) * T:(j % 8 + 1) * T],
              waits=dw,
              incs=[S["ldt0" if p == 0 else "ldt1"]])
        xw = [(S["dspx"], 16 * (j + 1)), (S["vy3"], k.val(S["vy3"]))]
        zw = [(S["dspz"], 16 * (j + 1)), (S["vy3"], k.val(S["vy3"]))]
        if j == 0:
            xw += P1DONE
            zw += P1DONE
        k.dma("sync", xct3[:, :, :],
              sp_xc[j].rearrange("p (c t) -> p c t", c=NCH),
              waits=xw, incs=[S["lxc0" if p == 0 else "lxc1"]])
        k.dma("sync", szt3[:, :, :],
              sp_sz[j].rearrange("p (c t) -> p c t", c=NCH),
              waits=zw, incs=[S["lsz0" if p == 0 else "lsz1"]])

    def p3_row_load(j, s, sp):
        cc = CC1A if j < 8 else CC1B
        extra = P1DONE if j == 0 else []
        k.dma("gpsimd", brow16[sp][:, :],
              cc_xdbl[j // 8][64 + s:65 + s, (j % 8) * T:(j % 8 + 1) * T],
              waits=[cc, (S["tbr"], k.val(S["tbr"]))] + extra,
              incs=[S["gacc" if sp == 0 else "ghc"]])
        k.dma("gpsimd", crow16[sp][:, :],
              cc_xdbl[j // 8][80 + s:81 + s, (j % 8) * T:(j % 8 + 1) * T],
              waits=[cc, (S["tcr"], k.val(S["tcr"]))] + extra,
              incs=[S["scr" if sp == 0 else "sbe"]])

    p3_load(0)
    for s in range(2):
        p3_row_load(0, s, s % 2)
    nrow_loaded = [2]
    it3 = [0]
    pend_copy = [None]
    next_rep = [0]

    for j in range(NTL):
        p = j % 2
        ldt = (S["ldt0" if p == 0 else "ldt1"], 16 * (j // 2 + 1))
        lxc = (S["lxc0" if p == 0 else "lxc1"], 16 * (j // 2 + 1))
        lsz = (S["lsz0" if p == 0 else "lsz1"], 16 * (j // 2 + 1))
        for m in range(NCH):
            bank = mmP[m % 2]
            k.op("tensor", lambda e, m=m, bank=bank, p=p: e.matmul(
                bank[:, :], WDTP[:, m, :], dtraw16[p],
                start=True, stop=True),
                waits=[ldt, (S[LDW_P3[0]], LDW_P3[1]),
                       (S["sdt"], max(0, k.val(S["sdt"]) - 2)),
                       (S["so1"], 8 * j)],
                incs=[S["tdtp"]])
            k.op("scalar", lambda e, m=m, bank=bank: e.activation(
                dtex[:, :], bank[:, :], AF.Exp, bias=bdtc[:, m:m + 1]),
                waits=[cur("tdtp")], incs=[S["sdt"]])
            k.op("scalar", lambda e, m=m: e.activation(
                dtt[:, m, :], dtex[:, :], AF.Ln, bias=1.0),
                incs=[S["sdt"]])
        sdtv = cur("sdt")
        if DEBUG:
            k.dma("gpsimd", dbg_dt[j].rearrange("p (c t) -> p c t", c=NCH),
                  dtt[:, :, :], waits=[sdtv], incs=[S["ddbg"]])
        for m in range(NCH):
            w = []
            if m == 0:
                w = [sdtv, lxc]
            k.op("vector", lambda e, m=m: e.tensor_tensor(
                dtxt[:, m, :], dtt[:, m, :], xct3[:, m, :], ALU.mult),
                waits=w, incs=[S["vdtx"]])
        vdtxv = cur("vdtx")
        yPb = [rowP[0], rowP[1], mmP[0], mmP[1]]

        def emit_reps(g):
            gsp = g % 2
            bw = [(S["gacc" if gsp == 0 else "ghc"], 16 * (g // 2 + 1))]
            if g >= 2:
                bw.append((S["vscan"], 4 * (g - 1)))
            if g == 0:
                bw.append((S["misc"], k.val(S["misc"])))
            k.op("tensor", lambda e, gsp=gsp: e.matmul(
                brP[gsp][:, :], ones_r16[:, :], brow16[gsp][:, :],
                start=True, stop=True),
                waits=bw, incs=[S["tbr"]])
            cw = [(S["scr" if gsp == 0 else "sbe"], 16 * (g // 2 + 1))]
            if g >= 2:
                cw.append((S["vhc"], 4 * (g - 1)))
            k.op("tensor", lambda e, gsp=gsp: e.matmul(
                crP[gsp][:, :], ones_r16[:, :], crow16[gsp][:, :],
                start=True, stop=True),
                waits=cw, incs=[S["tcr"]])

        for H in range(2):
            for s in range(DS):
                gi = 32 * j + 16 * H + s
                sp = gi % 2
                while next_rep[0] <= min(32 * NTL - 1, gi + 1):
                    emit_reps(next_rep[0])
                    next_rep[0] += 1
                while nrow_loaded[0] < min(32 * NTL, gi + 4):
                    n = nrow_loaded[0]
                    p3_row_load(n // 32, n % 16, n % 2)
                    nrow_loaded[0] += 1
                for kc in range(4 * H, 4 * H + 4):
                    i = it3[0]
                    k.op("scalar", lambda e, kc=kc, s=s, i=i: e.activation(
                        dAr[i % 4][:, :], dtt[:, kc, :], AF.Exp,
                        scale=Acol[:, kc, s:s + 1]),
                        waits=[(S["vscan"], max(0, k.val(S["vscan"]) - 3))],
                        incs=[S["sda"]])
                    k.op("vector", lambda e, kc=kc, sp=sp: e.tensor_tensor(
                        u_t[:, :], dtxt[:, kc, :], brP[sp][:, :], ALU.mult),
                        waits=([vdtxv, (S["tbr"], gi + 1)]
                               if kc == 4 * H else ()))
                    k.op("vector",
                         lambda e, kc=kc, s=s, i=i: e.tensor_tensor_scan(
                             h_r[i % 2][:, :], dAr[i % 4][:, :], u_t[:, :],
                             state[:, kc, s:s + 1], ALU.mult, ALU.add),
                         waits=[(S["sda"], k.val(S["sda"])),
                                (S["lbr"], k.val(S["lbr"]))],
                         incs=[S["vscan"]])
                    hw = [(S["tacc"], max(0, k.val(S["tacc"]) - 3))]
                    if kc == 4 * H:
                        hw.append((S["tcr"], gi + 1))
                    k.op("vector", lambda e, i=i, sp=sp: e.tensor_tensor(
                        hc16[i % 4][:, :], h_r[i % 2][:, :], crP[sp][:, :],
                        ALU.mult),
                        waits=hw, incs=[S["vhc"]])
                    if pend_copy[0] is not None:
                        k.op("scalar", pend_copy[0],
                             waits=[(S["vscan"],
                                     max(0, k.val(S["vscan"]) - 1))],
                             incs=[S["lbr"]])
                    pend_copy[0] = (
                        lambda e, kc=kc, i=i, s=s: e.activation(
                            state[:, kc, s:s + 1], h_r[i % 2][:, T - 1:T],
                            AF.Copy))
                    tw = [(S["vhc"], k.val(S["vhc"]))]
                    if s == 0 and kc == 4 * H:
                        if H == 0:
                            tw.append((S["vy3"], k.val(S["vy3"])))
                            if j == 0:
                                tw += P1DONE
                        else:
                            tw.append((S["sdt"], k.val(S["sdt"])))
                    k.op("tensor", lambda e, kc=kc, s=s, i=i: e.matmul(
                        yPb[kc % 4][:, :], i16[:, :], hc16[i % 4][:, :],
                        start=(s == 0), stop=(s == DS - 1)),
                        waits=tw, incs=[S["tacc"]])
                    it3[0] += 1
            for kc in range(4 * H, 4 * H + 4):
                sw = []
                if kc == 4 * H:
                    sw = [(S["tacc"], k.val(S["tacc"]))]
                    if H == 0:
                        sw += [lsz, (S["top"], k.val(S["top"]))]
                k.op("vector", lambda e, kc=kc: e.scalar_tensor_tensor(
                    yscr[:, :], xct3[:, kc, :], Dcol[:, kc:kc + 1],
                    yPb[kc % 4][:, :], ALU.mult, ALU.add),
                    waits=sw, incs=[S["vy3"]])
                k.op("vector", lambda e, kc=kc: e.tensor_tensor(
                    y3_16[:, kc, :], yscr[:, :], szt3[:, kc, :], ALU.mult),
                    incs=[S["vy3"]])
        if pend_copy[0] is not None:
            k.op("scalar", pend_copy[0],
                 waits=[(S["vscan"], k.val(S["vscan"]))], incs=[S["lbr"]])
            pend_copy[0] = None
        vy3v = cur("vy3")
        if DEBUG:
            k.dma("gpsimd", dbg_y[j].rearrange("p (c t) -> p c t", c=NCH),
                  y3_16[:, :, :], waits=[vy3v], incs=[S["ddbg"]])
        if j + 1 < NTL:
            p3_load(j + 1)
        for m in range(ND):
            bank = mmP[m % 2]
            for kc in range(NCH):
                w = []
                if kc == 0:
                    w = [vy3v, (S["so1"], max(0, k.val(S["so1"]) - 1))]
                k.op("tensor", lambda e, m=m, kc=kc, bank=bank: e.matmul(
                    bank[:, :], WOUT[:, kc, m * 128:(m + 1) * 128],
                    y3_16[:, kc, :], start=(kc == 0), stop=(kc == NCH - 1)),
                    waits=w, incs=[S["top"]] if kc == NCH - 1 else ())
            dsem = S["do1" if m % 2 == 0 else "do1b"]
            k.op("scalar", lambda e, m=m, bank=bank: e.activation(
                o1ev[m % 2][:, :], bank[:, :], AF.Copy),
                waits=[cur("top"), (dsem, k.val(dsem))],
                incs=[S["so1"]])
            k.dma("sync", cc_o1_in[j // 8, j % 8][:, m * T:(m + 1) * T],
                  o1ev[m % 2][:, :],
                  waits=[cur("so1")], incs=[dsem])

    k.op("gpsimd", lambda e: e.collective_compute(
        "ReduceScatter", ALU.add, replica_groups=GROUPS,
        ins=[cc_o1_in], outs=[cc_o1]),
        waits=[(S["do1"], k.val(S["do1"])), (S["do1b"], k.val(S["do1b"]))],
        incs=[S["cc2"]])

    if DEBUG:
        k.dma("gpsimd", dbg_o1, cc_o1, waits=[cur("cc2")], incs=[S["ddbg"]])

    if STAGE == 3:
        k.op("gpsimd", lambda e: None,
             waits=[cur("cc2")] + ([cur("ddbg")] if DEBUG else []))
        k.finalize()
        _NC_CACHE[(STAGE, DEBUG)] = (k.nc, None)
        return _NC_CACHE[(STAGE, DEBUG)]

    CC2 = cur("cc2")

    def ln_chain(stats_src, nchunks, pre_waits, want_mr16=False):
        for c in range(nchunks):
            cp = c % 2
            w = list(pre_waits) if c == 0 else []
            w.append((S["tstat"], max(0, k.val(S["tstat"]) - 2)))
            k.op("scalar", lambda e, c=c, cp=cp: e.activation(
                st16[cp][:, :], stats_src[:, c, :], AF.Copy),
                waits=w, incs=[S["sstat"]])
            k.op("scalar", lambda e, c=c, cp=cp: e.activation(
                sqt16[cp][:, :], stats_src[:, c, :], AF.Square),
                incs=[S["sstat"]])
            w2 = [cur("sstat")]
            if c == 0:
                w2.append(cur("vrow"))
            k.op("tensor", lambda e, cp=cp, c=c, n=nchunks: e.matmul(
                rowP[0][0:1, :], ones_c16[:, :], st16[cp][:, :],
                start=(c == 0), stop=(c == n - 1)),
                waits=w2, incs=[S["tstat"]])
            k.op("tensor", lambda e, cp=cp, c=c, n=nchunks: e.matmul(
                rowP[1][0:1, :], ones_c16[:, :], sqt16[cp][:, :],
                start=(c == 0), stop=(c == n - 1)),
                incs=[S["tstat"]])
        tsv = cur("tstat")

        def vop(fn, waits=()):
            k.op("vector", fn, waits=waits, incs=[S["vrow"]])

        vop(lambda e: e.tensor_scalar_mul(r0[:, :], rowP[0][0:1, :], 1.0 / D),
            waits=[tsv])
        vop(lambda e: e.tensor_scalar_mul(r1[:, :], rowP[1][0:1, :], 1.0 / D))
        vop(lambda e: e.tensor_tensor(r2[:, :], r0[:, :], r0[:, :], ALU.mult))
        vop(lambda e: e.tensor_tensor(r1[:, :], r1[:, :], r2[:, :], ALU.subtract))
        vop(lambda e: e.tensor_scalar_add(r1[:, :], r1[:, :], EPS))
        k.op("scalar", lambda e: e.activation(r1[:, :], r1[:, :], AF.Ln),
             waits=[cur("vrow")], incs=[S["srow"]])
        k.op("scalar", lambda e: e.activation(rR[:, :], r1[:, :], AF.Exp,
                                              scale=-0.5), incs=[S["srow"]])
        vop(lambda e: e.tensor_tensor(rMR[:, :], r0[:, :], rR[:, :], ALU.mult),
            waits=[cur("srow")])
        if want_mr16:
            vop(lambda e: e.tensor_copy(rMR16[:, :], rMR[:, :]))
        return cur("vrow")

    o1t = view3(AFa, 0, ND, T)
    xht = view3(AFa, 4096, ND, T)
    vt = view3(AFa, 8192, ND, T)
    u2t = view3(AFa, 12288, ND, T)
    x1t = view3(AFa, 16384 - 4096 + 4096, ND, T) if False else None
    x1t = o1t
    vm16 = view3(AT, 0, ND, T)
    g16 = view3(AT, 4096, ND, T)
    g16h = view3(AT, 0, 16, T)

    WF1V = view3(AW, 0, ND, DFF)
    k.dma("gpsimd", WF1V, w_f1.rearrange("p (c m) -> p c m", c=ND),
          waits=[CC2], incs=[S["ldwb"]])
    LDW_F1 = k.val(S["ldwb"])
    k.dma("sync", g2row[:, :], g2r_in, waits=[CC2], incs=[S["prm"]])
    G2R_LD = (S["prm"], k.val(S["prm"]))
    for mt in range(DFF // T):
        for kc in range(ND):
            w = []
            if kc == 0:
                w = [(S["ldwb"], LDW_F1), PRM, PRMG,
                     (S["vrow"], k.val(S["vrow"]))]
            k.op("tensor", lambda e, mt=mt, kc=kc: e.matmul(
                rowP[0][0:1, :], g2col16[:, kc:kc + 1],
                WF1V[:, kc, mt * T:(mt + 1) * T],
                start=(kc == 0), stop=(kc == ND - 1)),
                waits=w, incs=[S["tstat"]] if kc == ND - 1 else ())
        k.op("vector", lambda e, mt=mt: e.tensor_scalar_mul(
            nwg2[0:1, mt * T:(mt + 1) * T], rowP[0][0:1, :], -1.0),
            waits=[cur("tstat")], incs=[S["vrow"]])
    for m in range(32):
        for kc in range(ND):
            k.op("tensor", lambda e, m=m, kc=kc: e.matmul(
                repP[0][:, m:m + 1], WF1V[:, kc, m * 128:(m + 1) * 128],
                b2col16[:, kc:kc + 1],
                start=(kc == 0), stop=(kc == ND - 1)),
                incs=[S["tstat"]] if (kc == ND - 1 and m == 31) else ())
    k.op("scalar", lambda e: e.activation(wb2c[:, :], repP[0][:, 0:32], AF.Copy),
         waits=[cur("tstat")], incs=[S["sevac"]])
    k.op("vector", lambda e: e.tensor_tensor(wb2c[:, :], wb2c[:, :],
                                             bf1c[:, :], ALU.add),
         waits=[cur("sevac"), PRM], incs=[S["vrow"]])
    P4A_RDY = [(S["vrow"], k.val(S["vrow"])), (S["sevac"], k.val(S["sevac"]))]

    def p4a_load(i):
        p = i % 2
        k.dma("sync", o1t[:, :, :], cc_o1[i].rearrange("p (c t) -> p c t", c=ND),
              waits=[CC2, (S["vv"], k.val(S["vv"]))],
              incs=[S["lo0" if p == 0 else "lo1"]])
        k.dma("sync", xht[:, :, :],
              xTh[i].rearrange("p (c t) -> p c t", c=ND),
              waits=[CC2, (S["vvm"], k.val(S["vvm"]))],
              incs=[S["lh0" if p == 0 else "lh1"]])

    p4a_load(0)
    for i in range(NTO):
        p = i % 2
        lo = (S["lo0" if p == 0 else "lo1"], 16 * (i // 2 + 1))
        lh = (S["lh0" if p == 0 else "lh1"], 16 * (i // 2 + 1))
        vr = ln_chain(xht, ND, [lh] + (P4A_RDY if i == 0 else []))
        k.op("tensor", lambda e: e.matmul(repP[0][:, :], ones_r[:, :],
                                          rR[:, :], start=True, stop=True),
             waits=[vr, (S["vv"], k.val(S["vv"]))], incs=[S["trep"]])
        k.op("tensor", lambda e: e.matmul(repP[1][:, :], ones_r[:, :],
                                          rMR[:, :], start=True, stop=True),
             incs=[S["trep"]])
        trv = cur("trep")
        for c in range(ND):
            k.op("vector", lambda e, c=c: e.tensor_tensor(
                vt[:, c, :], xht[:, c, :], repP[0][:, :], ALU.mult),
                waits=[trv, lo, (S["dspv"], k.val(S["dspv"]))]
                if c == 0 else (), incs=[S["vv"]])
            k.op("vector", lambda e, c=c: e.tensor_tensor(
                vt[:, c, :], vt[:, c, :], repP[1][:, :], ALU.subtract))
            k.op("vector", lambda e, c=c: e.tensor_tensor(
                vt[:, c, :], vt[:, c, :], o1t[:, c, :], ALU.add),
                incs=[S["vv"]])
        vvv = cur("vv")
        k.dma("sync", sp_v[i].rearrange("p (c t) -> p c t", c=ND), vt[:, :, :],
              waits=[vvv], incs=[S["dspv"]])
        vr2 = ln_chain(vt, ND, [vvv], want_mr16=True)
        for c in range(ND):
            bank = repP[2 + c % 2]
            k.op("tensor", lambda e, c=c, bank=bank: e.matmul(
                bank[:, :], g2row[0:1, c * 128:(c + 1) * 128], rR[:, :],
                start=True, stop=True),
                waits=[vr2, (S["vvm"], max(0, k.val(S["vvm"]) - 1))] +
                      ([G2R_LD] if i == 0 and c == 0 else []),
                incs=[S["trep"]])
            k.op("vector", lambda e, c=c, bank=bank: e.tensor_tensor(
                vm16[:, c, :], vt[:, c, :], bank[:, :], ALU.mult),
                waits=[cur("trep")], incs=[S["vvm"]])
        vmv = cur("vvm")
        for m in range(32):
            bank = mmP[m % 2]
            for kc in range(ND):
                w = []
                if kc == 0:
                    w = [vmv, (S["sg"], max(0, k.val(S["sg"]) - 1))]
                k.op("tensor", lambda e, m=m, kc=kc, bank=bank: e.matmul(
                    bank[:, :], WF1V[:, kc, m * 128:(m + 1) * 128],
                    vm16[:, kc, :], start=(kc == 0), stop=False),
                    waits=w)
            k.op("tensor", lambda e, m=m, bank=bank: e.matmul(
                bank[:, :], nwg2[0:1, m * 128:(m + 1) * 128], rMR16[:, :],
                start=False, stop=True), incs=[S["tmm"]])
            k.op("scalar", lambda e, m=m, bank=bank: e.activation(
                g16[:, m % 8, :], bank[:, :], AF.Gelu_apprx_tanh,
                bias=wb2c[:, m:m + 1]),
                waits=[cur("tmm")] +
                      ([(S["dspg"], 16 * (4 * i + m // 8))]
                       if m % 8 == 0 else []),
                incs=[S["sg"]])
            if m % 8 == 7:
                k.dma("sync", sp_g[i][:, (m - 7) * T:(m + 1) * T],
                      g16[:, :, :], waits=[cur("sg")], incs=[S["dspg"]])
        if i + 1 < NTO:
            p4a_load(i + 1)

    P4A_DONE = [(S["tmm"], k.val(S["tmm"])), (S["sg"], k.val(S["sg"])),
                (S["vvm"], k.val(S["vvm"]))]
    WF2V = view3(AW, 0, 32, D)
    k.dma("gpsimd", WF2V, w_f2.rearrange("p (c m) -> p c m", c=32),
          waits=P4A_DONE, incs=[S["ldwc"]])
    LDW_F2 = k.val(S["ldwc"])

    def p4b_load(i, half):
        p = (2 * i + half) % 2
        w = [(S["dspg"], 16 * 4 * (NTO if i == 0 and half == 0 else i + 1)),
             (S["tvm"], k.val(S["tvm"])),
             (S["tmm"], k.val(S["tmm"]))]
        if i == 0 and half == 0:
            w += P4A_DONE + [(S["vv"], k.val(S["vv"]))]
        k.dma("sync", g16h[:, :, :],
              sp_g[i][:, half * 16 * T:(half + 1) * 16 * T],
              waits=w, incs=[S["lg0" if p == 0 else "lg1"]])

    def p4b_load_vx(i):
        w1 = [(S["dspv"], 16 * (NTO if i == 0 else i + 1)),
              (S["vu2"], k.val(S["vu2"]))]
        w2 = [(S["vx1"], k.val(S["vx1"]))]
        if i == 0:
            w1 += P4A_DONE
            w2 += P4A_DONE + [(S["vv"], k.val(S["vv"]))]
        k.dma("sync", vt[:, :, :], sp_v[i].rearrange("p (c t) -> p c t", c=ND),
              waits=w1, incs=[S["lv0" if i % 2 == 0 else "lv1"]])
        k.dma("sync", xht[:, :, :], xTh[i].rearrange("p (c t) -> p c t", c=ND),
              waits=w2, incs=[S["lh0" if i % 2 == 0 else "lh1"]])

    LH_BASE = k.val(S["lh0"]), k.val(S["lh1"])
    p4b_load(0, 0)
    p4b_load_vx(0)
    for i in range(NTO):
        p = i % 2
        lv = (S["lv0" if p == 0 else "lv1"], 16 * (i // 2 + 1))
        lh = (S["lh0" if p == 0 else "lh1"],
              (LH_BASE[p] if p == 0 else LH_BASE[1]) + 16 * (i // 2 + 1))
        for half in range(2):
            lg = (S["lg0" if (2 * i + half) % 2 == 0 else "lg1"],
                  16 * ((2 * i + half) // 2 + 1))
            banks = [mmP[0], mmP[1], repP[0], repP[1], repP[2], repP[3],
                     rowP[0], rowP[1]]
            for m in range(ND):
                for kcl in range(16):
                    kc = half * 16 + kcl
                    w = []
                    if m == 0 and kcl == 0:
                        w = [lg, (S["ldwc"], LDW_F2)]
                        if half == 0:
                            w += [(S["vu2"], k.val(S["vu2"])),
                                  (S["vx1"], k.val(S["vx1"])),
                                  (S["vw"], k.val(S["vw"]))]
                    incs = []
                    if half == 1 and kcl == 15:
                        incs = [S["tmm"]]
                    elif half == 0 and kcl == 15 and m == ND - 1:
                        incs = [S["tvm"]]
                    k.op("tensor", lambda e, m=m, kc=kc, kcl=kcl, half=half,
                         bank=banks[m]: e.matmul(
                        bank[:, :], WF2V[:, kc, m * 128:(m + 1) * 128],
                        g16h[:, kcl, :],
                        start=(half == 0 and kcl == 0),
                        stop=(half == 1 and kcl == 15)),
                        waits=w, incs=incs)
            if half == 0:
                p4b_load(i, 1)
        tff2 = cur("tmm")
        banks = [mmP[0], mmP[1], repP[0], repP[1], repP[2], repP[3],
                 rowP[0], rowP[1]]
        for c in range(ND):
            k.op("vector", lambda e, c=c, bank=banks[c]: e.scalar_tensor_tensor(
                u2t[:, c, :], bank[:, :], b2c[:, c:c + 1], vt[:, c, :],
                ALU.add, ALU.add),
                waits=[tff2, lv, (S["dspw"], k.val(S["dspw"]))]
                if c == 0 else (), incs=[S["vu2"]])
        vu2v = cur("vu2")
        vr3 = ln_chain(u2t, ND, [vu2v])
        k.op("tensor", lambda e: e.matmul(repP[0][:, :], ones_r[:, :],
                                          rR[:, :], start=True, stop=True),
             waits=[vr3, (S["vx1"], k.val(S["vx1"]))], incs=[S["trep"]])
        k.op("tensor", lambda e: e.matmul(repP[1][:, :], ones_r[:, :],
                                          rMR[:, :], start=True, stop=True),
             incs=[S["trep"]])
        trv = cur("trep")
        for c in range(ND):
            k.op("vector", lambda e, c=c: e.tensor_tensor(
                x1t[:, c, :], u2t[:, c, :], repP[0][:, :], ALU.mult),
                waits=[trv, lh, (S["dspx1"], k.val(S["dspx1"]))]
                if c == 0 else (), incs=[S["vx1"]])
            k.op("vector", lambda e, c=c: e.tensor_tensor(
                x1t[:, c, :], x1t[:, c, :], repP[1][:, :], ALU.subtract))
            k.op("vector", lambda e, c=c: e.tensor_tensor(
                x1t[:, c, :], x1t[:, c, :], xht[:, c, :], ALU.add),
                incs=[S["vx1"]])
        vx1v = cur("vx1")
        k.dma("sync", sp_x1[i].rearrange("p (c t) -> p c t", c=ND),
              x1t[:, :, :], waits=[vx1v], incs=[S["dspx1"]])
        vr4 = ln_chain(x1t, ND, [vx1v])
        k.op("tensor", lambda e: e.matmul(repP[2][:, :], ones_r[:, :],
                                          rR[:, :], start=True, stop=True),
             waits=[vr4, (S["vw"], k.val(S["vw"]))], incs=[S["trep"]])
        k.op("tensor", lambda e: e.matmul(repP[3][:, :], ones_r[:, :],
                                          rMR[:, :], start=True, stop=True),
             incs=[S["trep"]])
        trv = cur("trep")
        for c in range(ND):
            k.op("vector", lambda e, c=c: e.tensor_tensor(
                u2t[:, c, :], x1t[:, c, :], repP[2][:, :], ALU.mult),
                waits=[trv] if c == 0 else (), incs=[S["vw"]])
            k.op("vector", lambda e, c=c: e.tensor_tensor(
                u2t[:, c, :], u2t[:, c, :], repP[3][:, :], ALU.subtract),
                incs=[S["vw"]])
        vwv = cur("vw")
        k.dma("sync", sp_w[i].rearrange("p (c t) -> p c t", c=ND),
              u2t[:, :, :], waits=[vwv], incs=[S["dspw"]])
        k.op("vector", lambda e: e.tensor_reduce(
            mred[:, :], u2t[:, :, :], AX.X, ALU.add),
            waits=[vwv], incs=[S["vmean"]])
        k.op("vector", lambda e: e.nop(), waits=[cur("vmean")])
        if i == 0:
            k.op("vector", lambda e: e.tensor_copy(mean_acc[:, :], mred[:, :]),
                 incs=[S["vmean"]])
        else:
            k.op("vector", lambda e: e.tensor_tensor(
                mean_acc[:, :], mean_acc[:, :], mred[:, :], ALU.add),
                incs=[S["vmean"]])
        if i + 1 < NTO:
            p4b_load(i + 1, 0)
            p4b_load_vx(i + 1)

    k.dma("sync", cc_mean_in, mean_acc[:, :], waits=[cur("vmean")],
          incs=[S["dmean"]])
    k.op("gpsimd", lambda e: e.collective_compute(
        "AllReduce", ALU.add, replica_groups=GROUPS,
        ins=[cc_mean_in], outs=[cc_mean]),
        waits=[cur("dmean")], incs=[S["cc3"]])

    if DEBUG:
        k.dma("gpsimd", dbg_v, sp_v, waits=[cur("dspv")], incs=[S["ddbg"]])
        k.dma("gpsimd", dbg_x1, sp_x1, waits=[cur("dspx1")], incs=[S["ddbg"]])
        k.dma("gpsimd", dbg_w, sp_w, waits=[cur("dspw")], incs=[S["ddbg"]])
        k.dma("gpsimd", dbg_mean, cc_mean, waits=[cur("cc3")], incs=[S["ddbg"]])

    if STAGE == 4:
        k.op("gpsimd", lambda e: None,
             waits=[cur("cc3")] + ([cur("ddbg")] if DEBUG else []))
        k.finalize()
        _NC_CACHE[(STAGE, DEBUG)] = (k.nc, None)
        return _NC_CACHE[(STAGE, DEBUG)]

    P4B_TMM = (S["tmm"], k.val(S["tmm"]))
    WCN1 = view3(AW, 0, ND, D)
    WCN2 = view3(AW, 8192, ND, 512)
    WCN3 = view3(AW, 12288, 4, 256)
    WCN4 = view3(AW, 13312, 2, 1)
    WGC1 = view3(AW, 13320, ND, 512)
    WGC2 = view3(AW, 17416, 4, 256)
    WGC3 = view3(AW, 18440, 2, 1)
    for dst, srcw, c in ((WCN1, cn_w1, ND), (WCN2, cn_w2, ND),
                         (WCN3, cn_w3, 4), (WCN4, cn_w4, 2),
                         (WGC1, gc_w1, ND), (WGC2, gc_w2, 4),
                         (WGC3, gc_w3, 2)):
        k.dma("gpsimd", dst, srcw.rearrange("p (c m) -> p c m", c=c),
              waits=[P4B_TMM], incs=[S["ldwd"]])
    LDW_P5 = k.val(S["ldwd"])

    k.dma("sync", mean_acc[:, :], cc_mean, waits=[cur("cc3")], incs=[S["dmean"]])
    k.op("vector", lambda e: e.tensor_copy(ctx16[:, :], mean_acc[:, :]),
         waits=[cur("dmean")], incs=[S["vcst"]])
    k.op("vector", lambda e: e.tensor_scalar_mul(ctx16[:, :], ctx16[:, :], 0.0),
         waits=[(S["vcst"], 0)], incs=[]) if False else None
    k.op("vector", lambda e: e.tensor_scalar_mul(ctx16[:, :], mean_acc[:, :],
                                                 1.0 / L),
         waits=[cur("vcst")], incs=[S["vcst"]])
    CTXV = cur("vcst")
    for m in range(4):
        for kc in range(ND):
            w = []
            if m == 0 and kc == 0:
                w = [CTXV, (S["ldwd"], LDW_P5)]
            k.op("tensor", lambda e, m=m, kc=kc: e.matmul(
                repP[0][:, m:m + 1], WGC1[:, kc, m * 128:(m + 1) * 128],
                ctx16[:, kc:kc + 1], start=(kc == 0), stop=(kc == ND - 1)),
                waits=w, incs=[S["trep"]] if kc == ND - 1 else ())
        k.op("scalar", lambda e, m=m: e.activation(
            gc_h1[:, m:m + 1], repP[0][:, m:m + 1], AF.Gelu,
            bias=gb1c[:, m:m + 1]),
            waits=[cur("trep")], incs=[S["sgf"]])
    for m in range(2):
        for kc in range(4):
            k.op("tensor", lambda e, m=m, kc=kc: e.matmul(
                repP[1][:, m:m + 1], WGC2[:, kc, m * 128:(m + 1) * 128],
                gc_h1[:, kc:kc + 1], start=(kc == 0), stop=(kc == 3)),
                waits=[cur("sgf")] if kc == 0 else (),
                incs=[S["trep"]] if kc == 3 else ())
        k.op("scalar", lambda e, m=m: e.activation(
            gc_h2[:, m:m + 1], repP[1][:, m:m + 1], AF.Gelu,
            bias=gb2c[:, m:m + 1]),
            waits=[cur("trep")], incs=[S["sgf"]])
    for kc in range(2):
        k.op("tensor", lambda e, kc=kc: e.matmul(
            rowP[0][0:1, 0:1], WGC3[:, kc, 0:1], gc_h2[:, kc:kc + 1],
            start=(kc == 0), stop=(kc == 1)),
            waits=[cur("sgf")] if kc == 0 else (),
            incs=[S["trep"]] if kc == 1 else ())
    def cst_op(fn, extra=()):
        k.op("vector", fn, waits=[cur("vcst")] + list(extra), incs=[S["vcst"]])

    cst_op(lambda e: e.tensor_scalar(cst[:, 3:4], cst[:, 1:2], 0.5, None,
                                     ALU.mult), extra=[PRM])
    k.op("scalar", lambda e: e.activation(cst[:, 5:6], rowP[0][0:1, 0:1],
                                          AF.Tanh, scale=0.5,
                                          bias=cst[:, 3:4]),
         waits=[cur("trep"), cur("vcst")], incs=[S["sgf"]])
    cst_op(lambda e: e.tensor_scalar(cst[:, 6:7], cst[:, 5:6], 0.5, 0.5,
                                     ALU.mult, ALU.add), extra=[cur("sgf")])
    cst_op(lambda e: e.tensor_scalar(cst[:, 7:8], cst[:, 6:7], cst[:, 0:1],
                                     None, ALU.mult))
    cst_op(lambda e: e.tensor_scalar(cst[:, 8:9], cst[:, 7:8], 1.4, 0.1,
                                     ALU.mult, ALU.add))
    cst_op(lambda e: e.tensor_scalar(cst[:, 9:10], cst[:, 8:9], cst[:, 0:1],
                                     None, ALU.subtract))
    cst_op(lambda e: e.tensor_scalar(cst[:, 11:12], cst[:, 0:1], 0.6, None,
                                     ALU.mult))
    cst_op(lambda e: e.tensor_scalar(cst[:, 10:11], cst[:, 11:12], 0.5,
                                     cst[:, 9:10], ALU.mult, ALU.add))
    cst_op(lambda e: e.tensor_scalar(cst[:, 12:13], cst[:, 11:12], 0.5, None,
                                     ALU.mult))
    cst_op(lambda e: e.tensor_scalar(cst[:, 4:5], cst[:, 2:3], 0.5, None,
                                     ALU.mult))
    GFV = cur("vcst")

    outs5 = [view3(AFa, 0, ND, T), view3(AFa, 4096, ND, T)]
    x1t5 = view3(AFa, 8192, ND, T)
    twsc = AFa[0:128, 12288:12800]
    o16 = view3(AT, 0, ND, T)
    c1_16 = view3(AT, 4096, ND, T)
    c2_16 = view3(AT, 8192, 4, T)
    c3_16 = view3(AT, 10240, 2, T)
    SIGN = [ALU.add, ALU.add, ALU.subtract, ALU.subtract,
            ALU.subtract, ALU.subtract, ALU.add, ALU.add]
    P4B_ALLSP = [(S["dspw"], k.val(S["dspw"])), (S["dspx1"], k.val(S["dspx1"]))]

    def p5_load(i):
        w1 = [(S["dspw"], 16 * (i + 1))]
        w2 = [(S["dspx1"], 16 * (i + 1)), (S["vfin"], k.val(S["vfin"]))]
        if i == 0:
            w1 += P4B_ALLSP + [(S["vmean"], k.val(S["vmean"]))]
            w2 += P4B_ALLSP
        else:
            w1.append((S["dout"], 16 * i))
        k.dma("sync", outs5[0][:, :, :],
              sp_w[i].rearrange("p (c t) -> p c t", c=ND),
              waits=w1, incs=[S["lw0" if i % 2 == 0 else "lw1"]])
        k.dma("sync", x1t5[:, :, :],
              sp_x1[i].rearrange("p (c t) -> p c t", c=ND),
              waits=w2, incs=[S["l10" if i % 2 == 0 else "l11"]])

    if P5_LVL == 0:
        k.op("sync", lambda e: None, waits=[GFV] +
             ([cur("ddbg")] if DEBUG else []))
        k.finalize()
        _NC_CACHE[(STAGE, DEBUG)] = (k.nc, None)
        return _NC_CACHE[(STAGE, DEBUG)]

    p5_load(0)
    for i in range(min(NTO, P5_LVL)):
        p = i % 2
        lw = (S["lw0" if p == 0 else "lw1"], 16 * (i // 2 + 1))
        l1 = (S["l10" if p == 0 else "l11"], 16 * (i // 2 + 1))
        for r in range(3):
            src_, dst = outs5[r % 2], outs5[1 - r % 2]
            w = [lw] if r == 0 else [(S["vtw"], k.val(S["vtw"]))]
            w.append((S["tmm"], k.val(S["tmm"])))
            k.op("scalar", lambda e, src_=src_: e.activation(
                o16[:, :, :], src_[:, :, :], AF.Copy),
                waits=w, incs=[S["svm"]])
            sv = cur("svm")
            for m in range(ND):
                bank = mmP[m % 2]
                for kc in range(ND):
                    w = []
                    if kc == 0:
                        w = [sv, (S["sg"], max(0, k.val(S["sg"]) - 1))]
                    k.op("tensor", lambda e, m=m, kc=kc, bank=bank: e.matmul(
                        bank[:, :], WCN1[:, kc, m * 128:(m + 1) * 128],
                        o16[:, kc, :], start=(kc == 0), stop=(kc == ND - 1)),
                        waits=w, incs=[S["tmm"]] if kc == ND - 1 else ())
                k.op("scalar", lambda e, m=m, bank=bank: e.activation(
                    c1_16[:, m, :], bank[:, :], AF.Gelu, bias=cb1c[:, m:m + 1]),
                    waits=[cur("tmm")], incs=[S["sg"]])
            for m in range(4):
                bank = mmP[m % 2]
                for kc in range(ND):
                    w = []
                    if kc == 0:
                        w = [cur("sg") if m == 0 else (S["sg"], k.val(S["sg"])),
                             (S["sg"], max(0, k.val(S["sg"]) - 1))]
                    k.op("tensor", lambda e, m=m, kc=kc, bank=bank: e.matmul(
                        bank[:, :], WCN2[:, kc, m * 128:(m + 1) * 128],
                        c1_16[:, kc, :], start=(kc == 0), stop=(kc == ND - 1)),
                        waits=w, incs=[S["tmm"]] if kc == ND - 1 else ())
                k.op("scalar", lambda e, m=m, bank=bank: e.activation(
                    c2_16[:, m, :], bank[:, :], AF.Gelu, bias=cb2c[:, m:m + 1]),
                    waits=[cur("tmm")], incs=[S["sg"]])
            for m in range(2):
                bank = mmP[m % 2]
                for kc in range(4):
                    w = []
                    if kc == 0:
                        w = [(S["sg"], k.val(S["sg"])),
                             (S["sg"], max(0, k.val(S["sg"]) - 1))]
                    k.op("tensor", lambda e, m=m, kc=kc, bank=bank: e.matmul(
                        bank[:, :], WCN3[:, kc, m * 128:(m + 1) * 128],
                        c2_16[:, kc, :], start=(kc == 0), stop=(kc == 3)),
                        waits=w, incs=[S["tmm"]] if kc == 3 else ())
                k.op("scalar", lambda e, m=m, bank=bank: e.activation(
                    c3_16[:, m, :], bank[:, :], AF.Gelu, bias=cb3c[:, m:m + 1]),
                    waits=[cur("tmm")], incs=[S["sg"]])
            for kc in range(2):
                w = []
                if kc == 0:
                    w = [(S["sg"], k.val(S["sg"])),
                         (S["vrow"], k.val(S["vrow"]))]
                k.op("tensor", lambda e, kc=kc: e.matmul(
                    rowP[0][0:1, :], WCN4[:, kc, 0:1], c3_16[:, kc, :],
                    start=(kc == 0), stop=(kc == 1)),
                    waits=w, incs=[S["tmm"]] if kc == 1 else ())
            k.op("scalar", lambda e: e.activation(
                r1[:, :], rowP[0][0:1, :], AF.Tanh, scale=0.5,
                bias=cst[:, 4:5]),
                waits=[cur("tmm"), GFV], incs=[S["sgf"]])
            k.op("vector", lambda e: e.tensor_scalar(
                r2[:, :], r1[:, :], cst[:, 12:13], cst[:, 10:11],
                ALU.mult, ALU.add),
                waits=[cur("sgf")], incs=[S["vcoup"]])
            k.op("tensor", lambda e: e.matmul(
                repP[2][:, :], ones_r[:, :], r2[:, :], start=True, stop=True),
                waits=[cur("vcoup"), (S["vtw"], k.val(S["vtw"]))],
                incs=[S["trep"]])
            trv = cur("trep")
            for c in range(ND):
                k.op("vector", lambda e, c=c, src_=src_: e.tensor_tensor(
                    twsc[:, :], src_[:, (c + 4) % ND, :], repP[2][:, :],
                    ALU.mult),
                    waits=[trv] if c == 0 else (), incs=[S["vtw"]])
                k.op("vector", lambda e, c=c, src_=src_, dst=dst, op=SIGN[c]:
                     e.tensor_tensor(dst[:, c, :], src_[:, c, :], twsc[:, :],
                                     op),
                     incs=[S["vtw"]])
        res5 = outs5[1]
        vtwf = (S["vtw"], k.val(S["vtw"]))
        vr5 = ln_chain(res5, ND, [vtwf])
        k.op("tensor", lambda e: e.matmul(repP[0][:, :], ones_r[:, :],
                                          rR[:, :], start=True, stop=True),
             waits=[vr5, (S["vfin"], k.val(S["vfin"]))], incs=[S["trep"]])
        k.op("tensor", lambda e: e.matmul(repP[1][:, :], ones_r[:, :],
                                          rMR[:, :], start=True, stop=True),
             incs=[S["trep"]])
        trv = cur("trep")
        for c in range(ND):
            k.op("vector", lambda e, c=c: e.tensor_tensor(
                outs5[0][:, c, :], res5[:, c, :], repP[0][:, :], ALU.mult),
                waits=[trv, l1, (S["dout"], k.val(S["dout"]))]
                if c == 0 else (), incs=[S["vfin"]])
            k.op("vector", lambda e, c=c: e.tensor_tensor(
                outs5[0][:, c, :], outs5[0][:, c, :], repP[1][:, :],
                ALU.subtract))
            k.op("vector", lambda e, c=c: e.tensor_tensor(
                outs5[0][:, c, :], outs5[0][:, c, :], x1t5[:, c, :], ALU.add),
                incs=[S["vfin"]])
        k.dma("sync", out_d[i], outs5[0][:, :, :].rearrange("p c t -> p (c t)"),
              waits=[cur("vfin")], incs=[S["dout"]])
        if i + 1 < min(NTO, P5_LVL):
            p5_load(i + 1)

    k.op("sync", lambda e: None,
         waits=[(S["dout"], 16 * min(NTO, P5_LVL))] +
               ([cur("ddbg")] if DEBUG else []))
    k.finalize()
    _NC_CACHE[(STAGE, DEBUG)] = (k.nc, None)
    return _NC_CACHE[(STAGE, DEBUG)]


def pm(w):
    K, M = w.shape
    nk = K // 128
    return np.ascontiguousarray(
        w.reshape(nk, 128, M).transpose(1, 0, 2).reshape(128, nk * M))


def tok_tiles(a_dt, nt):
    Dm, Lt = a_dt.shape
    t = Lt // nt
    r = a_dt.reshape(ND, 128, nt, t).transpose(2, 1, 0, 3)
    return np.ascontiguousarray(r.reshape(nt, 128, ND * t))


def make_inmaps(inputs):
    x = np.asarray(inputs["L0_attn_x"])
    xt = np.ascontiguousarray(np.transpose(x, (0, 2, 1)))
    ipw = np.asarray(inputs["L0_attn_in_proj_w"])
    maps = []
    for c in range(8):
        b, h = c // 2, c % 2
        ch0 = h * DH
        m = {
            "xT": tok_tiles(xt[b], NTL),
            "xTh": tok_tiles(xt[b][:, h * TOK:(h + 1) * TOK], NTO),
            "w_inj": pm(np.concatenate(
                [ipw[:, ch0:ch0 + DH], ipw[:, DI + ch0:DI + ch0 + DH]], axis=1)),
            "conv_w": np.ascontiguousarray(
                np.asarray(inputs["L0_attn_conv_w"])[ch0:ch0 + DH, 0, :]),
            "conv_b": np.ascontiguousarray(
                np.asarray(inputs["L0_attn_conv_b"])[ch0:ch0 + DH].reshape(DH, 1)),
            "w_xp": pm(np.asarray(inputs["L0_attn_x_proj_w"])[ch0:ch0 + DH, :]),
            "w_dtp": np.ascontiguousarray(
                np.asarray(inputs["L0_attn_dt_proj_w"])[:, ch0:ch0 + DH]),
            "b_dtp": np.ascontiguousarray(
                np.asarray(inputs["L0_attn_dt_proj_b"])[ch0:ch0 + DH].reshape(DH, 1)),
            "A_log": np.ascontiguousarray(
                np.asarray(inputs["L0_attn_A_log"])[ch0:ch0 + DH, :]),
            "D_h": np.ascontiguousarray(
                np.asarray(inputs["L0_attn_D"])[ch0:ch0 + DH].reshape(DH, 1)),
            "w_out": pm(np.asarray(inputs["L0_attn_out_proj_w"])[ch0:ch0 + DH, :]),
            "p3_ident": np.eye(128, dtype=np.float32),
            "ln1_g": np.asarray(inputs["L0_attn_ln1_g"]).reshape(1, D),
            "ln1_b": np.asarray(inputs["L0_attn_ln1_b"]).reshape(1, D),
            "ln1_g_col": np.asarray(inputs["L0_attn_ln1_g"]).reshape(D, 1),
            "ln1_b_col": np.asarray(inputs["L0_attn_ln1_b"]).reshape(D, 1),
            "ln2_g": np.asarray(inputs["L0_attn_ln2_g"]).reshape(1, D),
            "ln2_g_col": np.asarray(inputs["L0_attn_ln2_g"]).reshape(D, 1),
            "ln2_b_col": np.asarray(inputs["L0_attn_ln2_b"]).reshape(D, 1),
            "w_f1": pm(np.asarray(inputs["L0_attn_ffn_w1"])),
            "b_f1": np.asarray(inputs["L0_attn_ffn_b1"]).reshape(DFF, 1),
            "w_f2": pm(np.asarray(inputs["L0_attn_ffn_w2"])),
            "b_f2": np.asarray(inputs["L0_attn_ffn_b2"]).reshape(D, 1),
            "cn_w1": pm(np.asarray(inputs["L0_ffn_cn_w1"])),
            "cn_b1": np.asarray(inputs["L0_ffn_cn_b1"]).reshape(D, 1),
            "cn_w2": pm(np.asarray(inputs["L0_ffn_cn_w2"])),
            "cn_b2": np.asarray(inputs["L0_ffn_cn_b2"]).reshape(512, 1),
            "cn_w3": pm(np.asarray(inputs["L0_ffn_cn_w3"])),
            "cn_b3": np.asarray(inputs["L0_ffn_cn_b3"]).reshape(256, 1),
            "cn_w4": pm(np.asarray(inputs["L0_ffn_cn_w4"])),
            "cn_b4": np.asarray(inputs["L0_ffn_cn_b4"]).reshape(1, 1),
            "gc_w1": pm(np.asarray(inputs["L0_ffn_gc_w1"])),
            "gc_b1": np.asarray(inputs["L0_ffn_gc_b1"]).reshape(512, 1),
            "gc_w2": pm(np.asarray(inputs["L0_ffn_gc_w2"])),
            "gc_b2": np.asarray(inputs["L0_ffn_gc_b2"]).reshape(256, 1),
            "gc_w3": pm(np.asarray(inputs["L0_ffn_gc_w3"])),
            "gc_b3": np.asarray(inputs["L0_ffn_gc_b3"]).reshape(1, 1),
            "ar": np.asarray(inputs["L0_ffn_adaptive_range"]).reshape(1, 1),
        }
        maps.append(m)
    return maps


def kernel(**inputs):
    nc, _ = build_program()
    maps = make_inmaps(inputs)
    res = run_bass_kernel_spmd(nc, maps, core_ids=list(range(8)))
    out = np.empty((4, L, D), np.float32)
    for c in range(8):
        b, h = c // 2, c % 2
        o = np.asarray(res.results[c]["out"])
        o = o.reshape(NTO, 128, ND, T).transpose(0, 3, 2, 1)
        out[b, h * TOK:(h + 1) * TOK] = o.reshape(TOK, D)
    return np.ascontiguousarray(out)


def run_debug(inputs):
    nc, _ = build_program()
    maps = make_inmaps(inputs)
    res = run_bass_kernel_spmd(nc, maps, core_ids=list(range(8)))
    return res


# version 17 (final)
# speedup vs baseline: 1.0870x; 1.0048x over previous
import contextlib
import os

import numpy as np

import concourse.bass as bass
import concourse.mybir as mybir
from concourse.bass_utils import run_bass_kernel_spmd

F32 = mybir.dt.float32
BF16 = mybir.dt.bfloat16
AF = mybir.ActivationFunctionType
ALU = mybir.AluOpType
AX = mybir.AxisListType

D = 1024
DI = 2048
DH = 1024
DS = 16
DTR = 64
DFF = 4096
L = 8192
TOK = 4096
T = 512
NTL = L // T
NTO = TOK // T
ND = D // 128
NCH = DH // 128
EPS = 1e-5

AW_COLS = 32800
AF_COLS = 21510
AT_COLS = 12290

STAGE = 5
DEBUG = False
SIM_SAFE = os.environ.get("BASS_SIM_FP64") == "1"
P5_LVL = 8
NTL_RUN = 16

ENGS = ("sync", "gpsimd", "scalar", "vector", "tensor")
GROUPS = [[0, 1], [2, 3], [4, 5], [6, 7]]


class KB:
    def __init__(self):
        self.nc = bass.Bass("TRN2", target_bir_lowering=False, debug=False)
        self.es = contextlib.ExitStack()
        self.q = {e: [] for e in ENGS}
        self.cnt = {}

    def sem(self, name):
        h = self.es.enter_context(self.nc.semaphore(name))
        self.cnt[h] = 0
        return h

    def val(self, s):
        return self.cnt[s]

    def sb(self, name, shape, dtype):
        return self.es.enter_context(self.nc.sbuf_tensor(name, shape, dtype))

    def psum(self, name, shape):
        return self.es.enter_context(self.nc.psum_tensor(name, shape, F32))

    def op(self, eng, fn, waits=(), incs=(), amt=1):
        if SIM_SAFE and amt == 1 and eng != "sync":
            lp = self.last_pub.get(eng)
            if lp is not None:
                waits = tuple(waits) + (lp,)
            if not incs:
                incs = (self.selfsem[eng],)
            for s in incs:
                self.cnt[s] += amt
            self.last_pub[eng] = (incs[-1], self.cnt[incs[-1]])
        else:
            for s in incs:
                self.cnt[s] += amt
        self.q[eng].append((fn, tuple(waits), tuple(incs), amt))

    def dma(self, eng, out, in_, waits=(), incs=(), slow=False):
        self.op(eng, lambda e, o=out, i=in_, s=slow: e.dma_start(
            out=o, in_=i, allow_slow_non_contiguous=s),
            waits, incs, amt=16)

    def init_selfsems(self):
        self.selfsem = {e: self.sem(f"zz_self_{e}") for e in ENGS}
        self.last_pub = {}

    def finalize(self):
        block = self.es.enter_context(self.nc.Block())
        for name in ENGS:
            items = self.q[name]

            def make_emit(items=items):
                def emit(engine):
                    for fn, waits, incs, amt in items:
                        for (s, v) in waits:
                            engine.wait_ge(s, v)
                        instr = fn(engine)
                        if incs:
                            if instr is None:
                                instr = engine.nop()
                            for s in incs:
                                instr.then_inc(s, amt)
                return emit

            getattr(block, name)(make_emit())


def view3(tensor, off, c, t, p=128):
    return tensor[0:p, off:off + c * t].rearrange("p (c t) -> p c t", c=c)


_NC_CACHE = {}


def build_program():
    if (STAGE, DEBUG) in _NC_CACHE:
        return _NC_CACHE[(STAGE, DEBUG)]
    _ = P5_LVL
    k = KB()
    nc = k.nc

    def din(name, shape, dtype=F32):
        return nc.dram_tensor(name, shape, dtype, kind="ExternalInput").ap()

    def dout(name, shape, dtype=F32):
        return nc.dram_tensor(name, shape, dtype, kind="ExternalOutput").ap()

    def dint(name, shape, dtype=F32):
        return nc.dram_tensor(name, shape, dtype).ap()

    xT = din("xT", [NTL, 128, ND * T])
    xTh = din("xTh", [NTO, 128, ND * T])
    w_inj = din("w_inj", [128, ND * 2 * DH])
    conv_w = din("conv_w", [DH, 4])
    conv_b = din("conv_b", [DH, 1])
    w_xp = din("w_xp", [128, NCH * 96])
    w_dtp = din("w_dtp", [DTR, DH])
    b_dtp = din("b_dtp", [DH, 1])
    A_log = din("A_log", [DH, DS])
    D_in = din("D_h", [DH, 1])
    w_out = din("w_out", [128, NCH * D])
    ident_in = din("p3_ident", [128, 128])
    g1r_in = din("ln1_g", [1, D])
    b1r_in = din("ln1_b", [1, D])
    b1c_in = din("ln1_b_col", [D, 1])
    g1c_in = din("ln1_g_col", [D, 1])
    g2r_in = din("ln2_g", [1, D])
    b2c_in = din("ln2_b_col", [D, 1])
    g2c_in = din("ln2_g_col", [D, 1])
    w_f1 = din("w_f1", [128, ND * DFF])
    b_f1 = din("b_f1", [DFF, 1])
    w_f2 = din("w_f2", [128, (DFF // 128) * D])
    b_f2 = din("b_f2", [D, 1])
    cn_w1 = din("cn_w1", [128, ND * D])
    cn_b1 = din("cn_b1", [D, 1])
    cn_w2 = din("cn_w2", [128, ND * 512])
    cn_b2 = din("cn_b2", [512, 1])
    cn_w3 = din("cn_w3", [128, 4 * 256])
    cn_b3 = din("cn_b3", [256, 1])
    cn_w4 = din("cn_w4", [128, 2 * 1])
    cn_b4 = din("cn_b4", [1, 1])
    gc_w1 = din("gc_w1", [128, ND * 512])
    gc_b1 = din("gc_b1", [512, 1])
    gc_w2 = din("gc_w2", [128, 4 * 256])
    gc_b2 = din("gc_b2", [256, 1])
    gc_w3 = din("gc_w3", [128, 2 * 1])
    gc_b3 = din("gc_b3", [1, 1])
    ar_in = din("ar", [1, 1])

    out_d = dout("out", [NTO, 128, ND * T])

    cc_xdbl_in = dint("cc_xdbl_in", [2, 96, L // 2])
    cc_xdbl = dint("cc_xdbl", [2, 96, L // 2])
    cc_o1_in = dint("cc_o1_in", [2, NTO, 128, ND * T])
    cc_o1 = dint("cc_o1", [NTO, 128, ND * T])
    cc_mean_in = dint("cc_mean_in", [128, ND])
    cc_mean = dint("cc_mean", [128, ND])
    sp_xc = dint("sp_xc", [NTL, 128, NCH * T])
    sp_sz = dint("sp_sz", [NTL, 128, NCH * T], BF16)
    sp_g = dint("sp_g", [NTO, 128, 32 * T], BF16)
    sp_v = dint("sp_v", [NTO, 128, ND * T])
    sp_x1 = dint("sp_x1", [NTO, 128, ND * T])
    sp_w = dint("sp_w", [NTO, 128, ND * T])

    if DEBUG:
        dbg_xdbl = dout("dbg_xdbl", [2, 96, L // 2])
        dbg_xc = dout("dbg_xc", [NTL, 128, NCH * T])
        dbg_sz = dout("dbg_sz", [NTL, 128, NCH * T], BF16)
        if STAGE >= 3:
            dbg_o1 = dout("dbg_o1", [NTO, 128, ND * T])
            dbg_dt = dout("dbg_dt", [NTL, 128, NCH * T])
            dbg_y = dout("dbg_y", [NTL, 128, NCH * T], BF16)
        if STAGE >= 4:
            dbg_v = dout("dbg_v", [NTO, 128, ND * T])
            dbg_x1 = dout("dbg_x1", [NTO, 128, ND * T])
            dbg_w = dout("dbg_w", [NTO, 128, ND * T])
            dbg_mean = dout("dbg_mean", [128, ND])

    AW = k.sb("AW", [128, AW_COLS], BF16)
    AFa = k.sb("AFa", [128, AF_COLS], F32)
    AT = k.sb("AT", [128, AT_COLS], BF16)

    ones_c16 = k.sb("ones_c16", [128, 1], BF16)
    ones_r = k.sb("ones_r", [1, 128], F32)
    ones_r16 = k.sb("ones_r16", [1, 128], BF16)
    g1row = k.sb("g1row", [1, D], F32)
    g2row = g1row
    g1col16 = k.sb("g1col16", [128, ND], BF16)
    b1col16 = k.sb("b1col16", [128, ND], BF16)
    g2col16 = k.sb("g2col16", [128, ND], BF16)
    b2col16 = k.sb("b2col16", [128, ND], BF16)
    nwg2 = k.sb("nwg2", [1, DFF], BF16)
    nwg1 = nwg2[0:1, 0:2 * DH]
    rMR16 = k.sb("rMR16", [1, T], BF16)
    wb1c = k.sb("wb1c", [128, 16], F32)
    wb2c = k.sb("wb2c", [128, 32], F32)
    bdtc = k.sb("bdtc", [128, NCH], F32)
    cbc = k.sb("cbc", [128, NCH], F32)
    cwc = k.sb("cwc", [128, NCH, 4], F32)
    Acol = k.sb("Acol", [128, NCH, DS], F32)
    Dcol = k.sb("Dcol", [128, NCH], F32)
    b2c = k.sb("b2c", [128, ND], F32)
    cb1c = k.sb("cb1c", [128, ND], F32)
    bf1c = k.sb("bf1c", [128, 32], F32)
    mred = k.sb("mred", [128, ND], F32)
    cb2c = k.sb("cb2c", [128, 4], F32)
    cb3c = k.sb("cb3c", [128, 2], F32)
    gb1c = k.sb("gb1c", [128, 4], F32)
    gb2c = k.sb("gb2c", [128, 2], F32)
    cst = k.sb("cst", [1, 16], F32)
    mean_acc = k.sb("mean_acc", [128, ND], F32)
    ctx16 = k.sb("ctx16", [128, ND], BF16)
    gc_h1 = k.sb("gc_h1", [128, 4], BF16)
    gc_h2 = k.sb("gc_h2", [128, 2], BF16)
    state = k.sb("state", [128, NCH, DS], F32)

    i32 = k.sb("i32", [128, 128], F32)
    i16 = k.sb("i16", [128, 128], BF16)
    r0 = k.sb("r0", [1, T], F32)
    r1 = k.sb("r1", [1, T], F32)
    r2 = k.sb("r2", [1, T], F32)
    r3 = k.sb("r3", [1, T], F32)
    rR = k.sb("rR", [1, T], F32)
    rMR = k.sb("rMR", [1, T], F32)
    brow = [r0, r1]
    crow = [r2, r3]
    st16 = [k.sb(f"st16_{i}", [128, T], BF16) for i in range(2)]
    sqt16 = [k.sb(f"sqt16_{i}", [128, T], BF16) for i in range(2)]

    mmP = [k.psum(f"mmP{i}", [128, T]) for i in range(2)]
    repP = [k.psum(f"repP{i}", [128, T]) for i in range(4)]
    rowP = [k.psum(f"rowP{i}", [128, T]) for i in range(2)]

    SEMS = ("ldw", "prm", "misc", "lx0", "lx1", "sstat", "tstat", "vrow",
            "srow", "trep", "vrep", "tmm", "sevac", "vconv", "vtail", "sxc",
            "dspx", "dspz", "sxd", "dxdbl", "cc1",
            "ldt0", "ldt1", "lxc0", "lxc1", "lsz0", "lsz1",
            "lbr", "lcr", "tdtp", "sdt", "vdtx", "sda", "vscan", "vsdone",
            "sbr", "tbr", "tcr", "sbe", "vhc", "ghc", "scr", "gacc", "tacc",
            "ldwa", "ldwb", "ldwc", "ldwd", "do1b", "cc1b", "prmg",
            "lbr1", "lcr1",
            "vy3", "top", "so1", "do1", "cc2",
            "lo0", "lo1", "lh0", "lh1", "vv", "svm", "tvm", "vvm", "sg",
            "dspg", "dspv", "lg0", "lg1", "lv0", "lv1", "vu2", "vx1",
            "dspx1", "vw", "dspw", "vmean", "dmean", "cc3",
            "lw0", "lw1", "l10", "l11", "sgf", "vcst", "vcoup", "vtw",
            "vfin", "dout", "ddbg")
    S = {n: k.sem(n) for n in SEMS}
    k.init_selfsems()

    def W(*pairs):
        return tuple(pairs)

    def cur(name):
        return (S[name], k.val(S[name]))

    WJ_O, WXP_O = 0, 16384
    WJ = view3(AW, WJ_O, ND, 2 * DH)
    WXP = view3(AW, WXP_O, NCH, 96)
    k.dma("gpsimd", WJ, w_inj.rearrange("p (c m) -> p c m", c=ND), incs=[S["ldw"]])
    k.dma("gpsimd", WXP, w_xp.rearrange("p (c m) -> p c m", c=NCH), incs=[S["ldw"]])
    LDW_P1 = k.val(S["ldw"])
    WDTP_O, WOUT_O = 17152, 18176
    WDTP = view3(AW, WDTP_O, NCH, 128, p=64)
    WOUT = view3(AW, WOUT_O, NCH, D)
    k.dma("gpsimd", WDTP, w_dtp.rearrange("k (c m) -> k c m", c=NCH),
          incs=[S["ldwa"]])
    k.dma("gpsimd", WOUT, w_out.rearrange("p (c m) -> p c m", c=NCH),
          incs=[S["ldwa"]])
    LDW_P3 = ("ldwa", k.val(S["ldwa"]))

    for dst, src in (
        (g1row[:, :], g1r_in), (i32[:, :], ident_in),
        (bdtc[:, :], b_dtp.rearrange("(c p) o -> p (c o)", p=128)),
        (cbc[:, :], conv_b.rearrange("(c p) o -> p (c o)", p=128)),
        (cwc[:, :, :], conv_w.rearrange("(c p) m -> p c m", p=128)),
        (Acol[:, :, :], A_log.rearrange("(c p) m -> p c m", p=128)),
        (Dcol[:, :], D_in.rearrange("(c p) o -> p (c o)", p=128)),
        (b2c[:, :], b_f2.rearrange("(c p) o -> p (c o)", p=128)),
        (cb1c[:, :], cn_b1.rearrange("(c p) o -> p (c o)", p=128)),
        (bf1c[:, :], b_f1.rearrange("(c p) o -> p (c o)", p=128)),
        (cb2c[:, 0:4], cn_b2.rearrange("(c p) o -> p (c o)", p=128)),
        (cb3c[:, 0:2], cn_b3.rearrange("(c p) o -> p (c o)", p=128)),
        (gb1c[:, 0:4], gc_b1.rearrange("(c p) o -> p (c o)", p=128)),
        (gb2c[:, 0:2], gc_b2.rearrange("(c p) o -> p (c o)", p=128)),
        (cst[:, 0:1], ar_in), (cst[:, 1:2], gc_b3), (cst[:, 2:3], cn_b4),
    ):
        k.dma("sync", dst, src, incs=[S["prm"]], slow=True)
    for dst, src in (
        (g1col16[:, :], g1c_in.rearrange("(c p) o -> p (c o)", p=128)),
        (b1col16[:, :], b1c_in.rearrange("(c p) o -> p (c o)", p=128)),
        (g2col16[:, :], g2c_in.rearrange("(c p) o -> p (c o)", p=128)),
        (b2col16[:, :], b2c_in.rearrange("(c p) o -> p (c o)", p=128)),
    ):
        k.dma("gpsimd", dst, src, incs=[S["prmg"]], slow=True)
    PRM = cur("prm")
    PRMG = cur("prmg")

    def early_stop():
        k.op("gpsimd", lambda e: None, waits=[PRM, (S["ldw"], LDW_P1)])
        k.finalize()
        _NC_CACHE[(STAGE, DEBUG)] = (k.nc, None)
        return _NC_CACHE[(STAGE, DEBUG)]

    if NTL_RUN == -1:
        return early_stop()

    k.op("scalar", lambda e: e.activation(Acol[:, :, :], Acol[:, :, :], AF.Exp),
         waits=[PRM], incs=[S["misc"]])
    k.op("vector", lambda e: e.tensor_scalar_mul(Acol[:, :, :], Acol[:, :, :], -1.0),
         waits=[cur("misc")], incs=[S["misc"]])
    k.op("vector", lambda e: e.memset(ones_c16[:, :], 1.0), incs=[S["misc"]])
    k.op("vector", lambda e: e.memset(ones_r[:, :], 1.0), incs=[S["misc"]])
    k.op("vector", lambda e: e.memset(ones_r16[:, :], 1.0), incs=[S["misc"]])
    k.op("scalar", lambda e: e.activation(i16[:, :], i32[:, :], AF.Copy),
         waits=[PRM], incs=[S["misc"]])

    if NTL_RUN == -2:
        return early_stop()

    WJLD = (S["ldw"], LDW_P1)
    for mt in range((2 * DH) // T):
        for kc in range(ND):
            w = []
            if kc == 0:
                w = [WJLD, PRM, PRMG, (S["vrow"], k.val(S["vrow"]))]
            k.op("tensor", lambda e, mt=mt, kc=kc: e.matmul(
                rowP[0][0:1, :], g1col16[:, kc:kc + 1], WJ[:, kc, mt * T:(mt + 1) * T],
                start=(kc == 0), stop=(kc == ND - 1)),
                waits=w,
                incs=[S["tstat"]] if kc == ND - 1 else ())
        k.op("vector", lambda e, mt=mt: e.tensor_scalar_mul(
            nwg1[0:1, mt * T:(mt + 1) * T], rowP[0][0:1, :], -1.0),
            waits=[cur("tstat")], incs=[S["vrow"]])
    for m in range(16):
        for kc in range(ND):
            k.op("tensor", lambda e, m=m, kc=kc: e.matmul(
                repP[0][:, m:m + 1], WJ[:, kc, m * 128:(m + 1) * 128],
                b1col16[:, kc:kc + 1],
                start=(kc == 0), stop=(kc == ND - 1)),
                incs=[S["tstat"]] if (kc == ND - 1 and m == 15) else ())
    k.op("scalar", lambda e: e.activation(wb1c[:, :], repP[0][:, 0:16], AF.Copy),
         waits=[cur("tstat")], incs=[S["sevac"]])
    P0_EVAC = cur("sevac")
    P0_VROW = cur("vrow")

    if NTL_RUN == -3:
        return early_stop()

    XB_O, XI_O, XCP_O, XD_O = 0, 8192, 12312, 16408
    XC_O = XCP_O
    xbuf = [view3(AFa, XB_O + i * 4096, ND, T) for i in range(2)]
    xibuf = view3(AFa, XI_O, NCH, T + 3)
    xcpre = view3(AFa, XCP_O, NCH, T)
    xct = view3(AFa, XC_O, NCH, T)
    xdblt = AFa[0:96, XD_O:XD_O + T]
    XM_O, SZ_O, XC16_O = 0, 4096, 8192
    xmul16 = view3(AT, XM_O, ND, T)
    szt16 = view3(AT, SZ_O, NCH, T)
    xc16 = view3(AT, XC16_O, NCH, T)

    def p1_load(j):
        p = j % 2
        w = [cur("vrep")] if j >= 2 else ()
        k.dma("sync", xbuf[p][:, :, :],
              xT[j].rearrange("p (c t) -> p c t", c=ND),
              waits=w, incs=[S["lx0" if p == 0 else "lx1"]])

    k.op("vector", lambda e: e.memset(xibuf[:, :, 0:3], 0.0), incs=[S["vtail"]])
    if NTL_RUN > 0:
        p1_load(0)
    if NTL_RUN > 1:
        p1_load(1)

    for j in range(NTL_RUN):
        p = j % 2
        lxv = (S["lx0" if p == 0 else "lx1"], 16 * (j // 2 + 1))
        for c in range(ND):
            cp = c % 2
            w = [lxv, PRM, (S["tstat"], max(0, k.val(S["tstat"]) - 2))]
            k.op("scalar", lambda e, c=c, cp=cp, p=p: e.activation(
                st16[cp][:, :], xbuf[p][:, c, :], AF.Copy),
                waits=w, incs=[S["sstat"]])
            k.op("scalar", lambda e, c=c, cp=cp, p=p: e.activation(
                sqt16[cp][:, :], xbuf[p][:, c, :], AF.Square),
                incs=[S["sstat"]])
            w2 = [cur("sstat")]
            if c == 0:
                w2.append(cur("vrow"))
            k.op("tensor", lambda e, cp=cp, c=c: e.matmul(
                rowP[0][0:1, :], ones_c16[:, :], st16[cp][:, :],
                start=(c == 0), stop=(c == ND - 1)),
                waits=w2, incs=[S["tstat"]])
            k.op("tensor", lambda e, cp=cp, c=c: e.matmul(
                rowP[1][0:1, :], ones_c16[:, :], sqt16[cp][:, :],
                start=(c == 0), stop=(c == ND - 1)),
                incs=[S["tstat"]])
        tsv = cur("tstat")

        def vop(fn, waits=()):
            k.op("vector", fn, waits=waits, incs=[S["vrow"]])

        vop(lambda e: e.tensor_scalar_mul(r0[:, :], rowP[0][0:1, :], 1.0 / D),
            waits=[tsv])
        vop(lambda e: e.tensor_scalar_mul(r1[:, :], rowP[1][0:1, :], 1.0 / D))
        vop(lambda e: e.tensor_tensor(r2[:, :], r0[:, :], r0[:, :], ALU.mult))
        vop(lambda e: e.tensor_tensor(r1[:, :], r1[:, :], r2[:, :], ALU.subtract))
        vop(lambda e: e.tensor_scalar_add(r2[:, :], r1[:, :], EPS))
        k.op("scalar", lambda e: e.activation(r3[:, :], r2[:, :], AF.Ln),
             waits=[cur("vrow")], incs=[S["srow"]])
        k.op("scalar", lambda e: e.activation(r3[:, :], r3[:, :], AF.Exp,
                                              scale=-0.5), incs=[S["srow"]])
        vop(lambda e: e.reciprocal(r2[:, :], r2[:, :]), waits=[cur("srow")])
        vop(lambda e: e.tensor_tensor(r1[:, :], r1[:, :], r2[:, :], ALU.mult))
        vop(lambda e: e.tensor_scalar_add(r1[:, :], r1[:, :], EPS))
        k.op("scalar", lambda e: e.activation(r1[:, :], r1[:, :], AF.Ln),
             waits=[cur("vrow")], incs=[S["srow"]])
        k.op("scalar", lambda e: e.activation(r1[:, :], r1[:, :], AF.Exp,
                                              scale=-0.5), incs=[S["srow"]])
        vop(lambda e: e.tensor_tensor(rR[:, :], r3[:, :], r1[:, :], ALU.mult),
            waits=[cur("srow")])
        vop(lambda e: e.tensor_tensor(rMR[:, :], r0[:, :], rR[:, :], ALU.mult))
        vop(lambda e: e.tensor_copy(rMR16[:, :], rMR[:, :]))
        vrows = cur("vrow")

        for c in range(ND):
            bank = repP[c % 2]
            w = [vrows, (S["vrep"], max(0, k.val(S["vrep"]) - 1))]
            k.op("tensor", lambda e, c=c, bank=bank: e.matmul(
                bank[:, :], g1row[0:1, c * 128:(c + 1) * 128], rR[:, :],
                start=True, stop=True),
                waits=w, incs=[S["trep"]])
            k.op("vector", lambda e, c=c, bank=bank, p=p: e.tensor_tensor(
                xmul16[:, c, :], xbuf[p][:, c, :], bank[:, :], ALU.mult),
                waits=[cur("trep")], incs=[S["vrep"]])
        vxm = cur("vrep")

        for m in range(16):
            bank = mmP[m % 2]
            for kc in range(ND):
                w = []
                if kc == 0:
                    w = [vxm, (S["sevac"], max(0, k.val(S["sevac"]) - 1))]
                k.op("tensor", lambda e, m=m, kc=kc, bank=bank: e.matmul(
                    bank[:, :], WJ[:, kc, m * 128:(m + 1) * 128],
                    xmul16[:, kc, :], start=(kc == 0), stop=False),
                    waits=w)
            k.op("tensor", lambda e, m=m, bank=bank: e.matmul(
                bank[:, :], nwg1[0:1, m * 128:(m + 1) * 128], rMR16[:, :],
                start=False, stop=True), incs=[S["tmm"]])
            tev = cur("tmm")
            if m < 8:
                k.op("scalar", lambda e, m=m, bank=bank: e.add(
                    xibuf[:, m, 3:3 + T], bank[:, :], wb1c[:, m:m + 1]),
                    waits=[tev, cur("vtail"), P0_EVAC], incs=[S["sevac"]])
            else:
                k.op("scalar", lambda e, m=m, bank=bank: e.activation(
                    szt16[:, m - 8, :], bank[:, :], AF.Silu,
                    bias=wb1c[:, m:m + 1]),
                    waits=[tev] + ([cur("dspz")] if m == 8 else []),
                    incs=[S["sevac"]])
        sev = k.val(S["sevac"])

        for c in range(NCH):
            k.op("vector", lambda e, c=c: e.tensor_scalar(
                xcpre[:, c, :], xibuf[:, c, 0:T], cwc[:, c, 0:1], None, ALU.mult),
                waits=[(S["sevac"], sev - 16 + c + 1), cur("sxc"), cur("dspx")],
                incs=[S["vconv"]])
            for jj in range(1, 4):
                k.op("vector", lambda e, c=c, jj=jj: e.scalar_tensor_tensor(
                    xcpre[:, c, :], xibuf[:, c, jj:jj + T], cwc[:, c, jj:jj + 1],
                    xcpre[:, c, :], ALU.mult, ALU.add),
                    incs=[S["vconv"]])
        k.op("vector", lambda e: e.tensor_copy(xibuf[:, :, 0:3],
                                               xibuf[:, :, T:T + 3]),
             waits=[cur("vconv")], incs=[S["vtail"]])
        vcv = cur("vconv")

        for c in range(NCH):
            w = [vcv] if c == 0 else []
            k.op("scalar", lambda e, c=c: e.activation(
                xc16[:, c, :], xcpre[:, c, :], AF.Silu, bias=cbc[:, c:c + 1]),
                waits=w, incs=[S["sxc"]])
            k.op("scalar", lambda e, c=c: e.activation(
                xct[:, c, :], xcpre[:, c, :], AF.Silu, bias=cbc[:, c:c + 1]),
                incs=[S["sxc"]])
        sxcv = cur("sxc")

        k.dma("sync", sp_xc[j].rearrange("p (c t) -> p c t", c=NCH),
              xct[:, :, :], waits=[sxcv], incs=[S["dspx"]])
        k.dma("sync", sp_sz[j].rearrange("p (c t) -> p c t", c=NCH),
              szt16[:, :, :], waits=[(S["sevac"], sev)], incs=[S["dspz"]])

        for kc in range(NCH):
            w = []
            if kc == 0:
                w = [sxcv, cur("sxd"),
                     (S["sevac"], max(0, k.val(S["sevac"]) - 1))]
            k.op("tensor", lambda e, kc=kc: e.matmul(
                mmP[0][0:96, :], WXP[:, kc, :], xc16[:, kc, :],
                start=(kc == 0), stop=(kc == NCH - 1)),
                waits=w, incs=[S["tmm"]] if kc == NCH - 1 else ())
        k.op("scalar", lambda e: e.activation(xdblt[:, :], mmP[0][0:96, :],
                                              AF.Copy),
             waits=[cur("tmm"), cur("dxdbl")], incs=[S["sxd"]])
        k.dma("sync", cc_xdbl_in[j // 8][:, (j % 8) * T:(j % 8 + 1) * T],
              xdblt[:, :], waits=[cur("sxd")], incs=[S["dxdbl"]])
        if j == 7:
            k.op("gpsimd", lambda e: e.collective_compute(
                "AllReduce", ALU.add, replica_groups=GROUPS,
                ins=[cc_xdbl_in[0]], outs=[cc_xdbl[0]]),
                waits=[(S["dxdbl"], 16 * 8)], incs=[S["cc1"]])

        if j + 2 < NTL_RUN:
            p1_load(j + 2)

    k.op("gpsimd", lambda e: e.collective_compute(
        "AllReduce", ALU.add, replica_groups=GROUPS,
        ins=[cc_xdbl_in[1]], outs=[cc_xdbl[1]]),
        waits=[cur("dxdbl")], incs=[S["cc1b"]])

    if DEBUG:
        k.dma("gpsimd", dbg_xdbl, cc_xdbl, waits=[cur("cc1")], incs=[S["ddbg"]])
        k.dma("gpsimd", dbg_xc, sp_xc, waits=[cur("dspx")], incs=[S["ddbg"]])
        k.dma("gpsimd", dbg_sz, sp_sz, waits=[cur("dspz")], incs=[S["ddbg"]])

    if STAGE == 1:
        k.op("gpsimd", lambda e: None,
             waits=[cur("cc1"), cur("cc1b")] + ([cur("ddbg")] if DEBUG else []))
        k.finalize()
        _NC_CACHE[(STAGE, DEBUG)] = (k.nc, None)
        return _NC_CACHE[(STAGE, DEBUG)]

    xct3 = view3(AFa, 0, NCH, T)
    dtt = view3(AFa, 4096, NCH, T)
    yacc = view3(AFa, 8192, NCH, T)
    dtxt = view3(AFa, 12288, NCH, T)
    h_r = [AFa[0:128, 16384 + i * 512:16384 + (i + 1) * 512] for i in range(2)]
    u_t = AFa[0:128, 17408:17920]
    dtex = AFa[0:128, 17920:18432]
    hc16 = [AT[0:128, 9216 + i * T:9216 + (i + 1) * T] for i in range(6)]
    brow16 = [AFa[0:1, 18432 + i * 256:18432 + (i + 1) * 256].bitcast(BF16)
              for i in range(2)]
    crow16 = [AFa[0:1, 18944 + i * 256:18944 + (i + 1) * 256].bitcast(BF16)
              for i in range(2)]
    dAr = [AFa[0:128, 19456 + i * 512:19456 + (i + 1) * 512] for i in range(4)]
    yscr = AFa[0:128, 8192:8704]
    crep_sb = [AFa[0:128, 20480 + i * 512:20480 + (i + 1) * 512]
               for i in range(2)]
    o1ev = [AW[0:128, 26368 + i * 1024:26368 + (i + 1) * 1024].bitcast(F32)
            for i in range(2)]
    y3_16 = view3(AT, 0, NCH, T)
    szt3 = view3(AT, 4096, NCH, T)
    dtraw16 = [AT[0:64, 8192 + i * T:8192 + (i + 1) * T] for i in range(2)]
    brP = [repP[0], repP[1]]
    crP = [repP[2], repP[3]]

    k.op("vector", lambda e: e.memset(state[:, :, :], 0.0), incs=[S["misc"]])
    CC1A = (S["cc1"], 1)
    CC1B = (S["cc1b"], 1)
    P1DONE = [cur("vrep"), cur("tmm"), cur("sxc"), cur("vrow"), cur("srow"),
              cur("sevac"), cur("dspx"), cur("dspz"), cur("dxdbl")]

    def p3_load(j):
        p = j % 2
        cc = CC1A if j < 8 else CC1B
        dw = [cc, (S["tdtp"], max(0, k.val(S["tdtp"]) - 8))]
        if j == 0:
            dw += P1DONE
        k.dma("gpsimd", dtraw16[p],
              cc_xdbl[j // 8][0:64, (j % 8) * T:(j % 8 + 1) * T],
              waits=dw,
              incs=[S["ldt0" if p == 0 else "ldt1"]])
        xw = [(S["dspx"], 16 * (j + 1)), (S["vy3"], k.val(S["vy3"]))]
        zw = [(S["dspz"], 16 * (j + 1)), (S["vy3"], k.val(S["vy3"]))]
        if j == 0:
            xw += P1DONE
            zw += P1DONE
        k.dma("sync", xct3[:, :, :],
              sp_xc[j].rearrange("p (c t) -> p c t", c=NCH),
              waits=xw, incs=[S["lxc0" if p == 0 else "lxc1"]])
        k.dma("sync", szt3[:, :, :],
              sp_sz[j].rearrange("p (c t) -> p c t", c=NCH),
              waits=zw, incs=[S["lsz0" if p == 0 else "lsz1"]])

    def p3_row_load(j, s, sp):
        cc = CC1A if j < 8 else CC1B
        extra = P1DONE if j == 0 else []
        k.dma("gpsimd", brow16[sp][:, :],
              cc_xdbl[j // 8][64 + s:65 + s, (j % 8) * T:(j % 8 + 1) * T],
              waits=[cc, (S["tbr"], k.val(S["tbr"]))] + extra,
              incs=[S["gacc" if sp == 0 else "ghc"]])
        k.dma("gpsimd", crow16[sp][:, :],
              cc_xdbl[j // 8][80 + s:81 + s, (j % 8) * T:(j % 8 + 1) * T],
              waits=[cc, (S["tcr"], k.val(S["tcr"]))] + extra,
              incs=[S["scr" if sp == 0 else "sbe"]])

    p3_load(0)
    for s in range(2):
        p3_row_load(0, s, s % 2)
    nrow_loaded = [2]
    it3 = [0]
    pend_copy = [None]
    next_rep = [0]

    for j in range(NTL):
        p = j % 2
        ldt = (S["ldt0" if p == 0 else "ldt1"], 16 * (j // 2 + 1))
        lxc = (S["lxc0" if p == 0 else "lxc1"], 16 * (j // 2 + 1))
        lsz = (S["lsz0" if p == 0 else "lsz1"], 16 * (j // 2 + 1))
        for m in range(NCH):
            bank = mmP[m % 2]
            k.op("tensor", lambda e, m=m, bank=bank, p=p: e.matmul(
                bank[:, :], WDTP[:, m, :], dtraw16[p],
                start=True, stop=True),
                waits=[ldt, (S[LDW_P3[0]], LDW_P3[1]),
                       (S["sdt"], max(0, k.val(S["sdt"]) - 2)),
                       (S["so1"], 8 * j)],
                incs=[S["tdtp"]])
            k.op("scalar", lambda e, m=m, bank=bank: e.activation(
                dtex[:, :], bank[:, :], AF.Exp, bias=bdtc[:, m:m + 1]),
                waits=[cur("tdtp")], incs=[S["sdt"]])
            k.op("scalar", lambda e, m=m: e.activation(
                dtt[:, m, :], dtex[:, :], AF.Ln, bias=1.0),
                incs=[S["sdt"]])
        sdtv = cur("sdt")
        if DEBUG:
            k.dma("gpsimd", dbg_dt[j].rearrange("p (c t) -> p c t", c=NCH),
                  dtt[:, :, :], waits=[sdtv], incs=[S["ddbg"]])
        for m in range(NCH):
            w = []
            if m == 0:
                w = [sdtv, lxc]
            k.op("vector", lambda e, m=m: e.tensor_tensor(
                dtxt[:, m, :], dtt[:, m, :], xct3[:, m, :], ALU.mult),
                waits=w, incs=[S["vdtx"]])
        vdtxv = cur("vdtx")
        yPb = [rowP[0], rowP[1], mmP[0], mmP[1]]

        def emit_reps(g):
            gsp = g % 2
            bw = [(S["gacc" if gsp == 0 else "ghc"], 16 * (g // 2 + 1))]
            if g >= 2:
                bw.append((S["vscan"], 4 * (g - 1)))
            if g == 0:
                bw.append((S["misc"], k.val(S["misc"])))
            k.op("tensor", lambda e, gsp=gsp: e.matmul(
                brP[gsp][:, :], ones_r16[:, :], brow16[gsp][:, :],
                start=True, stop=True),
                waits=bw, incs=[S["tbr"]])
            cw = [(S["scr" if gsp == 0 else "sbe"], 16 * (g // 2 + 1))]
            if g >= 2:
                cw.append((S["vhc"], 4 * (g - 1)))
            k.op("tensor", lambda e, gsp=gsp: e.matmul(
                crP[gsp][:, :], ones_r16[:, :], crow16[gsp][:, :],
                start=True, stop=True),
                waits=cw, incs=[S["tcr"]])

        for H in range(2):
            for s in range(DS):
                gi = 32 * j + 16 * H + s
                sp = gi % 2
                while next_rep[0] <= min(32 * NTL - 1, gi + 1):
                    emit_reps(next_rep[0])
                    next_rep[0] += 1
                while nrow_loaded[0] < min(32 * NTL, gi + 4):
                    n = nrow_loaded[0]
                    p3_row_load(n // 32, n % 16, n % 2)
                    nrow_loaded[0] += 1
                for kc in range(4 * H, 4 * H + 4):
                    i = it3[0]
                    k.op("scalar", lambda e, kc=kc, s=s, i=i: e.activation(
                        dAr[i % 4][:, :], dtt[:, kc, :], AF.Exp,
                        scale=Acol[:, kc, s:s + 1]),
                        waits=[(S["vscan"], max(0, k.val(S["vscan"]) - 3))],
                        incs=[S["sda"]])
                    k.op("vector", lambda e, kc=kc, sp=sp: e.tensor_tensor(
                        u_t[:, :], dtxt[:, kc, :], brP[sp][:, :], ALU.mult),
                        waits=([vdtxv, (S["tbr"], gi + 1)]
                               if kc == 4 * H else ()))
                    k.op("vector",
                         lambda e, kc=kc, s=s, i=i: e.tensor_tensor_scan(
                             h_r[i % 2][:, :], dAr[i % 4][:, :], u_t[:, :],
                             state[:, kc, s:s + 1], ALU.mult, ALU.add),
                         waits=[(S["sda"], k.val(S["sda"])),
                                (S["lbr"], k.val(S["lbr"]))],
                         incs=[S["vscan"]])
                    hw = [(S["tacc"], max(0, k.val(S["tacc"]) - 5))]
                    if kc == 4 * H:
                        hw.append((S["tcr"], gi + 1))
                    k.op("vector", lambda e, i=i, sp=sp: e.tensor_tensor(
                        hc16[i % 6][:, :], h_r[i % 2][:, :], crP[sp][:, :],
                        ALU.mult),
                        waits=hw, incs=[S["vhc"]])
                    if pend_copy[0] is not None:
                        k.op("scalar", pend_copy[0],
                             waits=[(S["vscan"],
                                     max(0, k.val(S["vscan"]) - 1))],
                             incs=[S["lbr"]])
                    pend_copy[0] = (
                        lambda e, kc=kc, i=i, s=s: e.activation(
                            state[:, kc, s:s + 1], h_r[i % 2][:, T - 1:T],
                            AF.Copy))
                    tw = [(S["vhc"], k.val(S["vhc"]))]
                    if s == 0 and kc == 4 * H:
                        if H == 0:
                            tw.append((S["vy3"], k.val(S["vy3"])))
                            if j == 0:
                                tw += P1DONE
                        else:
                            tw.append((S["sdt"], k.val(S["sdt"])))
                    k.op("tensor", lambda e, kc=kc, s=s, i=i: e.matmul(
                        yPb[kc % 4][:, :], i16[:, :], hc16[i % 6][:, :],
                        start=(s == 0), stop=(s == DS - 1)),
                        waits=tw, incs=[S["tacc"]])
                    it3[0] += 1
            for kc in range(4 * H, 4 * H + 4):
                sw = []
                if kc == 4 * H:
                    sw = [(S["tacc"], k.val(S["tacc"]))]
                    if H == 0:
                        sw += [lsz, (S["top"], k.val(S["top"]))]
                k.op("vector", lambda e, kc=kc: e.scalar_tensor_tensor(
                    yscr[:, :], xct3[:, kc, :], Dcol[:, kc:kc + 1],
                    yPb[kc % 4][:, :], ALU.mult, ALU.add),
                    waits=sw, incs=[S["vy3"]])
                k.op("vector", lambda e, kc=kc: e.tensor_tensor(
                    y3_16[:, kc, :], yscr[:, :], szt3[:, kc, :], ALU.mult),
                    incs=[S["vy3"]])
        if pend_copy[0] is not None:
            k.op("scalar", pend_copy[0],
                 waits=[(S["vscan"], k.val(S["vscan"]))], incs=[S["lbr"]])
            pend_copy[0] = None
        vy3v = cur("vy3")
        if DEBUG:
            k.dma("gpsimd", dbg_y[j].rearrange("p (c t) -> p c t", c=NCH),
                  y3_16[:, :, :], waits=[vy3v], incs=[S["ddbg"]])
        if j + 1 < NTL:
            p3_load(j + 1)
        for m in range(ND):
            bank = mmP[m % 2]
            for kc in range(NCH):
                w = []
                if kc == 0:
                    w = [vy3v, (S["so1"], max(0, k.val(S["so1"]) - 1))]
                k.op("tensor", lambda e, m=m, kc=kc, bank=bank: e.matmul(
                    bank[:, :], WOUT[:, kc, m * 128:(m + 1) * 128],
                    y3_16[:, kc, :], start=(kc == 0), stop=(kc == NCH - 1)),
                    waits=w, incs=[S["top"]] if kc == NCH - 1 else ())
            dsem = S["do1" if m % 2 == 0 else "do1b"]
            k.op("scalar", lambda e, m=m, bank=bank: e.activation(
                o1ev[m % 2][:, :], bank[:, :], AF.Copy),
                waits=[cur("top"), (dsem, k.val(dsem))],
                incs=[S["so1"]])
            k.dma("sync", cc_o1_in[j // 8, j % 8][:, m * T:(m + 1) * T],
                  o1ev[m % 2][:, :],
                  waits=[cur("so1")], incs=[dsem])

    k.op("gpsimd", lambda e: e.collective_compute(
        "ReduceScatter", ALU.add, replica_groups=GROUPS,
        ins=[cc_o1_in], outs=[cc_o1]),
        waits=[(S["do1"], k.val(S["do1"])), (S["do1b"], k.val(S["do1b"]))],
        incs=[S["cc2"]])

    if DEBUG:
        k.dma("gpsimd", dbg_o1, cc_o1, waits=[cur("cc2")], incs=[S["ddbg"]])

    if STAGE == 3:
        k.op("gpsimd", lambda e: None,
             waits=[cur("cc2")] + ([cur("ddbg")] if DEBUG else []))
        k.finalize()
        _NC_CACHE[(STAGE, DEBUG)] = (k.nc, None)
        return _NC_CACHE[(STAGE, DEBUG)]

    CC2 = cur("cc2")

    def ln_chain(stats_src, nchunks, pre_waits, want_mr16=False):
        for c in range(nchunks):
            cp = c % 2
            w = list(pre_waits) if c == 0 else []
            w.append((S["tstat"], max(0, k.val(S["tstat"]) - 2)))
            k.op("scalar", lambda e, c=c, cp=cp: e.activation(
                st16[cp][:, :], stats_src[:, c, :], AF.Copy),
                waits=w, incs=[S["sstat"]])
            k.op("scalar", lambda e, c=c, cp=cp: e.activation(
                sqt16[cp][:, :], stats_src[:, c, :], AF.Square),
                incs=[S["sstat"]])
            w2 = [cur("sstat")]
            if c == 0:
                w2.append(cur("vrow"))
            k.op("tensor", lambda e, cp=cp, c=c, n=nchunks: e.matmul(
                rowP[0][0:1, :], ones_c16[:, :], st16[cp][:, :],
                start=(c == 0), stop=(c == n - 1)),
                waits=w2, incs=[S["tstat"]])
            k.op("tensor", lambda e, cp=cp, c=c, n=nchunks: e.matmul(
                rowP[1][0:1, :], ones_c16[:, :], sqt16[cp][:, :],
                start=(c == 0), stop=(c == n - 1)),
                incs=[S["tstat"]])
        tsv = cur("tstat")

        def vop(fn, waits=()):
            k.op("vector", fn, waits=waits, incs=[S["vrow"]])

        vop(lambda e: e.tensor_scalar_mul(r0[:, :], rowP[0][0:1, :], 1.0 / D),
            waits=[tsv])
        vop(lambda e: e.tensor_scalar_mul(r1[:, :], rowP[1][0:1, :], 1.0 / D))
        vop(lambda e: e.tensor_tensor(r2[:, :], r0[:, :], r0[:, :], ALU.mult))
        vop(lambda e: e.tensor_tensor(r1[:, :], r1[:, :], r2[:, :], ALU.subtract))
        vop(lambda e: e.tensor_scalar_add(r1[:, :], r1[:, :], EPS))
        k.op("scalar", lambda e: e.activation(r1[:, :], r1[:, :], AF.Ln),
             waits=[cur("vrow")], incs=[S["srow"]])
        k.op("scalar", lambda e: e.activation(rR[:, :], r1[:, :], AF.Exp,
                                              scale=-0.5), incs=[S["srow"]])
        vop(lambda e: e.tensor_tensor(rMR[:, :], r0[:, :], rR[:, :], ALU.mult),
            waits=[cur("srow")])
        if want_mr16:
            vop(lambda e: e.tensor_copy(rMR16[:, :], rMR[:, :]))
        return cur("vrow")

    o1t = view3(AFa, 0, ND, T)
    xht = view3(AFa, 4096, ND, T)
    vt = view3(AFa, 8192, ND, T)
    u2t = view3(AFa, 12288, ND, T)
    x1t = view3(AFa, 16384 - 4096 + 4096, ND, T) if False else None
    x1t = o1t
    vm16 = view3(AT, 0, ND, T)
    g16 = view3(AT, 4096, ND, T)
    g16h = view3(AT, 0, 16, T)

    WF1V = view3(AW, 0, ND, DFF)
    k.dma("gpsimd", WF1V, w_f1.rearrange("p (c m) -> p c m", c=ND),
          waits=[CC2], incs=[S["ldwb"]])
    LDW_F1 = k.val(S["ldwb"])
    k.dma("sync", g2row[:, :], g2r_in, waits=[CC2], incs=[S["prm"]])
    G2R_LD = (S["prm"], k.val(S["prm"]))
    for mt in range(DFF // T):
        for kc in range(ND):
            w = []
            if kc == 0:
                w = [(S["ldwb"], LDW_F1), PRM, PRMG,
                     (S["vrow"], k.val(S["vrow"]))]
            k.op("tensor", lambda e, mt=mt, kc=kc: e.matmul(
                rowP[0][0:1, :], g2col16[:, kc:kc + 1],
                WF1V[:, kc, mt * T:(mt + 1) * T],
                start=(kc == 0), stop=(kc == ND - 1)),
                waits=w, incs=[S["tstat"]] if kc == ND - 1 else ())
        k.op("vector", lambda e, mt=mt: e.tensor_scalar_mul(
            nwg2[0:1, mt * T:(mt + 1) * T], rowP[0][0:1, :], -1.0),
            waits=[cur("tstat")], incs=[S["vrow"]])
    for m in range(32):
        for kc in range(ND):
            k.op("tensor", lambda e, m=m, kc=kc: e.matmul(
                repP[0][:, m:m + 1], WF1V[:, kc, m * 128:(m + 1) * 128],
                b2col16[:, kc:kc + 1],
                start=(kc == 0), stop=(kc == ND - 1)),
                incs=[S["tstat"]] if (kc == ND - 1 and m == 31) else ())
    k.op("scalar", lambda e: e.activation(wb2c[:, :], repP[0][:, 0:32], AF.Copy),
         waits=[cur("tstat")], incs=[S["sevac"]])
    k.op("vector", lambda e: e.tensor_tensor(wb2c[:, :], wb2c[:, :],
                                             bf1c[:, :], ALU.add),
         waits=[cur("sevac"), PRM], incs=[S["vrow"]])
    P4A_RDY = [(S["vrow"], k.val(S["vrow"])), (S["sevac"], k.val(S["sevac"]))]

    def p4a_load(i):
        p = i % 2
        k.dma("sync", o1t[:, :, :], cc_o1[i].rearrange("p (c t) -> p c t", c=ND),
              waits=[CC2, (S["vv"], k.val(S["vv"]))],
              incs=[S["lo0" if p == 0 else "lo1"]])
        k.dma("sync", xht[:, :, :],
              xTh[i].rearrange("p (c t) -> p c t", c=ND),
              waits=[CC2, (S["vvm"], k.val(S["vvm"]))],
              incs=[S["lh0" if p == 0 else "lh1"]])

    p4a_load(0)
    for i in range(NTO):
        p = i % 2
        lo = (S["lo0" if p == 0 else "lo1"], 16 * (i // 2 + 1))
        lh = (S["lh0" if p == 0 else "lh1"], 16 * (i // 2 + 1))
        vr = ln_chain(xht, ND, [lh] + (P4A_RDY if i == 0 else []))
        k.op("tensor", lambda e: e.matmul(repP[0][:, :], ones_r[:, :],
                                          rR[:, :], start=True, stop=True),
             waits=[vr, (S["vv"], k.val(S["vv"]))], incs=[S["trep"]])
        k.op("tensor", lambda e: e.matmul(repP[1][:, :], ones_r[:, :],
                                          rMR[:, :], start=True, stop=True),
             incs=[S["trep"]])
        trv = cur("trep")
        for c in range(ND):
            k.op("vector", lambda e, c=c: e.tensor_tensor(
                vt[:, c, :], xht[:, c, :], repP[0][:, :], ALU.mult),
                waits=[trv, lo, (S["dspv"], k.val(S["dspv"]))]
                if c == 0 else (), incs=[S["vv"]])
            k.op("vector", lambda e, c=c: e.tensor_tensor(
                vt[:, c, :], vt[:, c, :], repP[1][:, :], ALU.subtract))
            k.op("vector", lambda e, c=c: e.tensor_tensor(
                vt[:, c, :], vt[:, c, :], o1t[:, c, :], ALU.add),
                incs=[S["vv"]])
        vvv = cur("vv")
        k.dma("sync", sp_v[i].rearrange("p (c t) -> p c t", c=ND), vt[:, :, :],
              waits=[vvv], incs=[S["dspv"]])
        vr2 = ln_chain(vt, ND, [vvv], want_mr16=True)
        for c in range(ND):
            bank = repP[2 + c % 2]
            k.op("tensor", lambda e, c=c, bank=bank: e.matmul(
                bank[:, :], g2row[0:1, c * 128:(c + 1) * 128], rR[:, :],
                start=True, stop=True),
                waits=[vr2, (S["vvm"], max(0, k.val(S["vvm"]) - 1))] +
                      ([G2R_LD] if i == 0 and c == 0 else []),
                incs=[S["trep"]])
            k.op("vector", lambda e, c=c, bank=bank: e.tensor_tensor(
                vm16[:, c, :], vt[:, c, :], bank[:, :], ALU.mult),
                waits=[cur("trep")], incs=[S["vvm"]])
        vmv = cur("vvm")
        for m in range(32):
            bank = mmP[m % 2]
            for kc in range(ND):
                w = []
                if kc == 0:
                    w = [vmv, (S["sg"], max(0, k.val(S["sg"]) - 1))]
                k.op("tensor", lambda e, m=m, kc=kc, bank=bank: e.matmul(
                    bank[:, :], WF1V[:, kc, m * 128:(m + 1) * 128],
                    vm16[:, kc, :], start=(kc == 0), stop=False),
                    waits=w)
            k.op("tensor", lambda e, m=m, bank=bank: e.matmul(
                bank[:, :], nwg2[0:1, m * 128:(m + 1) * 128], rMR16[:, :],
                start=False, stop=True), incs=[S["tmm"]])
            k.op("scalar", lambda e, m=m, bank=bank: e.activation(
                g16[:, m % 8, :], bank[:, :], AF.Gelu_apprx_tanh,
                bias=wb2c[:, m:m + 1]),
                waits=[cur("tmm")] +
                      ([(S["dspg"], 16 * (4 * i + m // 8))]
                       if m % 8 == 0 else []),
                incs=[S["sg"]])
            if m % 8 == 7:
                k.dma("sync", sp_g[i][:, (m - 7) * T:(m + 1) * T],
                      g16[:, :, :], waits=[cur("sg")], incs=[S["dspg"]])
        if i + 1 < NTO:
            p4a_load(i + 1)

    P4A_DONE = [(S["tmm"], k.val(S["tmm"])), (S["sg"], k.val(S["sg"])),
                (S["vvm"], k.val(S["vvm"]))]
    WF2V = view3(AW, 0, 32, D)
    k.dma("gpsimd", WF2V, w_f2.rearrange("p (c m) -> p c m", c=32),
          waits=P4A_DONE, incs=[S["ldwc"]])
    LDW_F2 = k.val(S["ldwc"])

    def p4b_load(i, half):
        p = (2 * i + half) % 2
        w = [(S["dspg"], 16 * 4 * (NTO if i == 0 and half == 0 else i + 1)),
             (S["tvm"], k.val(S["tvm"])),
             (S["tmm"], k.val(S["tmm"]))]
        if i == 0 and half == 0:
            w += P4A_DONE + [(S["vv"], k.val(S["vv"]))]
        k.dma("sync", g16h[:, :, :],
              sp_g[i][:, half * 16 * T:(half + 1) * 16 * T],
              waits=w, incs=[S["lg0" if p == 0 else "lg1"]])

    def p4b_load_vx(i):
        w1 = [(S["dspv"], 16 * (NTO if i == 0 else i + 1)),
              (S["vu2"], k.val(S["vu2"]))]
        w2 = [(S["vx1"], k.val(S["vx1"]))]
        if i == 0:
            w1 += P4A_DONE
            w2 += P4A_DONE + [(S["vv"], k.val(S["vv"]))]
        k.dma("sync", vt[:, :, :], sp_v[i].rearrange("p (c t) -> p c t", c=ND),
              waits=w1, incs=[S["lv0" if i % 2 == 0 else "lv1"]])
        k.dma("sync", xht[:, :, :], xTh[i].rearrange("p (c t) -> p c t", c=ND),
              waits=w2, incs=[S["lh0" if i % 2 == 0 else "lh1"]])

    LH_BASE = k.val(S["lh0"]), k.val(S["lh1"])
    p4b_load(0, 0)
    p4b_load_vx(0)
    for i in range(NTO):
        p = i % 2
        lv = (S["lv0" if p == 0 else "lv1"], 16 * (i // 2 + 1))
        lh = (S["lh0" if p == 0 else "lh1"],
              (LH_BASE[p] if p == 0 else LH_BASE[1]) + 16 * (i // 2 + 1))
        for half in range(2):
            lg = (S["lg0" if (2 * i + half) % 2 == 0 else "lg1"],
                  16 * ((2 * i + half) // 2 + 1))
            banks = [mmP[0], mmP[1], repP[0], repP[1], repP[2], repP[3],
                     rowP[0], rowP[1]]
            for m in range(ND):
                for kcl in range(16):
                    kc = half * 16 + kcl
                    w = []
                    if m == 0 and kcl == 0:
                        w = [lg, (S["ldwc"], LDW_F2)]
                        if half == 0:
                            w += [(S["vu2"], k.val(S["vu2"])),
                                  (S["vx1"], k.val(S["vx1"])),
                                  (S["vw"], k.val(S["vw"]))]
                    incs = []
                    if half == 1 and kcl == 15:
                        incs = [S["tmm"]]
                    elif half == 0 and kcl == 15 and m == ND - 1:
                        incs = [S["tvm"]]
                    k.op("tensor", lambda e, m=m, kc=kc, kcl=kcl, half=half,
                         bank=banks[m]: e.matmul(
                        bank[:, :], WF2V[:, kc, m * 128:(m + 1) * 128],
                        g16h[:, kcl, :],
                        start=(half == 0 and kcl == 0),
                        stop=(half == 1 and kcl == 15)),
                        waits=w, incs=incs)
            if half == 0:
                p4b_load(i, 1)
        tff2 = cur("tmm")
        banks = [mmP[0], mmP[1], repP[0], repP[1], repP[2], repP[3],
                 rowP[0], rowP[1]]
        for c in range(ND):
            k.op("vector", lambda e, c=c, bank=banks[c]: e.scalar_tensor_tensor(
                u2t[:, c, :], bank[:, :], b2c[:, c:c + 1], vt[:, c, :],
                ALU.add, ALU.add),
                waits=[tff2, lv, (S["dspw"], k.val(S["dspw"]))]
                if c == 0 else (), incs=[S["vu2"]])
        vu2v = cur("vu2")
        vr3 = ln_chain(u2t, ND, [vu2v])
        k.op("tensor", lambda e: e.matmul(repP[0][:, :], ones_r[:, :],
                                          rR[:, :], start=True, stop=True),
             waits=[vr3, (S["vx1"], k.val(S["vx1"]))], incs=[S["trep"]])
        k.op("tensor", lambda e: e.matmul(repP[1][:, :], ones_r[:, :],
                                          rMR[:, :], start=True, stop=True),
             incs=[S["trep"]])
        trv = cur("trep")
        for c in range(ND):
            k.op("vector", lambda e, c=c: e.tensor_tensor(
                x1t[:, c, :], u2t[:, c, :], repP[0][:, :], ALU.mult),
                waits=[trv, lh, (S["dspx1"], k.val(S["dspx1"]))]
                if c == 0 else (), incs=[S["vx1"]])
            k.op("vector", lambda e, c=c: e.tensor_tensor(
                x1t[:, c, :], x1t[:, c, :], repP[1][:, :], ALU.subtract))
            k.op("vector", lambda e, c=c: e.tensor_tensor(
                x1t[:, c, :], x1t[:, c, :], xht[:, c, :], ALU.add),
                incs=[S["vx1"]])
        vx1v = cur("vx1")
        k.dma("sync", sp_x1[i].rearrange("p (c t) -> p c t", c=ND),
              x1t[:, :, :], waits=[vx1v], incs=[S["dspx1"]])
        vr4 = ln_chain(x1t, ND, [vx1v])
        k.op("tensor", lambda e: e.matmul(repP[2][:, :], ones_r[:, :],
                                          rR[:, :], start=True, stop=True),
             waits=[vr4, (S["vw"], k.val(S["vw"]))], incs=[S["trep"]])
        k.op("tensor", lambda e: e.matmul(repP[3][:, :], ones_r[:, :],
                                          rMR[:, :], start=True, stop=True),
             incs=[S["trep"]])
        trv = cur("trep")
        for c in range(ND):
            k.op("vector", lambda e, c=c: e.tensor_tensor(
                u2t[:, c, :], x1t[:, c, :], repP[2][:, :], ALU.mult),
                waits=[trv] if c == 0 else (), incs=[S["vw"]])
            k.op("vector", lambda e, c=c: e.tensor_tensor(
                u2t[:, c, :], u2t[:, c, :], repP[3][:, :], ALU.subtract),
                incs=[S["vw"]])
        vwv = cur("vw")
        k.dma("sync", sp_w[i].rearrange("p (c t) -> p c t", c=ND),
              u2t[:, :, :], waits=[vwv], incs=[S["dspw"]])
        k.op("vector", lambda e: e.tensor_reduce(
            mred[:, :], u2t[:, :, :], AX.X, ALU.add),
            waits=[vwv], incs=[S["vmean"]])
        k.op("vector", lambda e: e.nop(), waits=[cur("vmean")])
        if i == 0:
            k.op("vector", lambda e: e.tensor_copy(mean_acc[:, :], mred[:, :]),
                 incs=[S["vmean"]])
        else:
            k.op("vector", lambda e: e.tensor_tensor(
                mean_acc[:, :], mean_acc[:, :], mred[:, :], ALU.add),
                incs=[S["vmean"]])
        if i + 1 < NTO:
            p4b_load(i + 1, 0)
            p4b_load_vx(i + 1)

    k.dma("sync", cc_mean_in, mean_acc[:, :], waits=[cur("vmean")],
          incs=[S["dmean"]])
    k.op("gpsimd", lambda e: e.collective_compute(
        "AllReduce", ALU.add, replica_groups=GROUPS,
        ins=[cc_mean_in], outs=[cc_mean]),
        waits=[cur("dmean")], incs=[S["cc3"]])

    if DEBUG:
        k.dma("gpsimd", dbg_v, sp_v, waits=[cur("dspv")], incs=[S["ddbg"]])
        k.dma("gpsimd", dbg_x1, sp_x1, waits=[cur("dspx1")], incs=[S["ddbg"]])
        k.dma("gpsimd", dbg_w, sp_w, waits=[cur("dspw")], incs=[S["ddbg"]])
        k.dma("gpsimd", dbg_mean, cc_mean, waits=[cur("cc3")], incs=[S["ddbg"]])

    if STAGE == 4:
        k.op("gpsimd", lambda e: None,
             waits=[cur("cc3")] + ([cur("ddbg")] if DEBUG else []))
        k.finalize()
        _NC_CACHE[(STAGE, DEBUG)] = (k.nc, None)
        return _NC_CACHE[(STAGE, DEBUG)]

    P4B_TMM = (S["tmm"], k.val(S["tmm"]))
    WCN1 = view3(AW, 0, ND, D)
    WCN2 = view3(AW, 8192, ND, 512)
    WCN3 = view3(AW, 12288, 4, 256)
    WCN4 = view3(AW, 13312, 2, 1)
    WGC1 = view3(AW, 13320, ND, 512)
    WGC2 = view3(AW, 17416, 4, 256)
    WGC3 = view3(AW, 18440, 2, 1)
    for dst, srcw, c in ((WCN1, cn_w1, ND), (WCN2, cn_w2, ND),
                         (WCN3, cn_w3, 4), (WCN4, cn_w4, 2),
                         (WGC1, gc_w1, ND), (WGC2, gc_w2, 4),
                         (WGC3, gc_w3, 2)):
        k.dma("gpsimd", dst, srcw.rearrange("p (c m) -> p c m", c=c),
              waits=[P4B_TMM], incs=[S["ldwd"]])
    LDW_P5 = k.val(S["ldwd"])

    k.dma("sync", mean_acc[:, :], cc_mean, waits=[cur("cc3")], incs=[S["dmean"]])
    k.op("vector", lambda e: e.tensor_copy(ctx16[:, :], mean_acc[:, :]),
         waits=[cur("dmean")], incs=[S["vcst"]])
    k.op("vector", lambda e: e.tensor_scalar_mul(ctx16[:, :], ctx16[:, :], 0.0),
         waits=[(S["vcst"], 0)], incs=[]) if False else None
    k.op("vector", lambda e: e.tensor_scalar_mul(ctx16[:, :], mean_acc[:, :],
                                                 1.0 / L),
         waits=[cur("vcst")], incs=[S["vcst"]])
    CTXV = cur("vcst")
    for m in range(4):
        for kc in range(ND):
            w = []
            if m == 0 and kc == 0:
                w = [CTXV, (S["ldwd"], LDW_P5)]
            k.op("tensor", lambda e, m=m, kc=kc: e.matmul(
                repP[0][:, m:m + 1], WGC1[:, kc, m * 128:(m + 1) * 128],
                ctx16[:, kc:kc + 1], start=(kc == 0), stop=(kc == ND - 1)),
                waits=w, incs=[S["trep"]] if kc == ND - 1 else ())
        k.op("scalar", lambda e, m=m: e.activation(
            gc_h1[:, m:m + 1], repP[0][:, m:m + 1], AF.Gelu,
            bias=gb1c[:, m:m + 1]),
            waits=[cur("trep")], incs=[S["sgf"]])
    for m in range(2):
        for kc in range(4):
            k.op("tensor", lambda e, m=m, kc=kc: e.matmul(
                repP[1][:, m:m + 1], WGC2[:, kc, m * 128:(m + 1) * 128],
                gc_h1[:, kc:kc + 1], start=(kc == 0), stop=(kc == 3)),
                waits=[cur("sgf")] if kc == 0 else (),
                incs=[S["trep"]] if kc == 3 else ())
        k.op("scalar", lambda e, m=m: e.activation(
            gc_h2[:, m:m + 1], repP[1][:, m:m + 1], AF.Gelu,
            bias=gb2c[:, m:m + 1]),
            waits=[cur("trep")], incs=[S["sgf"]])
    for kc in range(2):
        k.op("tensor", lambda e, kc=kc: e.matmul(
            rowP[0][0:1, 0:1], WGC3[:, kc, 0:1], gc_h2[:, kc:kc + 1],
            start=(kc == 0), stop=(kc == 1)),
            waits=[cur("sgf")] if kc == 0 else (),
            incs=[S["trep"]] if kc == 1 else ())
    def cst_op(fn, extra=()):
        k.op("vector", fn, waits=[cur("vcst")] + list(extra), incs=[S["vcst"]])

    cst_op(lambda e: e.tensor_scalar(cst[:, 3:4], cst[:, 1:2], 0.5, None,
                                     ALU.mult), extra=[PRM])
    k.op("scalar", lambda e: e.activation(cst[:, 5:6], rowP[0][0:1, 0:1],
                                          AF.Tanh, scale=0.5,
                                          bias=cst[:, 3:4]),
         waits=[cur("trep"), cur("vcst")], incs=[S["sgf"]])
    cst_op(lambda e: e.tensor_scalar(cst[:, 6:7], cst[:, 5:6], 0.5, 0.5,
                                     ALU.mult, ALU.add), extra=[cur("sgf")])
    cst_op(lambda e: e.tensor_scalar(cst[:, 7:8], cst[:, 6:7], cst[:, 0:1],
                                     None, ALU.mult))
    cst_op(lambda e: e.tensor_scalar(cst[:, 8:9], cst[:, 7:8], 1.4, 0.1,
                                     ALU.mult, ALU.add))
    cst_op(lambda e: e.tensor_scalar(cst[:, 9:10], cst[:, 8:9], cst[:, 0:1],
                                     None, ALU.subtract))
    cst_op(lambda e: e.tensor_scalar(cst[:, 11:12], cst[:, 0:1], 0.6, None,
                                     ALU.mult))
    cst_op(lambda e: e.tensor_scalar(cst[:, 10:11], cst[:, 11:12], 0.5,
                                     cst[:, 9:10], ALU.mult, ALU.add))
    cst_op(lambda e: e.tensor_scalar(cst[:, 12:13], cst[:, 11:12], 0.5, None,
                                     ALU.mult))
    cst_op(lambda e: e.tensor_scalar(cst[:, 4:5], cst[:, 2:3], 0.5, None,
                                     ALU.mult))
    GFV = cur("vcst")

    outs5 = [view3(AFa, 0, ND, T), view3(AFa, 4096, ND, T)]
    x1t5 = view3(AFa, 8192, ND, T)
    twsc = AFa[0:128, 12288:12800]
    o16 = view3(AT, 0, ND, T)
    c1_16 = view3(AT, 4096, ND, T)
    c2_16 = view3(AT, 8192, 4, T)
    c3_16 = view3(AT, 10240, 2, T)
    SIGN = [ALU.add, ALU.add, ALU.subtract, ALU.subtract,
            ALU.subtract, ALU.subtract, ALU.add, ALU.add]
    P4B_ALLSP = [(S["dspw"], k.val(S["dspw"])), (S["dspx1"], k.val(S["dspx1"]))]

    def p5_load(i):
        w1 = [(S["dspw"], 16 * (i + 1))]
        w2 = [(S["dspx1"], 16 * (i + 1)), (S["vfin"], k.val(S["vfin"]))]
        if i == 0:
            w1 += P4B_ALLSP + [(S["vmean"], k.val(S["vmean"]))]
            w2 += P4B_ALLSP
        else:
            w1.append((S["dout"], 16 * i))
        k.dma("sync", outs5[0][:, :, :],
              sp_w[i].rearrange("p (c t) -> p c t", c=ND),
              waits=w1, incs=[S["lw0" if i % 2 == 0 else "lw1"]])
        k.dma("sync", x1t5[:, :, :],
              sp_x1[i].rearrange("p (c t) -> p c t", c=ND),
              waits=w2, incs=[S["l10" if i % 2 == 0 else "l11"]])

    if P5_LVL == 0:
        k.op("sync", lambda e: None, waits=[GFV] +
             ([cur("ddbg")] if DEBUG else []))
        k.finalize()
        _NC_CACHE[(STAGE, DEBUG)] = (k.nc, None)
        return _NC_CACHE[(STAGE, DEBUG)]

    p5_load(0)
    for i in range(min(NTO, P5_LVL)):
        p = i % 2
        lw = (S["lw0" if p == 0 else "lw1"], 16 * (i // 2 + 1))
        l1 = (S["l10" if p == 0 else "l11"], 16 * (i // 2 + 1))
        for r in range(3):
            src_, dst = outs5[r % 2], outs5[1 - r % 2]
            w = [lw] if r == 0 else [(S["vtw"], k.val(S["vtw"]))]
            w.append((S["tmm"], k.val(S["tmm"])))
            k.op("scalar", lambda e, src_=src_: e.activation(
                o16[:, :, :], src_[:, :, :], AF.Copy),
                waits=w, incs=[S["svm"]])
            sv = cur("svm")
            for m in range(ND):
                bank = mmP[m % 2]
                for kc in range(ND):
                    w = []
                    if kc == 0:
                        w = [sv, (S["sg"], max(0, k.val(S["sg"]) - 1))]
                    k.op("tensor", lambda e, m=m, kc=kc, bank=bank: e.matmul(
                        bank[:, :], WCN1[:, kc, m * 128:(m + 1) * 128],
                        o16[:, kc, :], start=(kc == 0), stop=(kc == ND - 1)),
                        waits=w, incs=[S["tmm"]] if kc == ND - 1 else ())
                k.op("scalar", lambda e, m=m, bank=bank: e.activation(
                    c1_16[:, m, :], bank[:, :], AF.Gelu, bias=cb1c[:, m:m + 1]),
                    waits=[cur("tmm")], incs=[S["sg"]])
            for m in range(4):
                bank = mmP[m % 2]
                for kc in range(ND):
                    w = []
                    if kc == 0:
                        w = [cur("sg") if m == 0 else (S["sg"], k.val(S["sg"])),
                             (S["sg"], max(0, k.val(S["sg"]) - 1))]
                    k.op("tensor", lambda e, m=m, kc=kc, bank=bank: e.matmul(
                        bank[:, :], WCN2[:, kc, m * 128:(m + 1) * 128],
                        c1_16[:, kc, :], start=(kc == 0), stop=(kc == ND - 1)),
                        waits=w, incs=[S["tmm"]] if kc == ND - 1 else ())
                k.op("scalar", lambda e, m=m, bank=bank: e.activation(
                    c2_16[:, m, :], bank[:, :], AF.Gelu, bias=cb2c[:, m:m + 1]),
                    waits=[cur("tmm")], incs=[S["sg"]])
            for m in range(2):
                bank = mmP[m % 2]
                for kc in range(4):
                    w = []
                    if kc == 0:
                        w = [(S["sg"], k.val(S["sg"])),
                             (S["sg"], max(0, k.val(S["sg"]) - 1))]
                    k.op("tensor", lambda e, m=m, kc=kc, bank=bank: e.matmul(
                        bank[:, :], WCN3[:, kc, m * 128:(m + 1) * 128],
                        c2_16[:, kc, :], start=(kc == 0), stop=(kc == 3)),
                        waits=w, incs=[S["tmm"]] if kc == 3 else ())
                k.op("scalar", lambda e, m=m, bank=bank: e.activation(
                    c3_16[:, m, :], bank[:, :], AF.Gelu, bias=cb3c[:, m:m + 1]),
                    waits=[cur("tmm")], incs=[S["sg"]])
            for kc in range(2):
                w = []
                if kc == 0:
                    w = [(S["sg"], k.val(S["sg"])),
                         (S["vrow"], k.val(S["vrow"]))]
                k.op("tensor", lambda e, kc=kc: e.matmul(
                    rowP[0][0:1, :], WCN4[:, kc, 0:1], c3_16[:, kc, :],
                    start=(kc == 0), stop=(kc == 1)),
                    waits=w, incs=[S["tmm"]] if kc == 1 else ())
            k.op("scalar", lambda e: e.activation(
                r1[:, :], rowP[0][0:1, :], AF.Tanh, scale=0.5,
                bias=cst[:, 4:5]),
                waits=[cur("tmm"), GFV], incs=[S["sgf"]])
            k.op("vector", lambda e: e.tensor_scalar(
                r2[:, :], r1[:, :], cst[:, 12:13], cst[:, 10:11],
                ALU.mult, ALU.add),
                waits=[cur("sgf")], incs=[S["vcoup"]])
            k.op("tensor", lambda e: e.matmul(
                repP[2][:, :], ones_r[:, :], r2[:, :], start=True, stop=True),
                waits=[cur("vcoup"), (S["vtw"], k.val(S["vtw"]))],
                incs=[S["trep"]])
            trv = cur("trep")
            for c in range(ND):
                k.op("vector", lambda e, c=c, src_=src_: e.tensor_tensor(
                    twsc[:, :], src_[:, (c + 4) % ND, :], repP[2][:, :],
                    ALU.mult),
                    waits=[trv] if c == 0 else (), incs=[S["vtw"]])
                k.op("vector", lambda e, c=c, src_=src_, dst=dst, op=SIGN[c]:
                     e.tensor_tensor(dst[:, c, :], src_[:, c, :], twsc[:, :],
                                     op),
                     incs=[S["vtw"]])
        res5 = outs5[1]
        vtwf = (S["vtw"], k.val(S["vtw"]))
        vr5 = ln_chain(res5, ND, [vtwf])
        k.op("tensor", lambda e: e.matmul(repP[0][:, :], ones_r[:, :],
                                          rR[:, :], start=True, stop=True),
             waits=[vr5, (S["vfin"], k.val(S["vfin"]))], incs=[S["trep"]])
        k.op("tensor", lambda e: e.matmul(repP[1][:, :], ones_r[:, :],
                                          rMR[:, :], start=True, stop=True),
             incs=[S["trep"]])
        trv = cur("trep")
        for c in range(ND):
            k.op("vector", lambda e, c=c: e.tensor_tensor(
                outs5[0][:, c, :], res5[:, c, :], repP[0][:, :], ALU.mult),
                waits=[trv, l1, (S["dout"], k.val(S["dout"]))]
                if c == 0 else (), incs=[S["vfin"]])
            k.op("vector", lambda e, c=c: e.tensor_tensor(
                outs5[0][:, c, :], outs5[0][:, c, :], repP[1][:, :],
                ALU.subtract))
            k.op("vector", lambda e, c=c: e.tensor_tensor(
                outs5[0][:, c, :], outs5[0][:, c, :], x1t5[:, c, :], ALU.add),
                incs=[S["vfin"]])
        k.dma("sync", out_d[i], outs5[0][:, :, :].rearrange("p c t -> p (c t)"),
              waits=[cur("vfin")], incs=[S["dout"]])
        if i + 1 < min(NTO, P5_LVL):
            p5_load(i + 1)

    k.op("sync", lambda e: None,
         waits=[(S["dout"], 16 * min(NTO, P5_LVL))] +
               ([cur("ddbg")] if DEBUG else []))
    k.finalize()
    _NC_CACHE[(STAGE, DEBUG)] = (k.nc, None)
    return _NC_CACHE[(STAGE, DEBUG)]


def pm(w):
    K, M = w.shape
    nk = K // 128
    return np.ascontiguousarray(
        w.reshape(nk, 128, M).transpose(1, 0, 2).reshape(128, nk * M))


def tok_tiles(a_dt, nt):
    Dm, Lt = a_dt.shape
    t = Lt // nt
    r = a_dt.reshape(ND, 128, nt, t).transpose(2, 1, 0, 3)
    return np.ascontiguousarray(r.reshape(nt, 128, ND * t))


def make_inmaps(inputs):
    x = np.asarray(inputs["L0_attn_x"])
    xt = np.ascontiguousarray(np.transpose(x, (0, 2, 1)))
    ipw = np.asarray(inputs["L0_attn_in_proj_w"])
    maps = []
    for c in range(8):
        b, h = c // 2, c % 2
        ch0 = h * DH
        m = {
            "xT": tok_tiles(xt[b], NTL),
            "xTh": tok_tiles(xt[b][:, h * TOK:(h + 1) * TOK], NTO),
            "w_inj": pm(np.concatenate(
                [ipw[:, ch0:ch0 + DH], ipw[:, DI + ch0:DI + ch0 + DH]], axis=1)),
            "conv_w": np.ascontiguousarray(
                np.asarray(inputs["L0_attn_conv_w"])[ch0:ch0 + DH, 0, :]),
            "conv_b": np.ascontiguousarray(
                np.asarray(inputs["L0_attn_conv_b"])[ch0:ch0 + DH].reshape(DH, 1)),
            "w_xp": pm(np.asarray(inputs["L0_attn_x_proj_w"])[ch0:ch0 + DH, :]),
            "w_dtp": np.ascontiguousarray(
                np.asarray(inputs["L0_attn_dt_proj_w"])[:, ch0:ch0 + DH]),
            "b_dtp": np.ascontiguousarray(
                np.asarray(inputs["L0_attn_dt_proj_b"])[ch0:ch0 + DH].reshape(DH, 1)),
            "A_log": np.ascontiguousarray(
                np.asarray(inputs["L0_attn_A_log"])[ch0:ch0 + DH, :]),
            "D_h": np.ascontiguousarray(
                np.asarray(inputs["L0_attn_D"])[ch0:ch0 + DH].reshape(DH, 1)),
            "w_out": pm(np.asarray(inputs["L0_attn_out_proj_w"])[ch0:ch0 + DH, :]),
            "p3_ident": np.eye(128, dtype=np.float32),
            "ln1_g": np.asarray(inputs["L0_attn_ln1_g"]).reshape(1, D),
            "ln1_b": np.asarray(inputs["L0_attn_ln1_b"]).reshape(1, D),
            "ln1_g_col": np.asarray(inputs["L0_attn_ln1_g"]).reshape(D, 1),
            "ln1_b_col": np.asarray(inputs["L0_attn_ln1_b"]).reshape(D, 1),
            "ln2_g": np.asarray(inputs["L0_attn_ln2_g"]).reshape(1, D),
            "ln2_g_col": np.asarray(inputs["L0_attn_ln2_g"]).reshape(D, 1),
            "ln2_b_col": np.asarray(inputs["L0_attn_ln2_b"]).reshape(D, 1),
            "w_f1": pm(np.asarray(inputs["L0_attn_ffn_w1"])),
            "b_f1": np.asarray(inputs["L0_attn_ffn_b1"]).reshape(DFF, 1),
            "w_f2": pm(np.asarray(inputs["L0_attn_ffn_w2"])),
            "b_f2": np.asarray(inputs["L0_attn_ffn_b2"]).reshape(D, 1),
            "cn_w1": pm(np.asarray(inputs["L0_ffn_cn_w1"])),
            "cn_b1": np.asarray(inputs["L0_ffn_cn_b1"]).reshape(D, 1),
            "cn_w2": pm(np.asarray(inputs["L0_ffn_cn_w2"])),
            "cn_b2": np.asarray(inputs["L0_ffn_cn_b2"]).reshape(512, 1),
            "cn_w3": pm(np.asarray(inputs["L0_ffn_cn_w3"])),
            "cn_b3": np.asarray(inputs["L0_ffn_cn_b3"]).reshape(256, 1),
            "cn_w4": pm(np.asarray(inputs["L0_ffn_cn_w4"])),
            "cn_b4": np.asarray(inputs["L0_ffn_cn_b4"]).reshape(1, 1),
            "gc_w1": pm(np.asarray(inputs["L0_ffn_gc_w1"])),
            "gc_b1": np.asarray(inputs["L0_ffn_gc_b1"]).reshape(512, 1),
            "gc_w2": pm(np.asarray(inputs["L0_ffn_gc_w2"])),
            "gc_b2": np.asarray(inputs["L0_ffn_gc_b2"]).reshape(256, 1),
            "gc_w3": pm(np.asarray(inputs["L0_ffn_gc_w3"])),
            "gc_b3": np.asarray(inputs["L0_ffn_gc_b3"]).reshape(1, 1),
            "ar": np.asarray(inputs["L0_ffn_adaptive_range"]).reshape(1, 1),
        }
        maps.append(m)
    return maps


def kernel(**inputs):
    nc, _ = build_program()
    maps = make_inmaps(inputs)
    res = run_bass_kernel_spmd(nc, maps, core_ids=list(range(8)))
    out = np.empty((4, L, D), np.float32)
    for c in range(8):
        b, h = c // 2, c % 2
        o = np.asarray(res.results[c]["out"])
        o = o.reshape(NTO, 128, ND, T).transpose(0, 3, 2, 1)
        out[b, h * TOK:(h + 1) * TOK] = o.reshape(TOK, D)
    return np.ascontiguousarray(out)


def run_debug(inputs):
    nc, _ = build_program()
    maps = make_inmaps(inputs)
    res = run_bass_kernel_spmd(nc, maps, core_ids=list(range(8)))
    return res
